# version 03
# baseline (speedup 1.0000x reference)
import contextlib
import numpy as np
import ml_dtypes
import concourse.bass as bass
import concourse.mybir as mybir
from concourse.bass_utils import run_bass_kernel_spmd

F32 = mybir.dt.float32
BF16 = mybir.dt.bfloat16
I32 = mybir.dt.int32
AF = mybir.ActivationFunctionType
ALU = mybir.AluOpType
AX = mybir.AxisListType
bf16 = ml_dtypes.bfloat16
ds = bass.ds

NC_ = 8
B, S, DM = 16, 2048, 1024
H, DH, DI, FF, E = 16, 64, 1024, 2048, 8
T = B * S
TOWN = T // NC_
NSEQ = 2
NCH = DM // 128
NDT = DI // 128
NBLK = T // 128
NOWN = TOWN // 128
NCK = S // 128
CAP = T // E
EPS = 1e-5
NITER = 64
MEGA = E + H + 1

DEBUG_OUTS = []
STAGES = {"0", "A", "B", "P", "S"}
SEQ_PHASES = "DEFG"
D_MMB = True
D_MMA = True
D_NCK = None
D_SUB = 99
P_PARTS = {1, 2, 3}
DEBUG_BIS = False
B_PARTS = 3
B_NT = None
B_PLAIN = False


class Sem:
    __slots__ = ("h", "n")

    def __init__(self, es, nc, name):
        self.h = es.enter_context(nc.semaphore(name))
        self.n = 0


class Ctx:
    pass


def _sb(ctx, name, shape, dt):
    t = ctx.es.enter_context(ctx.nc.sbuf_tensor(name, list(shape), dt))
    setattr(ctx, name, t)
    return t


def build_nc():
    nc = bass.Bass("TRN2", target_bir_lowering=False, detect_race_conditions=False)
    ctx = Ctx()
    ctx.nc = nc
    es = contextlib.ExitStack()
    ctx.es = es

    def din(name, shape, dt=F32):
        ap = nc.dram_tensor(name, list(shape), dt, kind="ExternalInput").ap()
        setattr(ctx, name, ap)

    din("x_own", [TOWN, DM])
    din("x_all", [T, DM])
    din("wq", [DM, DI]); din("wk", [DM, DI]); din("wv", [DM, DI])
    din("wga", [DM, DI]); din("wa", [DM, H]); din("ba", [H])
    din("wo", [DI, DM]); din("gnw", [DI]); din("gnb", [DI])
    din("wgf", [DM, E]); din("w1", [E, DM, FF]); din("w2", [E, FF, DM])
    din("csel", [1], I32)
    ctx.out = nc.dram_tensor("out", [TOWN, DM], F32, kind="ExternalOutput").ap()
    for name, shape, dt, _fn in DEBUG_OUTS:
        setattr(ctx, "dbg_" + name,
                nc.dram_tensor("dbg_" + name, list(shape), dt, kind="ExternalOutput").ap())

    def scr(name, shape, dt):
        ap = nc.dram_tensor(name, list(shape), dt, kind="Internal").ap()
        setattr(ctx, name, ap)

    scr("xhi_all", [T, DM], BF16)
    scr("xhi_own", [TOWN, DM], BF16)
    scr("w1_b", [E, DM, FF], BF16); scr("w2_b", [E, FF, DM], BF16)
    scr("msrow_d", [2, TOWN], F32)

    ctx.ps = es.enter_context(nc.psum_tensor("ps", [128, 4096], F32))

    _sb(ctx, "ones_f", [128, 128], F32)
    _sb(ctx, "ones_b", [128, 128], BF16)
    _sb(ctx, "ident_f", [128, 128], F32)
    _sb(ctx, "ident_b", [128, 128], BF16)
    _sb(ctx, "maskadd", [128, 128], F32)
    _sb(ctx, "exp2_f", [2, 128], F32)
    _sb(ctx, "exp2_b", [2, 128], BF16)
    _sb(ctx, "mega_hi", [128, NCH, 32], BF16)
    _sb(ctx, "mega_lo", [128, NCH, 32], BF16)
    _sb(ctx, "corr", [128, MEGA], F32)
    _sb(ctx, "ba_t", [128, H], F32)
    _sb(ctx, "gnw_t", [128, NDT], F32)
    _sb(ctx, "gnb_t", [128, NDT], F32)
    _sb(ctx, "csel_sb", [1, 1], I32)
    _sb(ctx, "cst", [128, 4], F32)
    _sb(ctx, "e16", [16, NDT, 128], BF16)
    _sb(ctx, "e2t", [128, 2], F32)
    _sb(ctx, "ecol", [128, 8], F32)
    _sb(ctx, "ecol2", [128, 8], F32)
    _sb(ctx, "tiny", [2, 4, 64], F32)

    _sb(ctx, "gates_own", [128, E, NOWN], F32)
    _sb(ctx, "awa_own", [128, H, NOWN], F32)
    _sb(ctx, "tau", [128, E], F32)
    _sb(ctx, "m_own", [128, NOWN], F32)
    _sb(ctx, "sinv_own", [128, NOWN], F32)
    _sb(ctx, "bis", [1, 32], F32)
    _sb(ctx, "bis2", [128, 4, E], F32)
    _sb(ctx, "cnt8", [128, E], F32)

    _sb(ctx, "big0", [128, 16384], BF16)
    _sb(ctx, "big1", [128, 16384], BF16)
    _sb(ctx, "big2", [128, 16384], BF16)
    _sb(ctx, "big3", [128, 16384], BF16)
    _sb(ctx, "big4", [128, 16384], BF16)
    _sb(ctx, "arena", [128, 18528], BF16)

    b0f = ctx.big0.ap().bitcast(F32)
    b1f = ctx.big1.ap().bitcast(F32)
    b2f = ctx.big2.ap().bitcast(F32)
    arf = ctx.arena.ap().bitcast(F32)
    arb = ctx.arena.ap()
    ctx.ALL = b0f[:, 0:MEGA * NBLK].rearrange("p (a b) -> p a b", a=MEGA)
    ctx.cmp_f = b1f[:, 0:E * NBLK].rearrange("p (a b) -> p a b", a=E)
    ctx.gates_all = b1f[:, E * NBLK:2 * E * NBLK].rearrange("p (a b) -> p a b", a=E)
    ctx.val_mx = b1f[:, 2 * E * NBLK:2 * E * NBLK + 4 * NBLK].rearrange(
        "p (a b) -> p a b", a=4)
    ctx.sumsq = b2f[:, 0:NBLK]
    ctx.m_all = b2f[:, NBLK:2 * NBLK]
    ctx.sinv_all = b2f[:, 2 * NBLK:3 * NBLK]
    ctx.ALLo = b2f[:, 3 * NBLK:3 * NBLK + MEGA * NOWN].rearrange(
        "p (a b) -> p a b", a=MEGA)
    ctx.sumsq_o = b2f[:, 3 * NBLK + MEGA * NOWN:3 * NBLK + MEGA * NOWN + NOWN]
    ctx.m_allo = b2f[:, 3 * NBLK + MEGA * NOWN + NOWN:3 * NBLK + MEGA * NOWN + 2 * NOWN]
    ctx.sinv_allo = b2f[:, 3 * NBLK + MEGA * NOWN + 2 * NOWN:3 * NBLK + MEGA * NOWN + 3 * NOWN]
    ctx.cmp_o = b1f[:, 2 * E * NBLK + 4 * NBLK:2 * E * NBLK + 4 * NBLK + E * NOWN].rearrange(
        "p (a b) -> p a b", a=E)
    ctx.val_mxo = b1f[:, 2 * E * NBLK + 4 * NBLK + E * NOWN:
                      2 * E * NBLK + 4 * NBLK + E * NOWN + 4 * NOWN].rearrange(
        "p (a b) -> p a b", a=4)
    ctx.xstage = arf[:, 0:2048].rearrange("p (a b) -> p a b", a=2)
    ctx.xhi_st = arb[:, 4096:6144].rearrange("p (a b) -> p a b", a=2)
    ctx.fmst = arb[:, 6144:8192].rearrange("p (a b) -> p a b", a=4)
    ctx.wstage = arf[:, 4096:5120]
    ctx.ost = arf[:, 5120:5632][0:MEGA, :]

    ctx.XFM = arb[:, 0:1024].rearrange("p (a b) -> p a b", a=2)
    ctx.LN_TMP = arf[:, 512:1024]
    ctx.MSLm = arf[0:1, 1024:2048]
    ctx.MSLs = arf[0:1, 2048:3072]
    ctx.WST2 = arf[:, 1024:2048]
    ctx.CUM = arf[0:16, 3072:5120]
    ctx.EBUF = arf[0:16, 5120:7168]
    ctx.EBE = arb[0:16, 28672 // 2:32768 // 2]
    ctx.CUMT = arf[:, 8192:8448].rearrange("p (a b) -> p a b", a=NCK)
    ctx.EDET = arf[:, 8448:8704].rearrange("p (a b) -> p a b", a=NCK)
    ctx.GAM2 = arf[:, 8704:8832]
    ctx.G2 = arf[0:2, 8832:8960]
    ctx.GAMST = arf[0:16, 6144:6160]
    ctx.SP_T = arf[:, 8960:9216].rearrange("p (a b) -> p a b", a=H)
    ctx.CB = arf[0:16, 9216:9232]
    ctx.CE = arf[0:16, 9232:9248]
    ctx.GAMS = arf[0:16, 9248:9264]
    ctx.GFLAT = arf[0:1, 6144:6400]
    ctx.EXPD = arb[:, 0:2048]
    ctx.A_SB = arb[:, 2048:4096]
    ctx.KDEC = arb[:, 4096:5120]
    ctx.S_B = arb[:, 5120:5632]
    ctx.S_F = arf[:, 5120:5632]
    ctx.PA = arb[:, 12288:13312].rearrange("p (a b) -> p a b", a=NDT)
    ctx.ECOLS = arf[:, 6912:6944]
    ctx.GT = arb[:, 13312:13824]
    ctx.G_FM = arb[:, 0:8192].rearrange("p (a b) -> p a b", a=16)
    ctx.WST2f = arf[:, 4096:5120]
    ctx.XG = arf[:, 0:2048].rearrange("p (a b) -> p a b", a=2)
    ctx.OUTST = arf[:, 2048:4096].rearrange("p (a b) -> p a b", a=2)
    ctx.T1A = arf[:, 5632:6144]
    ctx.T1B = arf[:, 6656:7168]

    ctx.sems = {}
    for s in ["dS", "dG", "dT", "dW", "dO", "cA", "cB", "cC", "cD", "cE", "cF", "BAR",
              "q1", "q2", "q3", "e1", "e2", "e3", "e4", "e5", "e6", "e7", "e8",
              "e9", "e10", "e11", "e12", "f1", "f2", "f3", "f4"]:
        setattr(ctx, s, Sem(es, nc, s))
    ctx.bar_n = 0

    with nc.Block() as blk:
        ctx.blk = blk
        emit_all(ctx)

    es.close()
    return nc, ctx


def barrier(ctx):
    blk = ctx.blk
    ctx.bar_n += 1
    target = 5 * ctx.bar_n
    bar = ctx.BAR.h
    for name in ["sync", "tensor", "vector", "scalar", "gpsimd"]:
        def f(eng):
            eng.drain().then_inc(bar, 1)
            eng.wait_ge(bar, target)
        getattr(blk, name)(f)


def emit_all(ctx):
    if "0" in STAGES:
        emit_stage0(ctx)
        barrier(ctx)
        emit_stage0b(ctx)
    if "A" in STAGES:
        emit_stageA(ctx)
    if "B" in STAGES:
        emit_stageB(ctx)
        barrier(ctx)
    if "P" in STAGES:
        emit_post_routing(ctx)
        barrier(ctx)
    if "S" in STAGES:
        emit_msrow(ctx)
        barrier(ctx)
        for s in range(NSEQ):
            emit_seq(ctx, s)
    for name, _s, _d, fn in DEBUG_OUTS:
        emit_debug_out(ctx, name, fn)
    finalize(ctx)


def emit_debug_out(ctx, name, fn):
    blk = ctx.blk
    ap = fn(ctx)

    @blk.sync
    def _(sy):
        sy.dma_start(out=getattr(ctx, "dbg_" + name), in_=ap).then_inc(ctx.dO.h, 16)
    ctx.dO.n += 16


def finalize(ctx):
    blk = ctx.blk

    @blk.sync
    def _(sy):
        if ctx.dO.n:
            sy.wait_ge(ctx.dO.h, ctx.dO.n)

    @blk.gpsimd
    def _(g):
        g.wait_ge(ctx.q3.h, ctx.q3.n)


def emit_stage0(ctx):
    blk = ctx.blk
    dW, dS = ctx.dW, ctx.dS

    @blk.gpsimd
    def _(g):
        g.memset(ctx.ones_f[:], 1.0)
        g.memset(ctx.ones_b[:], 1.0)
        g.memset(ctx.cst[:, 0:1], EPS)
        g.memset(ctx.cst[:, 1:2], 1.0)
        g.memset(ctx.cst[:, 2:3], 0.0)
        g.affine_select(ctx.ident_f[:], ctx.ones_f[:], compare_op=ALU.is_equal,
                        base=0, pattern=[[1, 128]], channel_multiplier=-1, fill=0.0)
        g.affine_select(ctx.ident_b[:], ctx.ones_b[:], compare_op=ALU.is_equal,
                        base=0, pattern=[[1, 128]], channel_multiplier=-1, fill=0.0)
        g.memset(ctx.maskadd[:], 0.0)
        g.affine_select(ctx.maskadd[:], ctx.maskadd[:], compare_op=ALU.is_ge,
                        base=0, pattern=[[1, 128]], channel_multiplier=-1, fill=-1e30)
        g.memset(ctx.exp2_f[:], 1.0)
        g.affine_select(ctx.exp2_f[:], ctx.exp2_f[:], compare_op=ALU.is_ge,
                        base=0, pattern=[[1, 128]], channel_multiplier=-64, fill=0.0)
        g.affine_select(ctx.exp2_f[:], ctx.exp2_f[:], compare_op=ALU.is_ge,
                        base=63, pattern=[[-1, 128]], channel_multiplier=64, fill=0.0)
        g.tensor_copy(ctx.exp2_b[:], ctx.exp2_f[:])
        g.memset(ctx.e16[:], 1.0)
        for dt in range(NDT):
            g.affine_select(ctx.e16[:, dt, :], ctx.e16[:, dt, :], compare_op=ALU.is_ge,
                            base=128 * dt, pattern=[[1, 128]], channel_multiplier=-64,
                            fill=0.0)
            g.affine_select(ctx.e16[:, dt, :], ctx.e16[:, dt, :], compare_op=ALU.is_ge,
                            base=63 - 128 * dt, pattern=[[-1, 128]], channel_multiplier=64,
                            fill=0.0)
        g.memset(ctx.e2t[:], 1.0)
        g.affine_select(ctx.e2t[:], ctx.e2t[:], compare_op=ALU.is_ge,
                        base=0, pattern=[[-64, 2]], channel_multiplier=1, fill=0.0)
        g.affine_select(ctx.e2t[:], ctx.e2t[:], compare_op=ALU.is_ge,
                        base=63, pattern=[[64, 2]], channel_multiplier=-1, fill=0.0)

    @blk.sync
    def _(sy):
        sy.dma_start(out=ctx.wstage[:, 0:NCH * E],
                     in_=ctx.wgf.rearrange("(c p) e -> p c e", p=128)).then_inc(dS.h, 16)
        sy.dma_start(out=ctx.wstage[:, NCH * E:NCH * E + NCH * H],
                     in_=ctx.wa.rearrange("(c p) h -> p c h", p=128)).then_inc(dS.h, 16)
        sy.dma_start(out=ctx.tiny[0:1, 0, 0:H], in_=ctx.ba.unsqueeze(0)).then_inc(dS.h, 16)
        sy.dma_start(out=ctx.gnw_t[:],
                     in_=ctx.gnw.rearrange("(c p) -> p c", p=128),
                     allow_slow_non_contiguous=True).then_inc(dS.h, 16)
        sy.dma_start(out=ctx.gnb_t[:],
                     in_=ctx.gnb.rearrange("(c p) -> p c", p=128),
                     allow_slow_non_contiguous=True).then_inc(dS.h, 16)
        sy.dma_start(out=ctx.csel_sb[:], in_=ctx.csel.unsqueeze(0)).then_inc(dS.h, 16)
        dS.n += 16 * 6
        sy.wait_ge(dS.h, dS.n)


def emit_stage0b(ctx):
    blk, ps = ctx.blk, ctx.ps
    cB, cC = ctx.cB, ctx.cC

    @blk.vector
    def _(v):
        v.memset(ctx.mega_hi[:], 0.0)
        v.memset(ctx.mega_lo[:], 0.0)
        for c in range(NCH):
            src = ctx.wstage[:, c * E:(c + 1) * E]
            wa_src = ctx.wstage[:, NCH * E + c * H:NCH * E + (c + 1) * H]
            v.tensor_copy(ctx.mega_hi[:, c, 0:E], src)
            v.tensor_copy(ctx.mega_hi[:, c, E:E + H], wa_src)
            v.tensor_copy(ctx.mega_hi[:, c, E + H:MEGA], ctx.ones_f[:, 0:1])
            v.drain()
            v.scalar_tensor_tensor(ctx.mega_lo[:, c, 0:E], ctx.mega_hi[:, c, 0:E],
                                   -1.0, src, ALU.mult, ALU.add)
            v.scalar_tensor_tensor(ctx.mega_lo[:, c, E:E + H], ctx.mega_hi[:, c, E:E + H],
                                   -1.0, wa_src, ALU.mult, ALU.add)
        v.drain().then_inc(cB.h, 1)
        cB.n += 1

    @blk.tensor
    def _(t):
        for c in range(NCH):
            t.matmul(ps[0:1, 0:E], ctx.ones_f[:, 0:1], ctx.wstage[:, c * E:(c + 1) * E],
                     start=(c == 0), stop=(c == NCH - 1))
        for c in range(NCH):
            t.matmul(ps[0:1, 32:32 + H], ctx.ones_f[:, 0:1],
                     ctx.wstage[:, NCH * E + c * H:NCH * E + (c + 1) * H],
                     start=(c == 0), stop=(c == NCH - 1))
        t.drain().then_inc(cC.h, 1)
        cC.n += 1

    @blk.vector
    def _(v):
        v.wait_ge(cC.h, cC.n)
        v.tensor_copy(ctx.tiny[0:1, 1, 0:E], ps[0:1, 0:E])
        v.tensor_copy(ctx.tiny[0:1, 1, E:E + H], ps[0:1, 32:32 + H])
        v.memset(ctx.tiny[0:1, 1, E + H:MEGA], float(DM))
        v.drain().then_inc(cC.h, 1)
        cC.n += 1

    @blk.tensor
    def _(t):
        t.wait_ge(cC.h, cC.n)
        t.matmul(ps[:, 0:32][:, 0:MEGA], ctx.ones_f[0:1, :], ctx.tiny[0:1, 1, 0:MEGA],
                 start=True, stop=True)
        t.matmul(ps[:, 32:32 + H], ctx.ones_f[0:1, :], ctx.tiny[0:1, 0, 0:H],
                 start=True, stop=True)
        t.drain().then_inc(cC.h, 1)
        cC.n += 1

    @blk.vector
    def _(v):
        v.wait_ge(cC.h, cC.n)
        v.tensor_copy(ctx.corr[:], ps[:, 0:MEGA])
        v.tensor_copy(ctx.ba_t[:], ps[:, 32:32 + H])
        v.drain().then_inc(cC.h, 1)
        cC.n += 1


def emit_stageA(ctx):
    blk = ctx.blk
    dS, dG, cD = ctx.dS, ctx.dG, ctx.cD
    xa = ctx.x_all.rearrange("(n p) d -> n p d", p=128)
    xo = ctx.x_own.rearrange("(n p) d -> n p d", p=128)
    xha = ctx.xhi_all.rearrange("(n p) d -> n p d", p=128)
    xho = ctx.xhi_own.rearrange("(n p) d -> n p d", p=128)
    ntile = NBLK + NOWN
    load_base = dS.n
    done_base = cD.n
    wr_base = dG.n
    ctx.dG_allbase = dG.n

    @blk.sync
    def _(sy):
        for i in range(ntile):
            b = i % 2
            if i >= 2:
                sy.wait_ge(cD.h, done_base + (i - 2) + 1)
            src = xa[i] if i < NBLK else xo[i - NBLK]
            sy.dma_start(out=ctx.xstage[:, b], in_=src).then_inc(dS.h, 16)
        dS.n += 16 * ntile

    @blk.vector
    def _(v):
        for i in range(ntile):
            b = i % 2
            v.wait_ge(dS.h, load_base + 16 * (i + 1))
            if i >= 2:
                v.wait_ge(dG.h, wr_base + 16 * (i - 1))
            v.tensor_copy(ctx.xhi_st[:, b], ctx.xstage[:, b])
            acc = (ctx.sumsq[:, i:i + 1] if i < NBLK
                   else ctx.sumsq_o[:, i - NBLK:i - NBLK + 1])
            v.scalar_tensor_tensor(ctx.xstage[:, b], ctx.xhi_st[:, b], 1.0,
                                   ctx.xhi_st[:, b], ALU.bypass, ALU.mult,
                                   accum_out=acc)
            v.drain().then_inc(cD.h, 1)
        cD.n += ntile

    @blk.gpsimd
    def _(g):
        for i in range(ntile):
            b = i % 2
            g.wait_ge(cD.h, done_base + i + 1)
            dst = xha[i] if i < NBLK else xho[i - NBLK]
            g.dma_start(out=dst, in_=ctx.xhi_st[:, b]).then_inc(dG.h, 16)
        dG.n += 16 * ntile


def emit_stageB(ctx):
    blk, ps = ctx.blk, ctx.ps
    dT, cE, cF, cC = ctx.dT, ctx.cE, ctx.cF, ctx.cC
    NT = T // 512 + TOWN // 512
    if B_NT is not None:
        NT = B_NT

    rd_base = dT.n
    mm_base = cE.n
    ev_base = cF.n
    tr_base = cC.n
    sc_base = ctx.cB.n
    mega_done = ctx.cB.n
    ps_free = ctx.cC.n

    @blk.sync
    def _(sy):
        for i in range(NT):
            ntile_done = (4 * i + 4) if i < T // 512 else (NBLK + 4 * (i - T // 512) + 4)
            sy.wait_ge(ctx.dG.h, ctx.dG_allbase + 16 * ntile_done)
            for c in range(NCH):
                k = i * NCH + c
                if k >= 4 and B_PARTS >= 2:
                    sy.wait_ge(cE.h, mm_base + (k - 4) + 1)
                if i < T // 512:
                    srcap = ctx.xhi_all[i * 512:(i + 1) * 512, c * 128:(c + 1) * 128]
                else:
                    io = i - T // 512
                    srcap = ctx.xhi_own[io * 512:(io + 1) * 512, c * 128:(c + 1) * 128]
                if B_PLAIN:
                    sy.dma_start(out=ctx.fmst[:, k % 4],
                                 in_=ctx.xhi_all.rearrange("(a p) d -> a p d", p=128)[k % 200, :, 0:512]
                                 ).then_inc(dT.h, 16)
                else:
                    sy.dma_start_transpose(ctx.fmst[:, k % 4], srcap).then_inc(dT.h, 16)
        dT.n += 16 * NT * NCH

    @blk.tensor
    def _(t):
        if B_PARTS < 2:
            return
        t.wait_ge(ctx.cB.h, mega_done)
        t.wait_ge(cC.h, ps_free)
        for i in range(NT + 1):
            if i < NT:
                if i >= 1:
                    t.wait_ge(cF.h, ev_base + i)
                for c in range(NCH):
                    k = i * NCH + c
                    t.wait_ge(dT.h, rd_base + 16 * (k + 1))
                    t.matmul(ps[0:MEGA, 0:512], ctx.mega_hi[:, c, 0:MEGA], ctx.fmst[:, k % 4],
                             start=(c == 0), stop=False)
                    t.matmul(ps[0:MEGA, 0:512], ctx.mega_lo[:, c, 0:MEGA], ctx.fmst[:, k % 4],
                             start=False, stop=(c == NCH - 1)).then_inc(cE.h, 1)
            if i >= 1 and B_PARTS >= 3:
                j0 = i - 1
                t.wait_ge(cF.h, ev_base + j0 + 1)
                if j0 >= 1:
                    t.wait_ge(ctx.cB.h, sc_base + j0)
                for j in range(4):
                    t.transpose(ps[:, 2048 + j * 32:2048 + j * 32 + MEGA],
                                ctx.ost[:, j * 128:(j + 1) * 128],
                                ctx.ident_f[0:MEGA, 0:MEGA])
                t.drain().then_inc(cC.h, 1)
        cE.n += NT * NCH
        if B_PARTS >= 3:
            cC.n += NT

    @blk.vector
    def _(v):
        if B_PARTS < 2:
            return
        for i in range(NT):
            v.wait_ge(cE.h, mm_base + NCH * (i + 1))
            if i >= 1 and B_PARTS >= 3:
                v.wait_ge(cC.h, tr_base + i)
            v.tensor_copy(ctx.ost[:], ps[0:MEGA, 0:512])
            v.drain().then_inc(cF.h, 1)
            if i >= 1 and B_PARTS >= 3:
                j0 = i - 1
                v.wait_ge(cC.h, tr_base + j0 + 1)
                for j in range(4):
                    if j0 < T // 512:
                        dst = ctx.ALL[:, :, 4 * j0 + j]
                    else:
                        dst = ctx.ALLo[:, :, 4 * (j0 - T // 512) + j]
                    v.tensor_copy(dst, ps[:, 2048 + j * 32:2048 + j * 32 + MEGA])
                v.drain().then_inc(ctx.cB.h, 1)
        if B_PARTS >= 3:
            v.wait_ge(cC.h, tr_base + NT)
            for j in range(4):
                dst = ctx.ALLo[:, :, 4 * (NT - 1 - T // 512) + j]
                v.tensor_copy(dst, ps[:, 2048 + j * 32:2048 + j * 32 + MEGA])
            v.drain().then_inc(ctx.cB.h, 1)
            ctx.cB.n += NT
        cF.n += NT


def emit_post_routing(ctx):
    blk, ps = ctx.blk, ctx.ps
    cC, cD = ctx.cC, ctx.cD
    Bp = ctx.ALL[:][:, 0:E, :]
    sums = ctx.ALL[:][:, E + H, :]
    Bpo = ctx.ALLo[:][:, 0:E, :]
    awao = ctx.ALLo[:][:, E:E + H, :]
    sumso = ctx.ALLo[:][:, E + H, :]
    if 1 not in P_PARTS:
        return

    @blk.vector
    def _(v):
        v.tensor_scalar(ctx.m_all[:], sums, 1.0 / DM, None, ALU.mult)
        v.tensor_tensor(ctx.sinv_all[:], ctx.m_all[:], ctx.m_all[:], ALU.mult)
        v.scalar_tensor_tensor(ctx.sinv_all[:], ctx.sumsq[:], 1.0 / DM, ctx.sinv_all[:],
                               ALU.mult, ALU.subtract)
        v.tensor_scalar(ctx.m_allo[:], sumso, 1.0 / DM, None, ALU.mult)
        v.drain()
        v.tensor_tensor(ctx.sinv_allo[:], ctx.m_allo[:], ctx.m_allo[:], ALU.mult)
        v.drain()
        v.scalar_tensor_tensor(ctx.sinv_allo[:], ctx.sumsq_o[:], 1.0 / DM,
                               ctx.sinv_allo[:], ALU.mult, ALU.subtract)
        v.drain().then_inc(cC.h, 1)
        cC.n += 1

    @blk.scalar
    def _(s):
        s.wait_ge(cC.h, cC.n)
        s.activation(out=ctx.sinv_all[:], in_=ctx.sinv_all[:], func=AF.Ln, bias=ctx.cst[:, 0:1])
        s.activation(out=ctx.sinv_all[:], in_=ctx.sinv_all[:], func=AF.Exp, scale=-0.5)
        s.activation(out=ctx.sinv_allo[:], in_=ctx.sinv_allo[:], func=AF.Ln, bias=ctx.cst[:, 0:1])
        s.drain()
        s.activation(out=ctx.sinv_allo[:], in_=ctx.sinv_allo[:], func=AF.Exp, scale=-0.5)
        s.drain()
        s.drain().then_inc(cC.h, 1)
        cC.n += 1

    @blk.vector
    def _(v):
        v.wait_ge(cC.h, cC.n)
        m_v = ctx.m_all[:].unsqueeze(1).to_broadcast([128, E, NBLK])
        s_v = ctx.corr[:, 0:E].unsqueeze(2).to_broadcast([128, E, NBLK])
        sinv_v = ctx.sinv_all[:].unsqueeze(1).to_broadcast([128, E, NBLK])
        v.tensor_tensor(ctx.cmp_f[:], m_v, s_v, ALU.mult)
        v.tensor_tensor(Bp, Bp, ctx.cmp_f[:], ALU.subtract)
        v.tensor_tensor(Bp, Bp, sinv_v, ALU.mult)
        mx = ctx.val_mx[:]
        v.tensor_tensor(mx[:, 0:4, :], Bp[:, 0:4, :], Bp[:, 4:8, :], ALU.max)
        v.tensor_tensor(mx[:, 0:2, :], mx[:, 0:2, :], mx[:, 2:4, :], ALU.max)
        v.tensor_tensor(mx[:, 0:1, :], mx[:, 0:1, :], mx[:, 1:2, :], ALU.max)
        v.tensor_tensor(Bp, Bp, mx[:, 0:1, :].to_broadcast([128, E, NBLK]), ALU.subtract)
        m_vo = ctx.m_allo[:].unsqueeze(1).to_broadcast([128, E, NOWN])
        s_vo = ctx.corr[:, 0:E].unsqueeze(2).to_broadcast([128, E, NOWN])
        sinv_vo = ctx.sinv_allo[:].unsqueeze(1).to_broadcast([128, E, NOWN])
        v.tensor_tensor(ctx.cmp_o[:], m_vo, s_vo, ALU.mult)
        v.tensor_tensor(Bpo, Bpo, ctx.cmp_o[:], ALU.subtract)
        v.tensor_tensor(Bpo, Bpo, sinv_vo, ALU.mult)
        mo = ctx.val_mxo[:]
        v.tensor_tensor(mo[:, 0:4, :], Bpo[:, 0:4, :], Bpo[:, 4:8, :], ALU.max)
        v.tensor_tensor(mo[:, 0:2, :], mo[:, 0:2, :], mo[:, 2:4, :], ALU.max)
        v.drain()
        v.tensor_tensor(mo[:, 0:1, :], mo[:, 0:1, :], mo[:, 1:2, :], ALU.max)
        v.drain()
        v.tensor_tensor(Bpo, Bpo, mo[:, 0:1, :].to_broadcast([128, E, NOWN]), ALU.subtract)
        sw_vo = ctx.corr[:, E:E + H].unsqueeze(2).to_broadcast([128, H, NOWN])
        m_vo16 = ctx.m_allo[:].unsqueeze(1).to_broadcast([128, H, NOWN])
        sinv_vo16 = ctx.sinv_allo[:].unsqueeze(1).to_broadcast([128, H, NOWN])
        v.tensor_tensor(ctx.awa_own[:], m_vo16, sw_vo, ALU.mult)
        v.drain()
        v.tensor_tensor(ctx.awa_own[:], awao, ctx.awa_own[:], ALU.subtract)
        v.drain()
        v.tensor_tensor(ctx.awa_own[:], ctx.awa_own[:], sinv_vo16, ALU.mult)
        v.drain()
        v.tensor_copy(ctx.m_own[:], ctx.m_allo[:])
        v.tensor_copy(ctx.sinv_own[:], ctx.sinv_allo[:])
        v.drain().then_inc(cC.h, 1)
        cC.n += 1

    @blk.scalar
    def _(s):
        s.wait_ge(cC.h, cC.n)
        s.activation(out=Bp, in_=Bp, func=AF.Exp)
        s.activation(out=Bpo, in_=Bpo, func=AF.Exp)
        s.drain().then_inc(cC.h, 1)
        cC.n += 1

    @blk.vector
    def _(v):
        v.wait_ge(cC.h, cC.n)
        mx = ctx.val_mx[:]
        v.tensor_tensor(mx[:, 0:4, :], Bp[:, 0:4, :], Bp[:, 4:8, :], ALU.add)
        v.tensor_tensor(mx[:, 0:2, :], mx[:, 0:2, :], mx[:, 2:4, :], ALU.add)
        v.tensor_tensor(mx[:, 0:1, :], mx[:, 0:1, :], mx[:, 1:2, :], ALU.add)
        v.reciprocal(mx[:, 1:2, :], mx[:, 0:1, :])
        v.tensor_tensor(Bp, Bp, mx[:, 1:2, :].to_broadcast([128, E, NBLK]), ALU.mult)
        mo = ctx.val_mxo[:]
        v.tensor_tensor(mo[:, 0:4, :], Bpo[:, 0:4, :], Bpo[:, 4:8, :], ALU.add)
        v.tensor_tensor(mo[:, 0:2, :], mo[:, 0:2, :], mo[:, 2:4, :], ALU.add)
        v.drain()
        v.tensor_tensor(mo[:, 0:1, :], mo[:, 0:1, :], mo[:, 1:2, :], ALU.add)
        v.drain()
        v.reciprocal(mo[:, 1:2, :], mo[:, 0:1, :])
        v.drain()
        v.tensor_tensor(Bpo, Bpo, mo[:, 1:2, :].to_broadcast([128, E, NOWN]), ALU.mult)
        v.memset(ctx.bis2[:, 0, :], 0.0)
        v.memset(ctx.bis2[:, 1, :], 1.0)
        v.memset(ctx.tau[:], 0.5)
        v.drain()
        v.drain().then_inc(cC.h, 1)
        cC.n += 1

    if 2 not in P_PARTS:
        return
    for it in range(NITER):
        @blk.vector
        def _(v):
            v.wait_ge(cC.h, cC.n)
            tau_v = ctx.tau[:].unsqueeze(2).to_broadcast([128, E, NBLK])
            v.tensor_tensor(ctx.cmp_f[:], Bp, tau_v, ALU.is_ge)
            v.tensor_reduce(ctx.cnt8[:], ctx.cmp_f[:], AX.X, ALU.add).then_inc(cD.h, 1)
            cD.n += 1

        @blk.tensor
        def _(t):
            t.wait_ge(cD.h, cD.n)
            t.matmul(ps[0:128, 3584:3584 + E], ctx.ones_f[:, 0:128], ctx.cnt8[:],
                     start=True, stop=True).then_inc(cD.h, 1)
            cD.n += 1

        @blk.vector
        def _(v):
            v.wait_ge(cD.h, cD.n)
            lo = ctx.bis2[:, 0, :]
            hi = ctx.bis2[:, 1, :]
            d1 = ctx.bis2[:, 2, :]
            g = ctx.bis2[:, 3, :]
            cntR = ctx.cnt8[:]
            mid = ctx.tau[:]
            v.tensor_copy(cntR, ps[0:128, 3584:3584 + E])
            v.drain()
            v.tensor_scalar(g, cntR, float(CAP), None, ALU.is_ge)
            v.tensor_tensor(d1, mid, lo, ALU.subtract)
            v.drain()
            v.tensor_tensor(d1, d1, g, ALU.mult)
            v.drain()
            v.tensor_tensor(lo, lo, d1, ALU.add)
            v.tensor_tensor(d1, hi, mid, ALU.subtract)
            v.drain()
            v.tensor_tensor(d1, d1, g, ALU.mult)
            v.drain()
            v.tensor_tensor(hi, mid, d1, ALU.add)
            v.drain()
            v.tensor_tensor(d1, lo, hi, ALU.add)
            v.drain()
            v.tensor_scalar(ctx.tau[:], d1, 0.5, None, ALU.mult).then_inc(cC.h, 1)
            cC.n += 1
    @blk.vector
    def _(v):
        v.wait_ge(cC.h, cC.n)
        v.drain()
        v.tensor_copy(ctx.tau[:], ctx.bis2[:, 0, :])
        v.drain().then_inc(cC.h, 1)
        cC.n += 1

    if 3 not in P_PARTS:
        return

    @blk.vector
    def _(v):
        v.wait_ge(cC.h, cC.n)
        tau_v = ctx.tau[:].unsqueeze(2).to_broadcast([128, E, NOWN])
        sel = ctx.cmp_o[:]
        mx = ctx.val_mxo[:]
        v.tensor_tensor(sel, Bpo, tau_v, ALU.is_ge)
        v.scalar_tensor_tensor(sel, sel, 2.0, Bpo, ALU.mult, ALU.add)
        v.tensor_tensor(mx[:, 0:4, :], sel[:, 0:4, :], sel[:, 4:8, :], ALU.max)
        v.tensor_tensor(mx[:, 0:2, :], mx[:, 0:2, :], mx[:, 2:4, :], ALU.max)
        v.tensor_tensor(mx[:, 0:1, :], mx[:, 0:1, :], mx[:, 1:2, :], ALU.max)
        seen = mx[:, 1:2, :]
        hit = mx[:, 2:3, :]
        v.memset(seen, 0.0)
        v.drain()
        for e in range(E):
            ga = ctx.gates_own[:, e:e + 1, :]
            v.tensor_tensor(hit, sel[:, e:e + 1, :], mx[:, 0:1, :], ALU.is_ge)
            v.drain()
            v.tensor_tensor(ga, hit, seen, ALU.subtract)
            v.drain()
            v.tensor_scalar_max(ga, ga, 0.0)
            v.tensor_tensor(seen, seen, hit, ALU.max)
            v.drain()
            v.tensor_tensor(ga, ga, Bpo[:, e:e + 1, :], ALU.mult)
            v.drain()
        v.drain().then_inc(cC.h, 1)
        cC.n += 1



def emit_seq(ctx, s):
    emit_C(ctx, s)
    barrier(ctx)
    if "D" not in SEQ_PHASES:
        return
    emit_D(ctx, s)
    barrier(ctx)
    if "E" not in SEQ_PHASES:
        return
    emit_E(ctx, s)
    barrier(ctx)
    if "F" not in SEQ_PHASES:
        return
    emit_F(ctx, s)
    barrier(ctx)
    emit_G(ctx, s)
    barrier(ctx)


def emit_msrow(ctx):
    blk, ps = ctx.blk, ctx.ps

    @blk.tensor
    def _(t):
        t.transpose(ps[0:32, 0:128], ctx.m_own[:], ctx.ident_f[:])
        t.transpose(ps[0:32, 128:256], ctx.sinv_own[:], ctx.ident_f[:]).then_inc(ctx.e1.h, 1)
        ctx.e1.n += 1

    @blk.vector
    def _(v):
        v.wait_ge(ctx.e1.h, ctx.e1.n)
        v.tensor_copy(ctx.wstage[0:32, 0:128], ps[0:32, 0:128])
        v.tensor_copy(ctx.wstage[0:32, 128:256], ps[0:32, 128:256]).then_inc(ctx.e2.h, 1)
        ctx.e2.n += 1

    @blk.sync
    def _(sy):
        sy.wait_ge(ctx.e2.h, ctx.e2.n)
        sy.dma_start(out=ctx.msrow_d[0:1, :], in_=ctx.wstage[0:32, 0:128]).then_inc(ctx.q1.h, 16)
        sy.dma_start(out=ctx.msrow_d[1:2, :], in_=ctx.wstage[0:32, 128:256]).then_inc(ctx.q1.h, 16)
        ctx.q1.n += 32
        sy.wait_ge(ctx.q1.h, ctx.q1.n)


def emit_C(ctx, s):
    blk, ps = ctx.blk, ctx.ps
    nc = ctx.nc
    q1, q2, e1, e2, e3, e4 = ctx.q1, ctx.q2, ctx.e1, ctx.e2, ctx.e3, ctx.e4

    @blk.vector
    def _(v):
        v.tensor_tensor(ctx.SP_T[:], ctx.awa_own[:, :, s * NCK:(s + 1) * NCK],
                        ctx.ba_t[:].unsqueeze(2).to_broadcast([128, H, NCK]),
                        ALU.add).then_inc(e1.h, 1)
        e1.n += 1

    @blk.scalar
    def _(sc):
        sc.wait_ge(e1.h, e1.n)
        sc.activation(out=ctx.SP_T[:], in_=ctx.SP_T[:], func=AF.Exp)
        sc.drain()
        sc.activation(out=ctx.SP_T[:], in_=ctx.SP_T[:], func=AF.Ln,
                      bias=ctx.cst[:, 1:2])
        sc.drain().then_inc(e1.h, 1)
        e1.n += 1

    @blk.tensor
    def _(t):
        t.wait_ge(e1.h, e1.n)
        for tt in range(NCK):
            t.transpose(ps[0:16, tt * 128:tt * 128 + 128],
                        ctx.SP_T[:, :, tt], ctx.ident_f[:])
        t.drain().then_inc(e2.h, 1)
        e2.n += 1

    @blk.vector
    def _(v):
        v.wait_ge(e2.h, e2.n)
        v.tensor_copy(ctx.EBUF[:], ps[0:16, 0:2048])
        v.drain()
        v.tensor_tensor_scan(ctx.CUM[:], ctx.EBUF[:],
                             ctx.cst[0:16, 2:3].to_broadcast([16, 2048]),
                             0.0, ALU.add, ALU.add)
        v.drain()
        v.memset(ctx.CB[:], 0.0)
        v.drain()
        v.tensor_copy(ctx.CB[:, 1:16], ctx.CUM[:, 127:2047:128])
        v.tensor_copy(ctx.CE[:], ctx.CUM[:, 127:2048:128])
        v.drain()
        v.tensor_tensor(ctx.EBUF[:],
                        ctx.CB[:].unsqueeze(2).to_broadcast([16, NCK, 128]),
                        ctx.CUM[:].rearrange("p (c i) -> p c i", c=NCK),
                        ALU.subtract)
        v.tensor_tensor(ctx.GAMS[:], ctx.CB[:], ctx.CE[:], ALU.subtract)
        v.drain().then_inc(e2.h, 1)
        e2.n += 1

    @blk.scalar
    def _(sc):
        sc.wait_ge(e2.h, e2.n)
        sc.activation(out=ctx.EBE[:], in_=ctx.EBUF[:], func=AF.Exp)
        sc.drain()
        sc.activation(out=ctx.GAMS[:], in_=ctx.GAMS[:], func=AF.Exp)
        sc.drain().then_inc(e3.h, 1)
        e3.n += 1

    @blk.vector
    def _(v):
        v.wait_ge(e3.h, e3.n)
        v.tensor_tensor(ctx.EBUF[:],
                        ctx.CUM[:].rearrange("p (c i) -> p c i", c=NCK),
                        ctx.CE[:].unsqueeze(2).to_broadcast([16, NCK, 128]),
                        ALU.subtract).then_inc(e3.h, 1)
        e3.n += 1

    @blk.scalar
    def _(sc):
        sc.wait_ge(e3.h, e3.n)
        sc.activation(out=ctx.EBUF[:], in_=ctx.EBUF[:], func=AF.Exp)
        sc.drain().then_inc(e3.h, 1)
        e3.n += 1

    @blk.tensor
    def _(t):
        t.wait_ge(e3.h, e3.n)
        t.transpose(ps[0:16, 512:528], ctx.GAMS[:], ctx.ident_f[0:16, 0:16]
                    ).then_inc(e3.h, 1)
        e3.n += 1

    @blk.vector
    def _(v):
        v.wait_ge(e3.h, e3.n)
        v.tensor_copy(ctx.GAMST[:], ps[0:16, 512:528]).then_inc(e3.h, 1)
        e3.n += 1

    @blk.sync
    def _(sy):
        sy.wait_ge(e3.h, e3.n)
        for j in range(2):
            sy.dma_start(out=ctx.G2[j:j + 1, :],
                         in_=ctx.GAMST[:, j:16:2],
                         allow_slow_non_contiguous=True).then_inc(q1.h, 16)
        q1.n += 32

    @blk.tensor
    def _(t):
        t.wait_ge(e3.h, e3.n)
        for c in range(NCK):
            t.transpose(ps[0:128, 2048 + c * 32:2048 + c * 32 + 16],
                        ctx.CUM[:, c * 128:(c + 1) * 128], ctx.ident_f[0:16, 0:16])
            t.transpose(ps[0:128, 3072 + c * 32:3072 + c * 32 + 16],
                        ctx.EBUF[:, c * 128:(c + 1) * 128], ctx.ident_f[0:16, 0:16])
        t.drain().then_inc(e4.h, 1)
        e4.n += 1
        t.wait_ge(q1.h, q1.n)
        t.matmul(ps[0:128, 1536:1664], ctx.exp2_f[:], ctx.G2[:],
                 start=True, stop=True).then_inc(e4.h, 1)
        e4.n += 1

    @blk.vector
    def _(v):
        v.wait_ge(e4.h, e4.n)
        for c in range(NCK):
            v.tensor_copy(ctx.CUMT[:, c, :], ps[0:128, 2048 + c * 32:2048 + c * 32 + 16])
            v.tensor_copy(ctx.EDET[:, c, :], ps[0:128, 3072 + c * 32:3072 + c * 32 + 16])
        v.tensor_copy(ctx.GAM2[:], ps[0:128, 1536:1664])
        v.drain().then_inc(e4.h, 1)
        e4.n += 1

    xho = ctx.xhi_own
    base_q2 = q2.n
    base_e5 = ctx.e5.n
    base_e6 = ctx.e6.n

    @blk.scalar
    def _(sc):
        for t4 in range(4):
            if t4 >= 1:
                sc.wait_ge(ctx.e5.h, base_e5 + t4)
            tok0 = s * S + t4 * 512
            sc.dma_start(out=ctx.MSLm[:, (t4 % 2) * 512:(t4 % 2) * 512 + 512],
                         in_=ctx.msrow_d[0:1, s * S + t4 * 512: s * S + (t4 + 1) * 512]
                         ).then_inc(q2.h, 16)
            sc.dma_start(out=ctx.MSLs[:, (t4 % 2) * 512:(t4 % 2) * 512 + 512],
                         in_=ctx.msrow_d[1:2, s * S + t4 * 512: s * S + (t4 + 1) * 512]
                         ).then_inc(q2.h, 16)
            for ch in range(NCH):
                k = t4 * NCH + ch
                if k >= 2:
                    sc.wait_ge(ctx.e6.h, base_e6 + (k - 2) + 1)
                sc.dma_start_transpose(
                    ctx.XFM[:, k % 2],
                    xho[s * S + t4 * 512:s * S + (t4 + 1) * 512,
                        ch * 128:(ch + 1) * 128]).then_inc(q2.h, 16)
        q2.n += 16 * 4 * (NCH + 2)

    @blk.tensor
    def _(t, _e4n=ctx.e4.n):
        t.wait_ge(ctx.e4.h, _e4n)
        for t4 in range(4):
            t.wait_ge(q2.h, base_q2 + 16 * (t4 * (NCH + 2) + 2))
            if t4 >= 1:
                t.wait_ge(ctx.e6.h, base_e6 + t4 * NCH)
            t.matmul(ps[:, 3072:3584], ctx.ones_f[0:1, :],
                     ctx.MSLm[:, (t4 % 2) * 512:(t4 % 2) * 512 + 512],
                     start=True, stop=True)
            t.matmul(ps[:, 3584:4096], ctx.ones_f[0:1, :],
                     ctx.MSLs[:, (t4 % 2) * 512:(t4 % 2) * 512 + 512],
                     start=True, stop=True).then_inc(ctx.e5.h, 1)
        ctx.e5.n += 4

    @blk.vector
    def _(v):
        for t4 in range(4):
            for ch in range(NCH):
                k = t4 * NCH + ch
                v.wait_ge(q2.h, base_q2 + 16 * (t4 * (NCH + 2) + 2 + ch + 1))
                v.wait_ge(ctx.e5.h, base_e5 + t4 + 1)
                v.tensor_tensor(ctx.LN_TMP[:], ctx.XFM[:, k % 2],
                                ps[:, 3072:3584], ALU.subtract)
                v.tensor_tensor(ctx.big0[:, ch * 2048 + t4 * 512:
                                         ch * 2048 + (t4 + 1) * 512],
                                ctx.LN_TMP[:], ps[:, 3584:4096],
                                ALU.mult).then_inc(ctx.e6.h, 1)
        ctx.e6.n += 4 * NCH

    emit_proj(ctx, s)


def emit_proj(ctx, s):
    blk, ps = ctx.blk, ctx.ps
    q1, e7, e8 = ctx.q1, ctx.e7, ctx.e8
    wbuf = ctx.big4.ap()[:, 0:8192].rearrange("p (k o c) -> p k o c", k=NCH, o=NDT)
    xnv = ctx.big0.ap().rearrange("p (c t) -> p c t", c=NCH)

    for wi, (wsrc, dstbig, act) in enumerate([
            (ctx.wq, ctx.big1, "silu"), (ctx.wk, ctx.big2, "silu"),
            (ctx.wv, ctx.big3, "copy")]):
        base_q1 = q1.n
        base_e7 = e7.n
        base_e8 = e8.n

        @blk.sync
        def _(sy, wsrc=wsrc, base_e8=base_e8, base_e7=base_e7, wi=wi):
            for k in range(NCH):
                if k >= 1:
                    sy.wait_ge(e8.h, base_e8 + k)
                elif wi == 0:
                    sy.wait_ge(ctx.e6.h, ctx.e6.n)
                else:
                    sy.wait_ge(e8.h, base_e8)
                    sy.wait_ge(e7.h, base_e7)
                sy.dma_start(out=ctx.WST2[:],
                             in_=wsrc[k * 128:(k + 1) * 128, :]).then_inc(q1.h, 16)
            q1.n += 16 * NCH

        @blk.vector
        def _(v, base_q1=base_q1, wi=wi):
            for k in range(NCH):
                v.wait_ge(q1.h, base_q1 + 16 * (k + 1))
                if wi > 0 or k > 0:
                    pass
                v.tensor_copy(wbuf[:, k, :, :], ctx.WST2[:]).then_inc(e8.h, 1)
            e8.n += NCH

        if act == "silu":
            @blk.tensor
            def _(t, base_e7=base_e7, dstbig=dstbig):
                t.wait_ge(e8.h, e8.n)
                t.wait_ge(e7.h, base_e7)
                for oc in range(NDT):
                    if oc >= 1:
                        t.wait_ge(e7.h, base_e7 + 2 * oc)
                    for k in range(NCH):
                        for t4 in range(4):
                            t.matmul(ps[:, (t4 % 4) * 512:(t4 % 4) * 512 + 512],
                                     wbuf[:, k, oc, :], xnv[:, k, t4 * 512:(t4 + 1) * 512],
                                     start=(k == 0), stop=(k == NCH - 1))
                    t.drain().then_inc(e7.h, 1)

            @blk.scalar
            def _(sc, base_e7=base_e7, dstbig=dstbig):
                for oc in range(NDT):
                    sc.wait_ge(e7.h, base_e7 + 2 * oc + 1)
                    for t4 in range(4):
                        sc.activation(out=dstbig[:, oc * 2048 + t4 * 512:
                                                 oc * 2048 + (t4 + 1) * 512],
                                      in_=ps[:, (t4 % 4) * 512:(t4 % 4) * 512 + 512],
                                      func=AF.Silu)
                    sc.drain().then_inc(e7.h, 1)
            e7.n += 2 * NDT
        else:
            @blk.tensor
            def _(t, base_e7=base_e7, _e4n=ctx.e4.n):
                t.wait_ge(e8.h, e8.n)
                t.wait_ge(e7.h, base_e7)
                t.wait_ge(ctx.e4.h, _e4n)
                wvk = ctx.big4.ap()[:, 0:8192].rearrange("p (k d) -> p k d", k=NCH)
                for tc in range(NCK):
                    if tc >= 1:
                        t.wait_ge(e7.h, base_e7 + 2 * tc)
                    for k in range(NCH):
                        t.matmul(ps[:, 2048:2560], xnv[:, k, tc * 128:(tc + 1) * 128],
                                 wvk[:, k, 0:512], start=(k == 0), stop=(k == NCH - 1))
                        t.matmul(ps[:, 2560:3072], xnv[:, k, tc * 128:(tc + 1) * 128],
                                 wvk[:, k, 512:1024], start=(k == 0), stop=(k == NCH - 1))
                    t.drain().then_inc(e7.h, 1)

            @blk.vector
            def _(v, base_e7=base_e7):
                for tc in range(NCK):
                    v.wait_ge(e7.h, base_e7 + 2 * tc + 1)
                    v.tensor_copy(ctx.big3[:, tc * 1024:(tc + 1) * 1024],
                                  ps[:, 2048:3072]).then_inc(e7.h, 1)
            e7.n += 2 * NCK


def emit_D(ctx, s):
    blk, ps = ctx.blk, ctx.ps
    psb = ctx.ps.ap().bitcast(BF16)
    q1 = ctx.q1
    sems = {k: getattr(ctx, k) for k in
            ["e1", "e2", "e3", "e4", "e5", "e6", "e7", "e8", "e9", "e10", "e11", "e12"]}
    B = {k: v.n for k, v in sems.items()}
    bq1 = q1.n
    Q = ctx.big1.ap().rearrange("p (d t) -> p d t", d=NDT)
    K = ctx.big2.ap().rearrange("p (d t) -> p d t", d=NDT)
    V = ctx.big3.ap().rearrange("p (c d) -> p c d", c=NCK)
    O = ctx.big4.ap().rearrange("p (d t) -> p d t", d=NDT)

    NCKD = NCK if D_NCK is None else D_NCK

    @blk.vector
    def _(v):
        v.memset(ctx.S_F[:], 0.0)
        v.memset(ctx.S_B[:], 0.0)
        v.drain().then_inc(sems["e12"].h, 1)
    sems["e12"].n += 1
    B["e12"] = sems["e12"].n - 1

    @blk.sync
    def _(sy):
        for c in range(NCKD):
            if c >= 1:
                sy.wait_ge(sems["e1"].h, B["e1"] + c)
            sy.dma_start(out=ctx.T1A[0:1, :],
                         in_=ctx.CUM[0:4, c * 128:(c + 1) * 128]).then_inc(q1.h, 16)
            sy.dma_start(out=ctx.T1A[32:33, :],
                         in_=ctx.CUM[4:8, c * 128:(c + 1) * 128]).then_inc(q1.h, 16)
            sy.dma_start(out=ctx.T1A[64:65, :],
                         in_=ctx.CUM[8:12, c * 128:(c + 1) * 128]).then_inc(q1.h, 16)
            sy.dma_start(out=ctx.T1B[0:1, :],
                         in_=ctx.CUM[12:16, c * 128:(c + 1) * 128]).then_inc(q1.h, 16)
        q1.n += 64 * NCKD

    @blk.tensor
    def _(t):
        e = sems
        for c in range(NCKD):
            if D_SUB >= 1:
                t.wait_ge(q1.h, bq1 + 64 * (c + 1))
                if c >= 1:
                    t.wait_ge(e["e7"].h, B["e7"] + c)
                    t.wait_ge(e["e2"].h, B["e2"] + c)
                pairs = [(ctx.ones_f[0:1, :], ctx.T1A[0:1, :]),
                         (ctx.ones_f[32:33, :], ctx.T1A[32:33, :]),
                         (ctx.ones_f[64:65, :], ctx.T1A[64:65, :]),
                         (ctx.ones_f[0:1, :], ctx.T1B[0:1, :])]
                tps = [(0, 0), (32, 0), (64, 0), (0, 0)]
                for q in range(4):
                    t.matmul(ps[:, 2048 + q * 512:2048 + (q + 1) * 512],
                             pairs[q][0], pairs[q][1], start=True, stop=True,
                             tile_position=tps[q])
                t.drain().then_inc(e["e1"].h, 1)
            if D_SUB >= 4:
                t.wait_ge(e["e2"].h, B["e2"] + c + 1)
                for par in range(2):
                    for h in range(par, H, 2):
                        pb = par * 64
                        dt = h // 2
                        t.matmul(ps[:, 2048 + h * 128:2048 + (h + 1) * 128],
                                 K[pb:pb + 64, dt, c * 128:(c + 1) * 128],
                                 Q[pb:pb + 64, dt, c * 128:(c + 1) * 128],
                                 start=True, stop=True,
                                 tile_position=(pb, 0))
                    t.drain()
                t.drain().then_inc(e["e4"].h, 1)
            if D_SUB >= 5:
                if c >= 1:
                    t.wait_ge(e["e9"].h, B["e9"] + c)
                for dt in range(NDT):
                    t.transpose(psb[:, 3072 + dt * 128:3072 + (dt + 1) * 128],
                                K[:, dt, c * 128:(c + 1) * 128], ctx.ident_b[:])
                t.drain().then_inc(e["e8"].h, 1)
            if D_SUB >= 7:
                t.wait_ge(e["e5"].h, B["e5"] + c + 1)
                for dt in range(NDT):
                    t.matmul(ps[:, 2048 + dt * 128:2048 + (dt + 1) * 128],
                             ctx.e16[:, dt, :], ctx.EBE[:, c * 128:(c + 1) * 128],
                             start=True, stop=True)
                t.drain().then_inc(e["e6"].h, 1)
            if D_SUB >= 10:
                t.wait_ge(e["e7"].h, B["e7"] + c + 1)
                t.wait_ge(e["e9"].h, B["e9"] + c + 1)
                t.wait_ge(e["e12"].h, B["e12"] + c + 1)
                if c >= 1:
                    t.wait_ge(e["e11"].h, B["e11"] + 2 * c)
                for h in range(H):
                    dt = h // 2
                    pb = (h % 2) * 64
                    if D_MMA:
                        t.matmul(ps[pb:pb + 64, dt * 128:(dt + 1) * 128],
                                 V[:, c, h * 64:(h + 1) * 64],
                                 ctx.A_SB[:, h * 128:(h + 1) * 128],
                                 start=True, stop=(not D_MMB), tile_position=(0, pb))
                    if D_MMB:
                        t.matmul(ps[pb:pb + 64, dt * 128:(dt + 1) * 128],
                                 ctx.S_B[pb:pb + 64, dt * 64:(dt + 1) * 64],
                                 ctx.PA[pb:pb + 64, dt, :],
                                 start=(not D_MMA), stop=True, tile_position=(pb, pb))
                for h in range(H):
                    dt = h // 2
                    pb = (h % 2) * 64
                    t.matmul(ps[pb:pb + 64, 1024 + dt * 64:1024 + (dt + 1) * 64],
                             ctx.KDEC.rearrange("p (h d) -> p h d", h=H)[:, h, :],
                             V[:, c, h * 64:(h + 1) * 64],
                             start=True, stop=True, tile_position=(0, pb))
                t.drain().then_inc(e["e10"].h, 1)
        for k, lvl in [("e1", 1), ("e4", 4), ("e8", 5), ("e6", 7), ("e10", 10)]:
            if D_SUB >= lvl:
                sems[k].n += NCKD

    @blk.scalar
    def _(sc):
        e = sems
        for c in range(NCKD):
            if D_SUB >= 3:
                sc.wait_ge(e["e2"].h, B["e2"] + c + 1)
                sc.activation(out=ctx.EXPD[:], in_=ctx.EXPD[:], func=AF.Exp)
                sc.drain().then_inc(e["e3"].h, 1)
        if D_SUB >= 3:
            sems["e3"].n += NCKD

    @blk.vector
    def _(v):
        e = sems
        for c in range(NCKD):
            if D_SUB >= 2:
                v.wait_ge(e["e1"].h, B["e1"] + c + 1)
                v.tensor_tensor(ctx.EXPD.rearrange("p (h i) -> p h i", h=H),
                                ctx.CUMT[:, c, :].unsqueeze(2).to_broadcast([128, H, 128]),
                                ps[:, 2048:4096].rearrange("p (h i) -> p h i", h=H),
                                ALU.subtract)
                v.tensor_tensor(ctx.EXPD.rearrange("p (h i) -> p h i", h=H),
                                ctx.EXPD.rearrange("p (h i) -> p h i", h=H),
                                ctx.maskadd[:].unsqueeze(1).to_broadcast([128, H, 128]),
                                ALU.add)
                v.drain().then_inc(e["e2"].h, 1)
            if D_SUB >= 6:
                v.wait_ge(e["e4"].h, B["e4"] + c + 1)
                v.wait_ge(e["e3"].h, B["e3"] + c + 1)
                if c >= 1:
                    v.wait_ge(e["e10"].h, B["e10"] + c)
                v.tensor_tensor(ctx.A_SB[:], ps[:, 2048:4096], ctx.EXPD[:],
                                ALU.mult).then_inc(e["e5"].h, 1)
            if D_SUB >= 8:
                v.wait_ge(e["e6"].h, B["e6"] + c + 1)
                for dt in range(NDT):
                    v.tensor_tensor(ctx.PA[:, dt, :], Q[:, dt, c * 128:(c + 1) * 128],
                                    ps[:, 2048 + dt * 128:2048 + (dt + 1) * 128],
                                    ALU.mult)
                v.drain().then_inc(e["e7"].h, 1)
            if D_SUB >= 9:
                v.wait_ge(e["e8"].h, B["e8"] + c + 1)
                v.tensor_tensor(ctx.KDEC.rearrange("p (h d) -> p h d", h=H),
                                psb[:, 3072:4096].rearrange("p (h d) -> p h d", h=H),
                                ctx.EDET[:, c, :].unsqueeze(2).to_broadcast([128, H, 64]),
                                ALU.mult).then_inc(e["e9"].h, 1)
            if D_SUB >= 11:
                v.wait_ge(e["e10"].h, B["e10"] + c + 1)
                v.tensor_copy(O[:, 0:4, c * 128:(c + 1) * 128], ps[:, 0:512])
                v.tensor_copy(O[:, 4:8, c * 128:(c + 1) * 128], ps[:, 512:1024])
                v.drain().then_inc(e["e11"].h, 1)
                gv = ctx.GAM2[:, c * 8:(c + 1) * 8].unsqueeze(2).to_broadcast([128, NDT, 64])
                v.tensor_tensor(ctx.S_F.rearrange("p (d v) -> p d v", d=NDT),
                                ctx.S_F.rearrange("p (d v) -> p d v", d=NDT), gv, ALU.mult)
                v.drain()
                v.tensor_tensor(ctx.S_F[:], ctx.S_F[:], ps[:, 1024:1536], ALU.add)
                v.drain().then_inc(e["e11"].h, 1)
                v.tensor_copy(ctx.S_B[:], ctx.S_F[:])
                v.drain().then_inc(e["e12"].h, 1)
        for k, amt, lvl in [("e2", NCKD, 2), ("e5", NCKD, 6), ("e7", NCKD, 8),
                            ("e9", NCKD, 9), ("e11", 2 * NCKD, 11), ("e12", NCKD, 11)]:
            if D_SUB >= lvl:
                sems[k].n += amt


def emit_E(ctx, s):
    blk, ps = ctx.blk, ctx.ps
    e1, e2, e3, q1 = ctx.e1, ctx.e2, ctx.e3, ctx.q1
    O = ctx.big4.ap().rearrange("p (d t) -> p d t", d=NDT)

    @blk.vector
    def _(v):
        for dt in range(NDT):
            od = ctx.big4.ap()[:, dt * 2048:(dt + 1) * 2048]
            v.tensor_reduce(ctx.ECOLS[:, dt:dt + 1], od, AX.X, ALU.add)
            v.tensor_tensor(ctx.EXPD[:], od, od, ALU.mult)
            v.tensor_reduce(ctx.ECOLS[:, 8 + dt:9 + dt], ctx.EXPD[:], AX.X, ALU.add)
        v.drain().then_inc(e1.h, 1)
        e1.n += 1

    @blk.tensor
    def _(t):
        t.wait_ge(e1.h, e1.n)
        t.matmul(ps[0:2, 0:16], ctx.e2t[:], ctx.ECOLS[:, 0:16],
                 start=True, stop=True).then_inc(e1.h, 1)
        e1.n += 1

    @blk.vector
    def _(v):
        v.wait_ge(e1.h, e1.n)
        st = ctx.GAMS
        NI = 1.0 / (S * DH)
        v.tensor_scalar(st[0:2, 0:16], ps[0:2, 0:16], NI, None, ALU.mult)
        v.drain()
        v.tensor_tensor(ctx.CB[0:2, 0:8], st[0:2, 0:8], st[0:2, 0:8], ALU.mult)
        v.drain()
        v.tensor_tensor(st[0:2, 8:16], st[0:2, 8:16], ctx.CB[0:2, 0:8], ALU.subtract)
        v.drain().then_inc(e2.h, 1)
        e2.n += 1

    @blk.scalar
    def _(sc):
        sc.wait_ge(e2.h, e2.n)
        sc.activation(out=ctx.GAMS[0:2, 8:16], in_=ctx.GAMS[0:2, 8:16], func=AF.Ln,
                      bias=ctx.cst[0:2, 0:1])
        sc.drain()
        sc.activation(out=ctx.GAMS[0:2, 8:16], in_=ctx.GAMS[0:2, 8:16], func=AF.Exp,
                      scale=-0.5)
        sc.drain().then_inc(e2.h, 1)
        e2.n += 1

    @blk.tensor
    def _(t):
        t.wait_ge(e2.h, e2.n)
        t.matmul(ps[0:128, 512:520], ctx.exp2_f[:], ctx.GAMS[0:2, 0:8],
                 start=True, stop=True)
        t.matmul(ps[0:128, 520:528], ctx.exp2_f[:], ctx.GAMS[0:2, 8:16],
                 start=True, stop=True).then_inc(e2.h, 1)
        e2.n += 1

    @blk.vector
    def _(v):
        v.wait_ge(e2.h, e2.n)
        v.tensor_tensor(ctx.ecol[:], ps[0:128, 520:528], ctx.gnw_t[:], ALU.mult)
        v.drain()
        v.tensor_tensor(ctx.ecol2[:], ps[0:128, 512:520], ctx.ecol[:], ALU.mult)
        v.drain()
        v.tensor_tensor(ctx.ecol2[:], ctx.gnb_t[:], ctx.ecol2[:], ALU.subtract)
        v.drain().then_inc(e3.h, 1)
        e3.n += 1

    @blk.vector
    def _(v):
        v.wait_ge(e3.h, e3.n)
        for dt in range(NDT):
            od = ctx.big4.ap()[:, dt * 2048:(dt + 1) * 2048]
            v.tensor_scalar(od, od, ctx.ecol[:, dt:dt + 1], ctx.ecol2[:, dt:dt + 1],
                            ALU.mult, ALU.add)
        v.drain().then_inc(e3.h, 1)
        e3.n += 1

    wbufe = ctx.big2.ap()[:, 0:8192].rearrange("p (k o c) -> p k o c", k=NCH, o=NDT)
    xnv = ctx.big0.ap().rearrange("p (c t) -> p c t", c=NCH)
    bq1 = q1.n
    be4 = ctx.e4.n

    @blk.sync
    def _(sy):
        for k in range(NCH):
            if k >= 1:
                sy.wait_ge(ctx.e4.h, be4 + k)
            else:
                sy.wait_ge(e3.h, e3.n)
            sy.dma_start(out=ctx.WST2f[:],
                         in_=ctx.wga[k * 128:(k + 1) * 128, :]).then_inc(q1.h, 16)
        q1.n += 16 * NCH

    @blk.vector
    def _(v):
        for k in range(NCH):
            v.wait_ge(q1.h, bq1 + 16 * (k + 1))
            v.tensor_copy(wbufe[:, k, :, :], ctx.WST2f[:]).then_inc(ctx.e4.h, 1)
        ctx.e4.n += NCH

    be5 = ctx.e5.n

    @blk.tensor
    def _(t):
        t.wait_ge(ctx.e4.h, ctx.e4.n)
        for oc in range(NDT):
            for t4 in range(4):
                if oc * 4 + t4 >= 2:
                    t.wait_ge(ctx.e5.h, be5 + 2 * (oc * 4 + t4) - 2)
                for k in range(NCH):
                    t.matmul(ps[:, ((oc * 4 + t4) % 2) * 512:
                                ((oc * 4 + t4) % 2) * 512 + 512],
                             wbufe[:, k, oc, :], xnv[:, k, t4 * 512:(t4 + 1) * 512],
                             start=(k == 0), stop=(k == NCH - 1))
                t.drain().then_inc(ctx.e5.h, 1)
        ctx.e5.n += 32

    be6 = ctx.e6.n

    @blk.scalar
    def _(sc):
        for i in range(32):
            sc.wait_ge(ctx.e5.h, be5 + 2 * i + 1)
            if i >= 1:
                sc.wait_ge(ctx.e6.h, be6 + i)
            sc.activation(out=ctx.GT[:], in_=ps[:, (i % 2) * 512:(i % 2) * 512 + 512],
                          func=AF.Sigmoid)
            sc.drain().then_inc(ctx.e5.h, 1)
        ctx.e5.n += 32

    @blk.vector
    def _(v):
        for i in range(32):
            oc, t4 = i // 4, i % 4
            v.wait_ge(ctx.e5.h, be5 + 2 * i + 2)
            sl = slice(oc * 2048 + t4 * 512, oc * 2048 + (t4 + 1) * 512)
            v.tensor_tensor(ctx.big1.ap()[:, sl], ctx.big4.ap()[:, sl], ctx.GT[:],
                            ALU.mult).then_inc(ctx.e6.h, 1)
        ctx.e6.n += 32


def emit_F(ctx, s):
    blk, ps = ctx.blk, ctx.ps
    q1, q3 = ctx.q1, ctx.q3
    f1, f2, f3, f4 = ctx.f1, ctx.f2, ctx.f3, ctx.f4
    e1 = ctx.e1
    xnv = ctx.big0.ap().rearrange("p (c t) -> p c t", c=NCH)
    W1 = ctx.big3.ap().rearrange("p (k f) -> p k f", k=NCH)
    W2 = ctx.big4.ap().rearrange("p (f d) -> p f d", f=NCK)
    gfm = ctx.G_FM
    acc = ctx.big2.ap()
    bf1, bf2, bf3, bf4 = f1.n, f2.n, f3.n, f4.n

    def w_chunks(e):
        w1f = [ctx.w1[e][(k // 2) * 128:(k // 2 + 1) * 128,
                         (k % 2) * 1024:(k % 2 + 1) * 1024] for k in range(16)]
        w1b = [ctx.w1_b[e][(k // 2) * 128:(k // 2 + 1) * 128,
                           (k % 2) * 1024:(k % 2 + 1) * 1024] for k in range(16)]
        w1d = [W1[:, k // 2, (k % 2) * 1024:(k % 2 + 1) * 1024] for k in range(16)]
        w2f = [ctx.w2[e][k * 128:(k + 1) * 128, :] for k in range(NCK)]
        w2b = [ctx.w2_b[e][k * 128:(k + 1) * 128, :] for k in range(NCK)]
        w2d = [W2[:, k, :] for k in range(NCK)]
        return w1f, w1b, w1d, w2f, w2b, w2d

    be1 = e1.n

    for e in range(E):
        w1f, w1b, w1d, w2f, w2b, w2d = w_chunks(e)
        if s == 0:
            @blk.sync
            def _(sy, e=e, w1f=w1f, w2f=w2f):
                for k in range(16):
                    sy.wait_ge(e1.h, be1 + 32 * e + k)
                    if k == 0:
                        sy.wait_ge(f1.h, bf1 + 8 * e)
                    sy.dma_start(out=ctx.WST2f[:], in_=w1f[k]).then_inc(q1.h, 16)
                for k in range(NCK):
                    sy.wait_ge(e1.h, be1 + 32 * e + 16 + k)
                    if k == 0:
                        sy.wait_ge(f3.h, bf3 + 16 * e)
                    sy.dma_start(out=ctx.WST2f[:], in_=w2f[k]).then_inc(q1.h, 16)
                q1.n += 16 * 32

            @blk.vector
            def _(v, e=e, w1d=w1d, w2d=w2d):
                bq = q1.n - 16 * 32
                for k in range(32):
                    v.wait_ge(q1.h, bq + 16 * (k + 1))
                    dst = w1d[k] if k < 16 else w2d[k - 16]
                    v.tensor_copy(dst, ctx.WST2f[:]).then_inc(e1.h, 1)
                e1.n += 32

            @blk.gpsimd
            def _(g, e=e, w1b=w1b, w2b=w2b, w1d=w1d, w2d=w2d):
                for k in range(32):
                    g.wait_ge(e1.h, be1 + 32 * e + k + 1)
                    src = w1d[k] if k < 16 else w2d[k - 16]
                    dstc = w1b[k] if k < 16 else w2b[k - 16]
                    g.dma_start(out=dstc, in_=src).then_inc(q3.h, 16)
                q3.n += 16 * 32
        else:
            @blk.sync
            def _(sy, e=e, w1b=w1b, w2b=w2b, w1d=w1d, w2d=w2d):
                for k in range(32):
                    if k == 0:
                        sy.wait_ge(f1.h, bf1 + 8 * e)
                    if k == 16:
                        sy.wait_ge(f3.h, bf3 + 16 * e)
                    dst = w1d[k] if k < 16 else w2d[k - 16]
                    src = w1b[k] if k < 16 else w2b[k - 16]
                    sy.dma_start(out=dst, in_=src).then_inc(q1.h, 16)
                q1.n += 16 * 32

            @blk.vector
            def _(v):
                v.wait_ge(q1.h, q1.n)
                v.drain().then_inc(e1.h, 32)
                e1.n += 32

        @blk.tensor
        def _(t, e=e):
            for t4 in range(4):
                for fh in range(2):
                    i1 = (e * 4 + t4) * 2 + fh
                    t.wait_ge(e1.h, be1 + 32 * e + 16)
                    if i1 >= 1:
                        t.wait_ge(f2.h, bf2 + i1)
                    t.wait_ge(f4.h, bf4 + 4 * (e * 4 + t4))
                    for fc in range(NDT):
                        for k in range(NCH):
                            t.matmul(ps[:, fc * 512:(fc + 1) * 512],
                                     W1[:, k, (fh * 8 + fc) * 128:(fh * 8 + fc + 1) * 128],
                                     xnv[:, k, t4 * 512:(t4 + 1) * 512],
                                     start=(k == 0), stop=(k == NCH - 1))
                    t.drain().then_inc(f1.h, 1)
                t.wait_ge(f2.h, bf2 + (e * 4 + t4) * 2 + 2)
                t.wait_ge(e1.h, be1 + 32 * e + 32)
                for ts in range(4):
                    tc = t4 * 4 + ts
                    i2 = e * 16 + tc
                    if i2 >= 2:
                        t.wait_ge(f4.h, bf4 + i2 - 1)
                    pbase = (tc % 2) * 1024
                    for fc in range(NCK):
                        t.matmul(ps[:, pbase:pbase + 512],
                                 gfm[:, fc, ts * 128:(ts + 1) * 128],
                                 W2[:, fc, 0:512], start=(fc == 0), stop=(fc == NCK - 1))
                        t.matmul(ps[:, pbase + 512:pbase + 1024],
                                 gfm[:, fc, ts * 128:(ts + 1) * 128],
                                 W2[:, fc, 512:1024], start=(fc == 0), stop=(fc == NCK - 1))
                    t.drain().then_inc(f3.h, 1)

        @blk.scalar
        def _(sc, e=e):
            for t4 in range(4):
                for fh in range(2):
                    i1 = (e * 4 + t4) * 2 + fh
                    sc.wait_ge(f1.h, bf1 + i1 + 1)
                    if e * 4 + t4 >= 1:
                        sc.wait_ge(f3.h, bf3 + 4 * (e * 4 + t4))
                    for fc in range(NDT):
                        sc.activation(out=gfm[:, fh * 8 + fc, :],
                                      in_=ps[:, fc * 512:(fc + 1) * 512], func=AF.Gelu)
                    sc.drain().then_inc(f2.h, 1)

        @blk.vector
        def _(v, e=e):
            for t4 in range(4):
                for ts in range(4):
                    tc = t4 * 4 + ts
                    i2 = e * 16 + tc
                    v.wait_ge(f3.h, bf3 + i2 + 1)
                    pbase = (tc % 2) * 1024
                    gate = ctx.gates_own[:, e, s * NCK + tc:s * NCK + tc + 1]
                    asl = acc[:, tc * 1024:(tc + 1) * 1024]
                    if e == 0:
                        v.tensor_scalar(asl[:, 0:512], ps[:, pbase:pbase + 512],
                                        gate, None, ALU.mult)
                        v.tensor_scalar(asl[:, 512:1024], ps[:, pbase + 512:pbase + 1024],
                                        gate, None, ALU.mult).then_inc(f4.h, 1)
                    else:
                        v.scalar_tensor_tensor(asl[:, 0:512], ps[:, pbase:pbase + 512],
                                               gate, asl[:, 0:512], ALU.mult, ALU.add)
                        v.scalar_tensor_tensor(asl[:, 512:1024],
                                               ps[:, pbase + 512:pbase + 1024],
                                               gate, asl[:, 512:1024],
                                               ALU.mult, ALU.add).then_inc(f4.h, 1)
        f1.n += 8
        f2.n += 8
        f3.n += 16
        f4.n += 16


def emit_G(ctx, s):
    blk, ps = ctx.blk, ctx.ps
    q1, q2, q3 = ctx.q1, ctx.q2, ctx.q3
    g1, g2, g3 = ctx.e7, ctx.e8, ctx.e9
    og = ctx.big1.ap().rearrange("p (k t) -> p k t", k=NCH)
    acc = ctx.big2.ap()
    wbufg = ctx.big3.ap()[:, 0:8192].rearrange("p (k d) -> p k d", k=NCH)
    bg1, bg2, bg3 = g1.n, g2.n, g3.n
    bq1, bq2, bq3 = q1.n, q2.n, q3.n
    xo = ctx.x_own
    outv = ctx.out

    @blk.sync
    def _(sy):
        for k in range(NCH):
            if k >= 1:
                sy.wait_ge(g1.h, bg1 + k)
            sy.dma_start(out=ctx.WST2f[:],
                         in_=ctx.wo[k * 128:(k + 1) * 128, :]).then_inc(q1.h, 16)
        q1.n += 16 * NCH

    @blk.vector
    def _(v):
        for k in range(NCH):
            v.wait_ge(q1.h, bq1 + 16 * (k + 1))
            v.tensor_copy(wbufg[:, k, :], ctx.WST2f[:]).then_inc(g1.h, 1)
        g1.n += NCH

    @blk.scalar
    def _(sc):
        for tc in range(NCK):
            if tc >= 2:
                sc.wait_ge(g3.h, bg3 + tc - 1)
            sc.dma_start(out=ctx.XG[:, tc % 2],
                         in_=xo[s * S + tc * 128:s * S + (tc + 1) * 128, :]
                         ).then_inc(q2.h, 16)
        q2.n += 16 * NCK

    @blk.tensor
    def _(t):
        t.wait_ge(g1.h, g1.n)
        for tc in range(NCK):
            if tc >= 2:
                t.wait_ge(g3.h, bg3 + tc - 1)
            pbase = (tc % 2) * 1024
            for k in range(NCH):
                t.matmul(ps[:, pbase:pbase + 512], og[:, k, tc * 128:(tc + 1) * 128],
                         wbufg[:, k, 0:512], start=(k == 0), stop=(k == NCH - 1))
                t.matmul(ps[:, pbase + 512:pbase + 1024],
                         og[:, k, tc * 128:(tc + 1) * 128],
                         wbufg[:, k, 512:1024], start=(k == 0), stop=(k == NCH - 1))
            t.drain().then_inc(g2.h, 1)
        g2.n += NCK

    @blk.vector
    def _(v):
        for tc in range(NCK):
            v.wait_ge(g2.h, bg2 + tc + 1)
            v.wait_ge(q2.h, bq2 + 16 * (tc + 1))
            if tc >= 2:
                v.wait_ge(q3.h, bq3 + 16 * (tc - 1))
            pbase = (tc % 2) * 1024
            o1 = ctx.OUTST[:, tc % 2]
            for hf in range(2):
                v.tensor_tensor(o1[:, hf * 512:(hf + 1) * 512],
                                ps[:, pbase + hf * 512:pbase + (hf + 1) * 512],
                                ctx.XG[:, tc % 2][:, hf * 512:(hf + 1) * 512], ALU.add)
                v.tensor_tensor(o1[:, hf * 512:(hf + 1) * 512],
                                o1[:, hf * 512:(hf + 1) * 512],
                                acc[:, tc * 1024 + hf * 512:tc * 1024 + (hf + 1) * 512],
                                ALU.add)
            v.drain().then_inc(g3.h, 1)
        g3.n += NCK

    @blk.gpsimd
    def _(g):
        for tc in range(NCK):
            g.wait_ge(g3.h, bg3 + tc + 1)
            g.dma_start(out=outv[s * S + tc * 128:s * S + (tc + 1) * 128, :],
                        in_=ctx.OUTST[:, tc % 2]).then_inc(q3.h, 16)
        q3.n += 16 * NCK


def kernel(L0_attn_x, L0_attn_Wq, L0_attn_Wk, L0_attn_Wv, L0_attn_Wg,
           L0_attn_Wa, L0_attn_ba, L0_attn_Wo, L0_attn_gn_w, L0_attn_gn_b,
           L0_ffn_Wg, L0_ffn_W1, L0_ffn_W2):
    x = np.ascontiguousarray(np.asarray(L0_attn_x, dtype=np.float32)).reshape(T, DM)
    base = {
        "x_all": x,
        "wq": np.ascontiguousarray(L0_attn_Wq, dtype=np.float32),
        "wk": np.ascontiguousarray(L0_attn_Wk, dtype=np.float32),
        "wv": np.ascontiguousarray(L0_attn_Wv, dtype=np.float32),
        "wga": np.ascontiguousarray(L0_attn_Wg, dtype=np.float32),
        "wa": np.ascontiguousarray(L0_attn_Wa, dtype=np.float32),
        "ba": np.ascontiguousarray(L0_attn_ba, dtype=np.float32),
        "wo": np.ascontiguousarray(L0_attn_Wo, dtype=np.float32),
        "gnw": np.ascontiguousarray(L0_attn_gn_w, dtype=np.float32),
        "gnb": np.ascontiguousarray(L0_attn_gn_b, dtype=np.float32),
        "wgf": np.ascontiguousarray(L0_ffn_Wg, dtype=np.float32),
        "w1": np.ascontiguousarray(L0_ffn_W1, dtype=np.float32),
        "w2": np.ascontiguousarray(L0_ffn_W2, dtype=np.float32),
    }
    in_maps = []
    for c in range(NC_):
        m = dict(base)
        m["x_own"] = np.ascontiguousarray(x[c * TOWN:(c + 1) * TOWN])
        m["csel"] = np.array([c], dtype=np.int32)
        in_maps.append(m)
    nc, _ctx = build_nc()
    res = run_bass_kernel_spmd(nc, in_maps, core_ids=list(range(NC_)))
    out = np.concatenate([r["out"] for r in res.results], axis=0)
    return out.reshape(B, S, DM)


# version 04
# speedup vs baseline: 1.2281x; 1.2281x over previous
import contextlib
import numpy as np
import ml_dtypes
import concourse.bass as bass
import concourse.mybir as mybir
from concourse.bass_utils import run_bass_kernel_spmd

F32 = mybir.dt.float32
BF16 = mybir.dt.bfloat16
I32 = mybir.dt.int32
AF = mybir.ActivationFunctionType
ALU = mybir.AluOpType
AX = mybir.AxisListType
bf16 = ml_dtypes.bfloat16
ds = bass.ds

NC_ = 8
B, S, DM = 16, 2048, 1024
H, DH, DI, FF, E = 16, 64, 1024, 2048, 8
T = B * S
TOWN = T // NC_
NSEQ = 2
NCH = DM // 128
NDT = DI // 128
NBLK = T // 128
NOWN = TOWN // 128
NCK = S // 128
CAP = T // E
EPS = 1e-5
NITER = 64
MEGA = E + H + 1

DEBUG_OUTS = []
STAGES = {"0", "A", "B", "P", "S"}
SEQ_PHASES = "DEFG"
D_MMB = True
D_MMA = True
D_NCK = None
D_SUB = 99
P_PARTS = {1, 2, 3}
DEBUG_BIS = False
B_PARTS = 3
B_NT = None
B_PLAIN = False


class Sem:
    __slots__ = ("h", "n")

    def __init__(self, es, nc, name):
        self.h = es.enter_context(nc.semaphore(name))
        self.n = 0


class Ctx:
    pass


def _sb(ctx, name, shape, dt):
    t = ctx.es.enter_context(ctx.nc.sbuf_tensor(name, list(shape), dt))
    setattr(ctx, name, t)
    return t


def build_nc():
    nc = bass.Bass("TRN2", target_bir_lowering=False, detect_race_conditions=False)
    ctx = Ctx()
    ctx.nc = nc
    es = contextlib.ExitStack()
    ctx.es = es

    def din(name, shape, dt=F32):
        ap = nc.dram_tensor(name, list(shape), dt, kind="ExternalInput").ap()
        setattr(ctx, name, ap)

    din("x_own", [TOWN, DM])
    din("x_all", [T, DM])
    din("wq", [DM, DI]); din("wk", [DM, DI]); din("wv", [DM, DI])
    din("wga", [DM, DI]); din("wa", [DM, H]); din("ba", [H])
    din("wo", [DI, DM]); din("gnw", [DI]); din("gnb", [DI])
    din("wgf", [DM, E]); din("w1", [E, DM, FF]); din("w2", [E, FF, DM])
    din("csel", [1], I32)
    ctx.out = nc.dram_tensor("out", [TOWN, DM], F32, kind="ExternalOutput").ap()
    for name, shape, dt, _fn in DEBUG_OUTS:
        setattr(ctx, "dbg_" + name,
                nc.dram_tensor("dbg_" + name, list(shape), dt, kind="ExternalOutput").ap())

    def scr(name, shape, dt):
        ap = nc.dram_tensor(name, list(shape), dt, kind="Internal").ap()
        setattr(ctx, name, ap)

    scr("xhi_all", [T, DM], BF16)
    scr("xhi_own", [TOWN, DM], BF16)
    scr("w1_b", [E, DM, FF], BF16); scr("w2_b", [E, FF, DM], BF16)
    scr("msrow_d", [2, TOWN], F32)

    ctx.ps = es.enter_context(nc.psum_tensor("ps", [128, 4096], F32))

    _sb(ctx, "ones_f", [128, 128], F32)
    _sb(ctx, "ones_b", [128, 128], BF16)
    _sb(ctx, "ident_f", [128, 128], F32)
    _sb(ctx, "ident_b", [128, 128], BF16)
    _sb(ctx, "maskadd", [128, 128], F32)
    _sb(ctx, "exp2_f", [2, 128], F32)
    _sb(ctx, "exp2_b", [2, 128], BF16)
    _sb(ctx, "mega_hi", [128, NCH, 32], BF16)
    _sb(ctx, "mega_lo", [128, NCH, 32], BF16)
    _sb(ctx, "corr", [128, MEGA], F32)
    _sb(ctx, "ba_t", [128, H], F32)
    _sb(ctx, "gnw_t", [128, NDT], F32)
    _sb(ctx, "gnb_t", [128, NDT], F32)
    _sb(ctx, "csel_sb", [1, 1], I32)
    _sb(ctx, "cst", [128, 4], F32)
    _sb(ctx, "e16", [16, NDT, 128], BF16)
    _sb(ctx, "e2t", [128, 2], F32)
    _sb(ctx, "ecol", [128, 8], F32)
    _sb(ctx, "ecol2", [128, 8], F32)
    _sb(ctx, "tiny", [2, 4, 64], F32)

    _sb(ctx, "gates_own", [128, E, NOWN], F32)
    _sb(ctx, "awa_own", [128, H, NOWN], F32)
    _sb(ctx, "tau", [128, E], F32)
    _sb(ctx, "m_own", [128, NOWN], F32)
    _sb(ctx, "sinv_own", [128, NOWN], F32)
    _sb(ctx, "bis", [1, 32], F32)
    _sb(ctx, "bis2", [128, 4, E], F32)
    _sb(ctx, "cnt8", [128, E], F32)

    _sb(ctx, "big0", [128, 16384], BF16)
    _sb(ctx, "big1", [128, 16384], BF16)
    _sb(ctx, "big2", [128, 16384], BF16)
    _sb(ctx, "big3", [128, 16384], BF16)
    _sb(ctx, "big4", [128, 16384], BF16)
    _sb(ctx, "arena", [128, 18528], BF16)

    b0f = ctx.big0.ap().bitcast(F32)
    b1f = ctx.big1.ap().bitcast(F32)
    b2f = ctx.big2.ap().bitcast(F32)
    arf = ctx.arena.ap().bitcast(F32)
    arb = ctx.arena.ap()
    ctx.ALL = b0f[:, 0:MEGA * NBLK].rearrange("p (a b) -> p a b", a=MEGA)
    ctx.cmp_f = b1f[:, 0:E * NBLK].rearrange("p (a b) -> p a b", a=E)
    ctx.gates_all = b1f[:, E * NBLK:2 * E * NBLK].rearrange("p (a b) -> p a b", a=E)
    ctx.val_mx = b1f[:, 2 * E * NBLK:2 * E * NBLK + 4 * NBLK].rearrange(
        "p (a b) -> p a b", a=4)
    ctx.sumsq = b2f[:, 0:NBLK]
    ctx.m_all = b2f[:, NBLK:2 * NBLK]
    ctx.sinv_all = b2f[:, 2 * NBLK:3 * NBLK]
    ctx.ALLo = b2f[:, 3 * NBLK:3 * NBLK + MEGA * NOWN].rearrange(
        "p (a b) -> p a b", a=MEGA)
    ctx.sumsq_o = b2f[:, 3 * NBLK + MEGA * NOWN:3 * NBLK + MEGA * NOWN + NOWN]
    ctx.m_allo = b2f[:, 3 * NBLK + MEGA * NOWN + NOWN:3 * NBLK + MEGA * NOWN + 2 * NOWN]
    ctx.sinv_allo = b2f[:, 3 * NBLK + MEGA * NOWN + 2 * NOWN:3 * NBLK + MEGA * NOWN + 3 * NOWN]
    ctx.cmp_o = b1f[:, 2 * E * NBLK + 4 * NBLK:2 * E * NBLK + 4 * NBLK + E * NOWN].rearrange(
        "p (a b) -> p a b", a=E)
    ctx.val_mxo = b1f[:, 2 * E * NBLK + 4 * NBLK + E * NOWN:
                      2 * E * NBLK + 4 * NBLK + E * NOWN + 4 * NOWN].rearrange(
        "p (a b) -> p a b", a=4)
    ctx.xstage = arf[:, 0:2048].rearrange("p (a b) -> p a b", a=2)
    ctx.xhi_st = arb[:, 4096:6144].rearrange("p (a b) -> p a b", a=2)
    ctx.fmst = arb[:, 6144:8192].rearrange("p (a b) -> p a b", a=4)
    ctx.wstage = arf[:, 4096:5120]
    ctx.ost = arf[:, 5120:5632][0:MEGA, :]

    ctx.XFM = arb[:, 0:1024].rearrange("p (a b) -> p a b", a=2)
    ctx.LN_TMP = arf[:, 512:1024]
    ctx.MSLm = arf[0:1, 1024:2048]
    ctx.MSLs = arf[0:1, 2048:3072]
    ctx.WST2 = arf[:, 1024:2048]
    ctx.CUM = arf[0:16, 3072:5120]
    ctx.EBUF = arf[0:16, 5120:7168]
    ctx.EBE = arb[0:16, 28672 // 2:32768 // 2]
    ctx.CUMT = arf[:, 8192:8448].rearrange("p (a b) -> p a b", a=NCK)
    ctx.EDET = arf[:, 8448:8704].rearrange("p (a b) -> p a b", a=NCK)
    ctx.GAM2 = arf[:, 8704:8832]
    ctx.G2 = arf[0:2, 8832:8960]
    ctx.GAMST = arf[0:16, 6144:6160]
    ctx.SP_T = arf[:, 8960:9216].rearrange("p (a b) -> p a b", a=H)
    ctx.CB = arf[0:16, 9216:9232]
    ctx.CE = arf[0:16, 9232:9248]
    ctx.GAMS = arf[0:16, 9248:9264]
    ctx.GFLAT = arf[0:1, 6144:6400]
    ctx.EXPD = arb[:, 0:2048]
    ctx.A_SB = arb[:, 2048:4096]
    ctx.KDEC = arb[:, 4096:5120]
    ctx.S_B = arb[:, 5120:5632]
    ctx.S_F = arf[:, 5120:5632]
    ctx.PA = arb[:, 12288:13312].rearrange("p (a b) -> p a b", a=NDT)
    ctx.ECOLS = arf[:, 6912:6944]
    ctx.GT = arb[:, 13312:13824]
    ctx.G_FM = arb[:, 0:8192].rearrange("p (a b) -> p a b", a=16)
    ctx.WST2f = arf[:, 4096:5120]
    ctx.XG = arf[:, 0:2048].rearrange("p (a b) -> p a b", a=2)
    ctx.OUTST = arf[:, 2048:4096].rearrange("p (a b) -> p a b", a=2)
    ctx.T1A = arf[:, 5632:6144]
    ctx.T1B = arf[:, 6656:7168]

    ctx.sems = {}
    for s in ["dS", "dG", "dT", "dW", "dO", "cA", "cB", "cC", "cD", "cE", "cF", "BAR",
              "q1", "q2", "q3", "e1", "e2", "e3", "e4", "e5", "e6", "e7", "e8",
              "e9", "e10", "e11", "e12", "f1", "f2", "f3", "f4", "fw"]:
        setattr(ctx, s, Sem(es, nc, s))
    ctx.bar_n = 0

    with nc.Block() as blk:
        ctx.blk = blk
        emit_all(ctx)

    es.close()
    return nc, ctx


def barrier(ctx):
    blk = ctx.blk
    ctx.bar_n += 1
    target = 5 * ctx.bar_n
    bar = ctx.BAR.h
    for name in ["sync", "tensor", "vector", "scalar", "gpsimd"]:
        def f(eng):
            eng.drain().then_inc(bar, 1)
            eng.wait_ge(bar, target)
        getattr(blk, name)(f)


def emit_all(ctx):
    nc = ctx.nc
    if "0" in STAGES:
        with nc.named_scope("s0"):
            emit_stage0(ctx)
            barrier(ctx)
            emit_stage0b(ctx)
    if "A" in STAGES:
        with nc.named_scope("sA"):
            emit_stageA(ctx)
    if "B" in STAGES:
        with nc.named_scope("sB"):
            emit_stageB(ctx)
            barrier(ctx)
    if "P" in STAGES:
        with nc.named_scope("sP"):
            emit_post_routing(ctx)
            barrier(ctx)
    if "S" in STAGES:
        with nc.named_scope("sM"):
            emit_msrow(ctx)
            barrier(ctx)
        for s in range(NSEQ):
            emit_seq(ctx, s)
    for name, _s, _d, fn in DEBUG_OUTS:
        emit_debug_out(ctx, name, fn)
    finalize(ctx)


def emit_debug_out(ctx, name, fn):
    blk = ctx.blk
    ap = fn(ctx)

    @blk.sync
    def _(sy):
        sy.dma_start(out=getattr(ctx, "dbg_" + name), in_=ap).then_inc(ctx.dO.h, 16)
    ctx.dO.n += 16


def finalize(ctx):
    blk = ctx.blk

    @blk.sync
    def _(sy):
        if ctx.dO.n:
            sy.wait_ge(ctx.dO.h, ctx.dO.n)

    @blk.gpsimd
    def _(g):
        g.wait_ge(ctx.q3.h, ctx.q3.n)


def emit_stage0(ctx):
    blk = ctx.blk
    dW, dS = ctx.dW, ctx.dS

    @blk.gpsimd
    def _(g):
        g.memset(ctx.ones_f[:], 1.0)
        g.memset(ctx.ones_b[:], 1.0)
        g.memset(ctx.cst[:, 0:1], EPS)
        g.memset(ctx.cst[:, 1:2], 1.0)
        g.memset(ctx.cst[:, 2:3], 0.0)
        g.affine_select(ctx.ident_f[:], ctx.ones_f[:], compare_op=ALU.is_equal,
                        base=0, pattern=[[1, 128]], channel_multiplier=-1, fill=0.0)
        g.affine_select(ctx.ident_b[:], ctx.ones_b[:], compare_op=ALU.is_equal,
                        base=0, pattern=[[1, 128]], channel_multiplier=-1, fill=0.0)
        g.memset(ctx.maskadd[:], 0.0)
        g.affine_select(ctx.maskadd[:], ctx.maskadd[:], compare_op=ALU.is_ge,
                        base=0, pattern=[[1, 128]], channel_multiplier=-1, fill=-1e30)
        g.memset(ctx.exp2_f[:], 1.0)
        g.affine_select(ctx.exp2_f[:], ctx.exp2_f[:], compare_op=ALU.is_ge,
                        base=0, pattern=[[1, 128]], channel_multiplier=-64, fill=0.0)
        g.affine_select(ctx.exp2_f[:], ctx.exp2_f[:], compare_op=ALU.is_ge,
                        base=63, pattern=[[-1, 128]], channel_multiplier=64, fill=0.0)
        g.tensor_copy(ctx.exp2_b[:], ctx.exp2_f[:])
        g.memset(ctx.e16[:], 1.0)
        for dt in range(NDT):
            g.affine_select(ctx.e16[:, dt, :], ctx.e16[:, dt, :], compare_op=ALU.is_ge,
                            base=128 * dt, pattern=[[1, 128]], channel_multiplier=-64,
                            fill=0.0)
            g.affine_select(ctx.e16[:, dt, :], ctx.e16[:, dt, :], compare_op=ALU.is_ge,
                            base=63 - 128 * dt, pattern=[[-1, 128]], channel_multiplier=64,
                            fill=0.0)
        g.memset(ctx.e2t[:], 1.0)
        g.affine_select(ctx.e2t[:], ctx.e2t[:], compare_op=ALU.is_ge,
                        base=0, pattern=[[-64, 2]], channel_multiplier=1, fill=0.0)
        g.affine_select(ctx.e2t[:], ctx.e2t[:], compare_op=ALU.is_ge,
                        base=63, pattern=[[64, 2]], channel_multiplier=-1, fill=0.0)

    @blk.sync
    def _(sy):
        sy.dma_start(out=ctx.wstage[:, 0:NCH * E],
                     in_=ctx.wgf.rearrange("(c p) e -> p c e", p=128)).then_inc(dS.h, 16)
        sy.dma_start(out=ctx.wstage[:, NCH * E:NCH * E + NCH * H],
                     in_=ctx.wa.rearrange("(c p) h -> p c h", p=128)).then_inc(dS.h, 16)
        sy.dma_start(out=ctx.tiny[0:1, 0, 0:H], in_=ctx.ba.unsqueeze(0)).then_inc(dS.h, 16)
        sy.dma_start(out=ctx.gnw_t[:],
                     in_=ctx.gnw.rearrange("(c p) -> p c", p=128),
                     allow_slow_non_contiguous=True).then_inc(dS.h, 16)
        sy.dma_start(out=ctx.gnb_t[:],
                     in_=ctx.gnb.rearrange("(c p) -> p c", p=128),
                     allow_slow_non_contiguous=True).then_inc(dS.h, 16)
        sy.dma_start(out=ctx.csel_sb[:], in_=ctx.csel.unsqueeze(0)).then_inc(dS.h, 16)
        dS.n += 16 * 6
        sy.wait_ge(dS.h, dS.n)


def emit_stage0b(ctx):
    blk, ps = ctx.blk, ctx.ps
    cB, cC = ctx.cB, ctx.cC

    @blk.vector
    def _(v):
        v.memset(ctx.mega_hi[:], 0.0)
        v.memset(ctx.mega_lo[:], 0.0)
        for c in range(NCH):
            src = ctx.wstage[:, c * E:(c + 1) * E]
            wa_src = ctx.wstage[:, NCH * E + c * H:NCH * E + (c + 1) * H]
            v.tensor_copy(ctx.mega_hi[:, c, 0:E], src)
            v.tensor_copy(ctx.mega_hi[:, c, E:E + H], wa_src)
            v.tensor_copy(ctx.mega_hi[:, c, E + H:MEGA], ctx.ones_f[:, 0:1])
            v.drain()
            v.scalar_tensor_tensor(ctx.mega_lo[:, c, 0:E], ctx.mega_hi[:, c, 0:E],
                                   -1.0, src, ALU.mult, ALU.add)
            v.scalar_tensor_tensor(ctx.mega_lo[:, c, E:E + H], ctx.mega_hi[:, c, E:E + H],
                                   -1.0, wa_src, ALU.mult, ALU.add)
        v.drain().then_inc(cB.h, 1)
        cB.n += 1

    @blk.tensor
    def _(t):
        for c in range(NCH):
            t.matmul(ps[0:1, 0:E], ctx.ones_f[:, 0:1], ctx.wstage[:, c * E:(c + 1) * E],
                     start=(c == 0), stop=(c == NCH - 1))
        for c in range(NCH):
            t.matmul(ps[0:1, 32:32 + H], ctx.ones_f[:, 0:1],
                     ctx.wstage[:, NCH * E + c * H:NCH * E + (c + 1) * H],
                     start=(c == 0), stop=(c == NCH - 1))
        t.drain().then_inc(cC.h, 1)
        cC.n += 1

    @blk.vector
    def _(v):
        v.wait_ge(cC.h, cC.n)
        v.tensor_copy(ctx.tiny[0:1, 1, 0:E], ps[0:1, 0:E])
        v.tensor_copy(ctx.tiny[0:1, 1, E:E + H], ps[0:1, 32:32 + H])
        v.memset(ctx.tiny[0:1, 1, E + H:MEGA], float(DM))
        v.drain().then_inc(cC.h, 1)
        cC.n += 1

    @blk.tensor
    def _(t):
        t.wait_ge(cC.h, cC.n)
        t.matmul(ps[:, 0:32][:, 0:MEGA], ctx.ones_f[0:1, :], ctx.tiny[0:1, 1, 0:MEGA],
                 start=True, stop=True)
        t.matmul(ps[:, 32:32 + H], ctx.ones_f[0:1, :], ctx.tiny[0:1, 0, 0:H],
                 start=True, stop=True)
        t.drain().then_inc(cC.h, 1)
        cC.n += 1

    @blk.vector
    def _(v):
        v.wait_ge(cC.h, cC.n)
        v.tensor_copy(ctx.corr[:], ps[:, 0:MEGA])
        v.tensor_copy(ctx.ba_t[:], ps[:, 32:32 + H])
        v.drain().then_inc(cC.h, 1)
        cC.n += 1


def emit_stageA(ctx):
    blk = ctx.blk
    dS, dG, cD = ctx.dS, ctx.dG, ctx.cD
    xa = ctx.x_all.rearrange("(n p) d -> n p d", p=128)
    xo = ctx.x_own.rearrange("(n p) d -> n p d", p=128)
    xha = ctx.xhi_all.rearrange("(n p) d -> n p d", p=128)
    xho = ctx.xhi_own.rearrange("(n p) d -> n p d", p=128)
    ntile = NBLK + NOWN
    load_base = dS.n
    done_base = cD.n
    wr_base = dG.n
    ctx.dG_allbase = dG.n

    @blk.sync
    def _(sy):
        for i in range(ntile):
            b = i % 2
            if i >= 2:
                sy.wait_ge(cD.h, done_base + (i - 2) + 1)
            src = xa[i] if i < NBLK else xo[i - NBLK]
            sy.dma_start(out=ctx.xstage[:, b], in_=src).then_inc(dS.h, 16)
        dS.n += 16 * ntile

    @blk.vector
    def _(v):
        for i in range(ntile):
            b = i % 2
            v.wait_ge(dS.h, load_base + 16 * (i + 1))
            if i >= 2:
                v.wait_ge(dG.h, wr_base + 16 * (i - 1))
            v.tensor_copy(ctx.xhi_st[:, b], ctx.xstage[:, b])
            acc = (ctx.sumsq[:, i:i + 1] if i < NBLK
                   else ctx.sumsq_o[:, i - NBLK:i - NBLK + 1])
            v.scalar_tensor_tensor(ctx.xstage[:, b], ctx.xhi_st[:, b], 1.0,
                                   ctx.xhi_st[:, b], ALU.bypass, ALU.mult,
                                   accum_out=acc)
            v.drain().then_inc(cD.h, 1)
        cD.n += ntile

    @blk.gpsimd
    def _(g):
        for i in range(ntile):
            b = i % 2
            g.wait_ge(cD.h, done_base + i + 1)
            dst = xha[i] if i < NBLK else xho[i - NBLK]
            g.dma_start(out=dst, in_=ctx.xhi_st[:, b]).then_inc(dG.h, 16)
        dG.n += 16 * ntile


def emit_stageB(ctx):
    blk, ps = ctx.blk, ctx.ps
    dT, cE, cF, cC = ctx.dT, ctx.cE, ctx.cF, ctx.cC
    NT = T // 512 + TOWN // 512
    if B_NT is not None:
        NT = B_NT

    rd_base = dT.n
    mm_base = cE.n
    ev_base = cF.n
    tr_base = cC.n
    sc_base = ctx.cB.n
    mega_done = ctx.cB.n
    ps_free = ctx.cC.n

    @blk.sync
    def _(sy):
        for i in range(NT):
            ntile_done = (4 * i + 4) if i < T // 512 else (NBLK + 4 * (i - T // 512) + 4)
            sy.wait_ge(ctx.dG.h, ctx.dG_allbase + 16 * ntile_done)
            for c in range(NCH):
                k = i * NCH + c
                if k >= 4 and B_PARTS >= 2:
                    sy.wait_ge(cE.h, mm_base + (k - 4) + 1)
                if i < T // 512:
                    srcap = ctx.xhi_all[i * 512:(i + 1) * 512, c * 128:(c + 1) * 128]
                else:
                    io = i - T // 512
                    srcap = ctx.xhi_own[io * 512:(io + 1) * 512, c * 128:(c + 1) * 128]
                if B_PLAIN:
                    sy.dma_start(out=ctx.fmst[:, k % 4],
                                 in_=ctx.xhi_all.rearrange("(a p) d -> a p d", p=128)[k % 200, :, 0:512]
                                 ).then_inc(dT.h, 16)
                else:
                    sy.dma_start_transpose(ctx.fmst[:, k % 4], srcap).then_inc(dT.h, 16)
        dT.n += 16 * NT * NCH

    @blk.tensor
    def _(t):
        if B_PARTS < 2:
            return
        t.wait_ge(ctx.cB.h, mega_done)
        t.wait_ge(cC.h, ps_free)
        for i in range(NT + 1):
            if i < NT:
                if i >= 1:
                    t.wait_ge(cF.h, ev_base + i)
                for c in range(NCH):
                    k = i * NCH + c
                    t.wait_ge(dT.h, rd_base + 16 * (k + 1))
                    t.matmul(ps[0:MEGA, 0:512], ctx.mega_hi[:, c, 0:MEGA], ctx.fmst[:, k % 4],
                             start=(c == 0), stop=False)
                    t.matmul(ps[0:MEGA, 0:512], ctx.mega_lo[:, c, 0:MEGA], ctx.fmst[:, k % 4],
                             start=False, stop=(c == NCH - 1)).then_inc(cE.h, 1)
            if i >= 1 and B_PARTS >= 3:
                j0 = i - 1
                t.wait_ge(cF.h, ev_base + j0 + 1)
                if j0 >= 1:
                    t.wait_ge(ctx.cB.h, sc_base + j0)
                for j in range(4):
                    t.transpose(ps[:, 2048 + j * 32:2048 + j * 32 + MEGA],
                                ctx.ost[:, j * 128:(j + 1) * 128],
                                ctx.ident_f[0:MEGA, 0:MEGA])
                t.drain().then_inc(cC.h, 1)
        cE.n += NT * NCH
        if B_PARTS >= 3:
            cC.n += NT

    @blk.vector
    def _(v):
        if B_PARTS < 2:
            return
        for i in range(NT):
            v.wait_ge(cE.h, mm_base + NCH * (i + 1))
            if i >= 1 and B_PARTS >= 3:
                v.wait_ge(cC.h, tr_base + i)
            v.tensor_copy(ctx.ost[:], ps[0:MEGA, 0:512])
            v.drain().then_inc(cF.h, 1)
            if i >= 1 and B_PARTS >= 3:
                j0 = i - 1
                v.wait_ge(cC.h, tr_base + j0 + 1)
                for j in range(4):
                    if j0 < T // 512:
                        dst = ctx.ALL[:, :, 4 * j0 + j]
                    else:
                        dst = ctx.ALLo[:, :, 4 * (j0 - T // 512) + j]
                    v.tensor_copy(dst, ps[:, 2048 + j * 32:2048 + j * 32 + MEGA])
                v.drain().then_inc(ctx.cB.h, 1)
        if B_PARTS >= 3:
            v.wait_ge(cC.h, tr_base + NT)
            for j in range(4):
                dst = ctx.ALLo[:, :, 4 * (NT - 1 - T // 512) + j]
                v.tensor_copy(dst, ps[:, 2048 + j * 32:2048 + j * 32 + MEGA])
            v.drain().then_inc(ctx.cB.h, 1)
            ctx.cB.n += NT
        cF.n += NT


def emit_post_routing(ctx):
    blk, ps = ctx.blk, ctx.ps
    cC, cD = ctx.cC, ctx.cD
    Bp = ctx.ALL[:][:, 0:E, :]
    sums = ctx.ALL[:][:, E + H, :]
    Bpo = ctx.ALLo[:][:, 0:E, :]
    awao = ctx.ALLo[:][:, E:E + H, :]
    sumso = ctx.ALLo[:][:, E + H, :]
    if 1 not in P_PARTS:
        return

    @blk.vector
    def _(v):
        v.tensor_scalar(ctx.m_all[:], sums, 1.0 / DM, None, ALU.mult)
        v.tensor_tensor(ctx.sinv_all[:], ctx.m_all[:], ctx.m_all[:], ALU.mult)
        v.scalar_tensor_tensor(ctx.sinv_all[:], ctx.sumsq[:], 1.0 / DM, ctx.sinv_all[:],
                               ALU.mult, ALU.subtract)
        v.tensor_scalar(ctx.m_allo[:], sumso, 1.0 / DM, None, ALU.mult)
        v.drain()
        v.tensor_tensor(ctx.sinv_allo[:], ctx.m_allo[:], ctx.m_allo[:], ALU.mult)
        v.drain()
        v.scalar_tensor_tensor(ctx.sinv_allo[:], ctx.sumsq_o[:], 1.0 / DM,
                               ctx.sinv_allo[:], ALU.mult, ALU.subtract)
        v.drain().then_inc(cC.h, 1)
        cC.n += 1

    @blk.scalar
    def _(s):
        s.wait_ge(cC.h, cC.n)
        s.activation(out=ctx.sinv_all[:], in_=ctx.sinv_all[:], func=AF.Ln, bias=ctx.cst[:, 0:1])
        s.activation(out=ctx.sinv_all[:], in_=ctx.sinv_all[:], func=AF.Exp, scale=-0.5)
        s.activation(out=ctx.sinv_allo[:], in_=ctx.sinv_allo[:], func=AF.Ln, bias=ctx.cst[:, 0:1])
        s.drain()
        s.activation(out=ctx.sinv_allo[:], in_=ctx.sinv_allo[:], func=AF.Exp, scale=-0.5)
        s.drain()
        s.drain().then_inc(cC.h, 1)
        cC.n += 1

    @blk.vector
    def _(v):
        v.wait_ge(cC.h, cC.n)
        m_v = ctx.m_all[:].unsqueeze(1).to_broadcast([128, E, NBLK])
        s_v = ctx.corr[:, 0:E].unsqueeze(2).to_broadcast([128, E, NBLK])
        sinv_v = ctx.sinv_all[:].unsqueeze(1).to_broadcast([128, E, NBLK])
        v.tensor_tensor(ctx.cmp_f[:], m_v, s_v, ALU.mult)
        v.tensor_tensor(Bp, Bp, ctx.cmp_f[:], ALU.subtract)
        v.tensor_tensor(Bp, Bp, sinv_v, ALU.mult)
        mx = ctx.val_mx[:]
        v.tensor_tensor(mx[:, 0:4, :], Bp[:, 0:4, :], Bp[:, 4:8, :], ALU.max)
        v.tensor_tensor(mx[:, 0:2, :], mx[:, 0:2, :], mx[:, 2:4, :], ALU.max)
        v.tensor_tensor(mx[:, 0:1, :], mx[:, 0:1, :], mx[:, 1:2, :], ALU.max)
        v.tensor_tensor(Bp, Bp, mx[:, 0:1, :].to_broadcast([128, E, NBLK]), ALU.subtract)
        m_vo = ctx.m_allo[:].unsqueeze(1).to_broadcast([128, E, NOWN])
        s_vo = ctx.corr[:, 0:E].unsqueeze(2).to_broadcast([128, E, NOWN])
        sinv_vo = ctx.sinv_allo[:].unsqueeze(1).to_broadcast([128, E, NOWN])
        v.tensor_tensor(ctx.cmp_o[:], m_vo, s_vo, ALU.mult)
        v.tensor_tensor(Bpo, Bpo, ctx.cmp_o[:], ALU.subtract)
        v.tensor_tensor(Bpo, Bpo, sinv_vo, ALU.mult)
        mo = ctx.val_mxo[:]
        v.tensor_tensor(mo[:, 0:4, :], Bpo[:, 0:4, :], Bpo[:, 4:8, :], ALU.max)
        v.tensor_tensor(mo[:, 0:2, :], mo[:, 0:2, :], mo[:, 2:4, :], ALU.max)
        v.drain()
        v.tensor_tensor(mo[:, 0:1, :], mo[:, 0:1, :], mo[:, 1:2, :], ALU.max)
        v.drain()
        v.tensor_tensor(Bpo, Bpo, mo[:, 0:1, :].to_broadcast([128, E, NOWN]), ALU.subtract)
        sw_vo = ctx.corr[:, E:E + H].unsqueeze(2).to_broadcast([128, H, NOWN])
        m_vo16 = ctx.m_allo[:].unsqueeze(1).to_broadcast([128, H, NOWN])
        sinv_vo16 = ctx.sinv_allo[:].unsqueeze(1).to_broadcast([128, H, NOWN])
        v.tensor_tensor(ctx.awa_own[:], m_vo16, sw_vo, ALU.mult)
        v.drain()
        v.tensor_tensor(ctx.awa_own[:], awao, ctx.awa_own[:], ALU.subtract)
        v.drain()
        v.tensor_tensor(ctx.awa_own[:], ctx.awa_own[:], sinv_vo16, ALU.mult)
        v.drain()
        v.tensor_copy(ctx.m_own[:], ctx.m_allo[:])
        v.tensor_copy(ctx.sinv_own[:], ctx.sinv_allo[:])
        v.drain().then_inc(cC.h, 1)
        cC.n += 1

    @blk.scalar
    def _(s):
        s.wait_ge(cC.h, cC.n)
        s.activation(out=Bp, in_=Bp, func=AF.Exp)
        s.activation(out=Bpo, in_=Bpo, func=AF.Exp)
        s.drain().then_inc(cC.h, 1)
        cC.n += 1

    @blk.vector
    def _(v):
        v.wait_ge(cC.h, cC.n)
        mx = ctx.val_mx[:]
        v.tensor_tensor(mx[:, 0:4, :], Bp[:, 0:4, :], Bp[:, 4:8, :], ALU.add)
        v.tensor_tensor(mx[:, 0:2, :], mx[:, 0:2, :], mx[:, 2:4, :], ALU.add)
        v.tensor_tensor(mx[:, 0:1, :], mx[:, 0:1, :], mx[:, 1:2, :], ALU.add)
        v.reciprocal(mx[:, 1:2, :], mx[:, 0:1, :])
        v.tensor_tensor(Bp, Bp, mx[:, 1:2, :].to_broadcast([128, E, NBLK]), ALU.mult)
        mo = ctx.val_mxo[:]
        v.tensor_tensor(mo[:, 0:4, :], Bpo[:, 0:4, :], Bpo[:, 4:8, :], ALU.add)
        v.tensor_tensor(mo[:, 0:2, :], mo[:, 0:2, :], mo[:, 2:4, :], ALU.add)
        v.drain()
        v.tensor_tensor(mo[:, 0:1, :], mo[:, 0:1, :], mo[:, 1:2, :], ALU.add)
        v.drain()
        v.reciprocal(mo[:, 1:2, :], mo[:, 0:1, :])
        v.drain()
        v.tensor_tensor(Bpo, Bpo, mo[:, 1:2, :].to_broadcast([128, E, NOWN]), ALU.mult)
        v.memset(ctx.bis2[:, 0, :], 0.0)
        v.memset(ctx.bis2[:, 1, :], 1.0)
        v.memset(ctx.tau[:], 0.5)
        v.drain()
        v.drain().then_inc(cC.h, 1)
        cC.n += 1

    if 2 not in P_PARTS:
        return
    for it in range(NITER):
        @blk.vector
        def _(v):
            v.wait_ge(cC.h, cC.n)
            tau_v = ctx.tau[:].unsqueeze(2).to_broadcast([128, E, NBLK])
            v.tensor_tensor(ctx.cmp_f[:], Bp, tau_v, ALU.is_ge)
            v.tensor_reduce(ctx.cnt8[:], ctx.cmp_f[:], AX.X, ALU.add).then_inc(cD.h, 1)
            cD.n += 1

        @blk.tensor
        def _(t):
            t.wait_ge(cD.h, cD.n)
            t.matmul(ps[0:128, 3584:3584 + E], ctx.ones_f[:, 0:128], ctx.cnt8[:],
                     start=True, stop=True).then_inc(cD.h, 1)
            cD.n += 1

        @blk.vector
        def _(v):
            v.wait_ge(cD.h, cD.n)
            lo = ctx.bis2[:, 0, :]
            hi = ctx.bis2[:, 1, :]
            d1 = ctx.bis2[:, 2, :]
            g = ctx.bis2[:, 3, :]
            cntR = ctx.cnt8[:]
            mid = ctx.tau[:]
            v.tensor_copy(cntR, ps[0:128, 3584:3584 + E])
            v.drain()
            v.tensor_scalar(g, cntR, float(CAP), None, ALU.is_ge)
            v.tensor_tensor(d1, mid, lo, ALU.subtract)
            v.drain()
            v.tensor_tensor(d1, d1, g, ALU.mult)
            v.drain()
            v.tensor_tensor(lo, lo, d1, ALU.add)
            v.tensor_tensor(d1, hi, mid, ALU.subtract)
            v.drain()
            v.tensor_tensor(d1, d1, g, ALU.mult)
            v.drain()
            v.tensor_tensor(hi, mid, d1, ALU.add)
            v.drain()
            v.tensor_tensor(d1, lo, hi, ALU.add)
            v.drain()
            v.tensor_scalar(ctx.tau[:], d1, 0.5, None, ALU.mult).then_inc(cC.h, 1)
            cC.n += 1
    @blk.vector
    def _(v):
        v.wait_ge(cC.h, cC.n)
        v.drain()
        v.tensor_copy(ctx.tau[:], ctx.bis2[:, 0, :])
        v.drain().then_inc(cC.h, 1)
        cC.n += 1

    if 3 not in P_PARTS:
        return

    @blk.vector
    def _(v):
        v.wait_ge(cC.h, cC.n)
        tau_v = ctx.tau[:].unsqueeze(2).to_broadcast([128, E, NOWN])
        sel = ctx.cmp_o[:]
        mx = ctx.val_mxo[:]
        v.tensor_tensor(sel, Bpo, tau_v, ALU.is_ge)
        v.scalar_tensor_tensor(sel, sel, 2.0, Bpo, ALU.mult, ALU.add)
        v.tensor_tensor(mx[:, 0:4, :], sel[:, 0:4, :], sel[:, 4:8, :], ALU.max)
        v.tensor_tensor(mx[:, 0:2, :], mx[:, 0:2, :], mx[:, 2:4, :], ALU.max)
        v.tensor_tensor(mx[:, 0:1, :], mx[:, 0:1, :], mx[:, 1:2, :], ALU.max)
        seen = mx[:, 1:2, :]
        hit = mx[:, 2:3, :]
        v.memset(seen, 0.0)
        v.drain()
        for e in range(E):
            ga = ctx.gates_own[:, e:e + 1, :]
            v.tensor_tensor(hit, sel[:, e:e + 1, :], mx[:, 0:1, :], ALU.is_ge)
            v.drain()
            v.tensor_tensor(ga, hit, seen, ALU.subtract)
            v.drain()
            v.tensor_scalar_max(ga, ga, 0.0)
            v.tensor_tensor(seen, seen, hit, ALU.max)
            v.drain()
            v.tensor_tensor(ga, ga, Bpo[:, e:e + 1, :], ALU.mult)
            v.drain()
        v.drain().then_inc(cC.h, 1)
        cC.n += 1



def emit_seq(ctx, s):
    nc = ctx.nc
    with nc.named_scope(f"C{s}"):
        emit_C(ctx, s)
        barrier(ctx)
    if "D" not in SEQ_PHASES:
        return
    with nc.named_scope(f"D{s}"):
        emit_D(ctx, s)
        barrier(ctx)
    if "E" not in SEQ_PHASES:
        return
    with nc.named_scope(f"E{s}"):
        emit_E(ctx, s)
        barrier(ctx)
    if "F" not in SEQ_PHASES:
        return
    with nc.named_scope(f"F{s}"):
        emit_F(ctx, s)
        barrier(ctx)
    with nc.named_scope(f"G{s}"):
        emit_G(ctx, s)
        barrier(ctx)


def emit_msrow(ctx):
    blk, ps = ctx.blk, ctx.ps

    @blk.tensor
    def _(t):
        t.transpose(ps[0:32, 0:128], ctx.m_own[:], ctx.ident_f[:])
        t.transpose(ps[0:32, 128:256], ctx.sinv_own[:], ctx.ident_f[:]).then_inc(ctx.e1.h, 1)
        ctx.e1.n += 1

    @blk.vector
    def _(v):
        v.wait_ge(ctx.e1.h, ctx.e1.n)
        v.tensor_copy(ctx.wstage[0:32, 0:128], ps[0:32, 0:128])
        v.tensor_copy(ctx.wstage[0:32, 128:256], ps[0:32, 128:256]).then_inc(ctx.e2.h, 1)
        ctx.e2.n += 1

    @blk.sync
    def _(sy):
        sy.wait_ge(ctx.e2.h, ctx.e2.n)
        sy.dma_start(out=ctx.msrow_d[0:1, :], in_=ctx.wstage[0:32, 0:128]).then_inc(ctx.q1.h, 16)
        sy.dma_start(out=ctx.msrow_d[1:2, :], in_=ctx.wstage[0:32, 128:256]).then_inc(ctx.q1.h, 16)
        ctx.q1.n += 32
        sy.wait_ge(ctx.q1.h, ctx.q1.n)


def emit_C(ctx, s):
    blk, ps = ctx.blk, ctx.ps
    nc = ctx.nc
    q1, q2, e1, e2, e3, e4 = ctx.q1, ctx.q2, ctx.e1, ctx.e2, ctx.e3, ctx.e4

    @blk.vector
    def _(v):
        v.tensor_tensor(ctx.SP_T[:], ctx.awa_own[:, :, s * NCK:(s + 1) * NCK],
                        ctx.ba_t[:].unsqueeze(2).to_broadcast([128, H, NCK]),
                        ALU.add).then_inc(e1.h, 1)
        e1.n += 1

    @blk.scalar
    def _(sc):
        sc.wait_ge(e1.h, e1.n)
        sc.activation(out=ctx.SP_T[:], in_=ctx.SP_T[:], func=AF.Exp)
        sc.drain()
        sc.activation(out=ctx.SP_T[:], in_=ctx.SP_T[:], func=AF.Ln,
                      bias=ctx.cst[:, 1:2])
        sc.drain().then_inc(e1.h, 1)
        e1.n += 1

    @blk.tensor
    def _(t):
        t.wait_ge(e1.h, e1.n)
        for tt in range(NCK):
            t.transpose(ps[0:16, tt * 128:tt * 128 + 128],
                        ctx.SP_T[:, :, tt], ctx.ident_f[:])
        t.drain().then_inc(e2.h, 1)
        e2.n += 1

    @blk.vector
    def _(v):
        v.wait_ge(e2.h, e2.n)
        v.tensor_copy(ctx.EBUF[:], ps[0:16, 0:2048])
        v.drain()
        v.tensor_tensor_scan(ctx.CUM[:], ctx.EBUF[:],
                             ctx.cst[0:16, 2:3].to_broadcast([16, 2048]),
                             0.0, ALU.add, ALU.add)
        v.drain()
        v.memset(ctx.CB[:], 0.0)
        v.drain()
        v.tensor_copy(ctx.CB[:, 1:16], ctx.CUM[:, 127:2047:128])
        v.tensor_copy(ctx.CE[:], ctx.CUM[:, 127:2048:128])
        v.drain()
        v.tensor_tensor(ctx.EBUF[:],
                        ctx.CB[:].unsqueeze(2).to_broadcast([16, NCK, 128]),
                        ctx.CUM[:].rearrange("p (c i) -> p c i", c=NCK),
                        ALU.subtract)
        v.tensor_tensor(ctx.GAMS[:], ctx.CB[:], ctx.CE[:], ALU.subtract)
        v.drain().then_inc(e2.h, 1)
        e2.n += 1

    @blk.scalar
    def _(sc):
        sc.wait_ge(e2.h, e2.n)
        sc.activation(out=ctx.EBE[:], in_=ctx.EBUF[:], func=AF.Exp)
        sc.drain()
        sc.activation(out=ctx.GAMS[:], in_=ctx.GAMS[:], func=AF.Exp)
        sc.drain().then_inc(e3.h, 1)
        e3.n += 1

    @blk.vector
    def _(v):
        v.wait_ge(e3.h, e3.n)
        v.tensor_tensor(ctx.EBUF[:],
                        ctx.CUM[:].rearrange("p (c i) -> p c i", c=NCK),
                        ctx.CE[:].unsqueeze(2).to_broadcast([16, NCK, 128]),
                        ALU.subtract).then_inc(e3.h, 1)
        e3.n += 1

    @blk.scalar
    def _(sc):
        sc.wait_ge(e3.h, e3.n)
        sc.activation(out=ctx.EBUF[:], in_=ctx.EBUF[:], func=AF.Exp)
        sc.drain().then_inc(e3.h, 1)
        e3.n += 1

    @blk.tensor
    def _(t):
        t.wait_ge(e3.h, e3.n)
        t.transpose(ps[0:16, 512:528], ctx.GAMS[:], ctx.ident_f[0:16, 0:16]
                    ).then_inc(e3.h, 1)
        e3.n += 1

    @blk.vector
    def _(v):
        v.wait_ge(e3.h, e3.n)
        v.tensor_copy(ctx.GAMST[:], ps[0:16, 512:528]).then_inc(e3.h, 1)
        e3.n += 1

    @blk.sync
    def _(sy):
        sy.wait_ge(e3.h, e3.n)
        for j in range(2):
            sy.dma_start(out=ctx.G2[j:j + 1, :],
                         in_=ctx.GAMST[:, j:16:2],
                         allow_slow_non_contiguous=True).then_inc(q1.h, 16)
        q1.n += 32

    @blk.tensor
    def _(t):
        t.wait_ge(e3.h, e3.n)
        for c in range(NCK):
            t.transpose(ps[0:128, 2048 + c * 32:2048 + c * 32 + 16],
                        ctx.CUM[:, c * 128:(c + 1) * 128], ctx.ident_f[0:16, 0:16])
            t.transpose(ps[0:128, 3072 + c * 32:3072 + c * 32 + 16],
                        ctx.EBUF[:, c * 128:(c + 1) * 128], ctx.ident_f[0:16, 0:16])
        t.drain().then_inc(e4.h, 1)
        e4.n += 1
        t.wait_ge(q1.h, q1.n)
        t.matmul(ps[0:128, 1536:1664], ctx.exp2_f[:], ctx.G2[:],
                 start=True, stop=True).then_inc(e4.h, 1)
        e4.n += 1

    @blk.vector
    def _(v):
        v.wait_ge(e4.h, e4.n)
        for c in range(NCK):
            v.tensor_copy(ctx.CUMT[:, c, :], ps[0:128, 2048 + c * 32:2048 + c * 32 + 16])
            v.tensor_copy(ctx.EDET[:, c, :], ps[0:128, 3072 + c * 32:3072 + c * 32 + 16])
        v.tensor_copy(ctx.GAM2[:], ps[0:128, 1536:1664])
        v.drain().then_inc(e4.h, 1)
        e4.n += 1

    xho = ctx.xhi_own
    base_q2 = q2.n
    base_e5 = ctx.e5.n
    base_e6 = ctx.e6.n

    @blk.scalar
    def _(sc):
        for t4 in range(4):
            if t4 >= 1:
                sc.wait_ge(ctx.e5.h, base_e5 + t4)
            tok0 = s * S + t4 * 512
            sc.dma_start(out=ctx.MSLm[:, (t4 % 2) * 512:(t4 % 2) * 512 + 512],
                         in_=ctx.msrow_d[0:1, s * S + t4 * 512: s * S + (t4 + 1) * 512]
                         ).then_inc(q2.h, 16)
            sc.dma_start(out=ctx.MSLs[:, (t4 % 2) * 512:(t4 % 2) * 512 + 512],
                         in_=ctx.msrow_d[1:2, s * S + t4 * 512: s * S + (t4 + 1) * 512]
                         ).then_inc(q2.h, 16)
            for ch in range(NCH):
                k = t4 * NCH + ch
                if k >= 2:
                    sc.wait_ge(ctx.e6.h, base_e6 + (k - 2) + 1)
                sc.dma_start_transpose(
                    ctx.XFM[:, k % 2],
                    xho[s * S + t4 * 512:s * S + (t4 + 1) * 512,
                        ch * 128:(ch + 1) * 128]).then_inc(q2.h, 16)
        q2.n += 16 * 4 * (NCH + 2)

    @blk.tensor
    def _(t, _e4n=ctx.e4.n):
        t.wait_ge(ctx.e4.h, _e4n)
        for t4 in range(4):
            t.wait_ge(q2.h, base_q2 + 16 * (t4 * (NCH + 2) + 2))
            if t4 >= 1:
                t.wait_ge(ctx.e6.h, base_e6 + t4 * NCH)
            t.matmul(ps[:, 3072:3584], ctx.ones_f[0:1, :],
                     ctx.MSLm[:, (t4 % 2) * 512:(t4 % 2) * 512 + 512],
                     start=True, stop=True)
            t.matmul(ps[:, 3584:4096], ctx.ones_f[0:1, :],
                     ctx.MSLs[:, (t4 % 2) * 512:(t4 % 2) * 512 + 512],
                     start=True, stop=True).then_inc(ctx.e5.h, 1)
        ctx.e5.n += 4

    @blk.vector
    def _(v):
        for t4 in range(4):
            for ch in range(NCH):
                k = t4 * NCH + ch
                v.wait_ge(q2.h, base_q2 + 16 * (t4 * (NCH + 2) + 2 + ch + 1))
                v.wait_ge(ctx.e5.h, base_e5 + t4 + 1)
                v.tensor_tensor(ctx.LN_TMP[:], ctx.XFM[:, k % 2],
                                ps[:, 3072:3584], ALU.subtract)
                v.tensor_tensor(ctx.big0[:, ch * 2048 + t4 * 512:
                                         ch * 2048 + (t4 + 1) * 512],
                                ctx.LN_TMP[:], ps[:, 3584:4096],
                                ALU.mult).then_inc(ctx.e6.h, 1)
        ctx.e6.n += 4 * NCH

    emit_proj(ctx, s)


def emit_proj(ctx, s):
    blk, ps = ctx.blk, ctx.ps
    q1, e7, e8 = ctx.q1, ctx.e7, ctx.e8
    wbuf = ctx.big4.ap()[:, 0:8192].rearrange("p (k o c) -> p k o c", k=NCH, o=NDT)
    xnv = ctx.big0.ap().rearrange("p (c t) -> p c t", c=NCH)

    for wi, (wsrc, dstbig, act) in enumerate([
            (ctx.wq, ctx.big1, "silu"), (ctx.wk, ctx.big2, "silu"),
            (ctx.wv, ctx.big3, "copy")]):
        base_q1 = q1.n
        base_e7 = e7.n
        base_e8 = e8.n

        @blk.sync
        def _(sy, wsrc=wsrc, base_e8=base_e8, base_e7=base_e7, wi=wi):
            for k in range(NCH):
                if k >= 1:
                    sy.wait_ge(e8.h, base_e8 + k)
                elif wi == 0:
                    sy.wait_ge(ctx.e6.h, ctx.e6.n)
                else:
                    sy.wait_ge(e8.h, base_e8)
                    sy.wait_ge(e7.h, base_e7)
                sy.dma_start(out=ctx.WST2[:],
                             in_=wsrc[k * 128:(k + 1) * 128, :]).then_inc(q1.h, 16)
            q1.n += 16 * NCH

        @blk.vector
        def _(v, base_q1=base_q1, wi=wi):
            for k in range(NCH):
                v.wait_ge(q1.h, base_q1 + 16 * (k + 1))
                if wi > 0 or k > 0:
                    pass
                v.tensor_copy(wbuf[:, k, :, :], ctx.WST2[:]).then_inc(e8.h, 1)
            e8.n += NCH

        if act == "silu":
            @blk.tensor
            def _(t, base_e7=base_e7, dstbig=dstbig):
                t.wait_ge(e8.h, e8.n)
                t.wait_ge(e7.h, base_e7)
                for oc in range(NDT):
                    if oc >= 1:
                        t.wait_ge(e7.h, base_e7 + 2 * oc)
                    for k in range(NCH):
                        for t4 in range(4):
                            t.matmul(ps[:, (t4 % 4) * 512:(t4 % 4) * 512 + 512],
                                     wbuf[:, k, oc, :], xnv[:, k, t4 * 512:(t4 + 1) * 512],
                                     start=(k == 0), stop=(k == NCH - 1))
                    t.drain().then_inc(e7.h, 1)

            @blk.scalar
            def _(sc, base_e7=base_e7, dstbig=dstbig):
                for oc in range(NDT):
                    sc.wait_ge(e7.h, base_e7 + 2 * oc + 1)
                    for t4 in range(4):
                        sc.activation(out=dstbig[:, oc * 2048 + t4 * 512:
                                                 oc * 2048 + (t4 + 1) * 512],
                                      in_=ps[:, (t4 % 4) * 512:(t4 % 4) * 512 + 512],
                                      func=AF.Silu)
                    sc.drain().then_inc(e7.h, 1)
            e7.n += 2 * NDT
        else:
            @blk.tensor
            def _(t, base_e7=base_e7, _e4n=ctx.e4.n):
                t.wait_ge(e8.h, e8.n)
                t.wait_ge(e7.h, base_e7)
                t.wait_ge(ctx.e4.h, _e4n)
                wvk = ctx.big4.ap()[:, 0:8192].rearrange("p (k d) -> p k d", k=NCH)
                for tc in range(NCK):
                    if tc >= 1:
                        t.wait_ge(e7.h, base_e7 + 2 * tc)
                    for k in range(NCH):
                        t.matmul(ps[:, 2048:2560], xnv[:, k, tc * 128:(tc + 1) * 128],
                                 wvk[:, k, 0:512], start=(k == 0), stop=(k == NCH - 1))
                        t.matmul(ps[:, 2560:3072], xnv[:, k, tc * 128:(tc + 1) * 128],
                                 wvk[:, k, 512:1024], start=(k == 0), stop=(k == NCH - 1))
                    t.drain().then_inc(e7.h, 1)

            @blk.vector
            def _(v, base_e7=base_e7):
                for tc in range(NCK):
                    v.wait_ge(e7.h, base_e7 + 2 * tc + 1)
                    v.tensor_copy(ctx.big3[:, tc * 1024:(tc + 1) * 1024],
                                  ps[:, 2048:3072]).then_inc(e7.h, 1)
            e7.n += 2 * NCK


def emit_D(ctx, s):
    blk, ps = ctx.blk, ctx.ps
    psb = ctx.ps.ap().bitcast(BF16)
    q1 = ctx.q1
    sems = {k: getattr(ctx, k) for k in
            ["e1", "e2", "e3", "e4", "e5", "e6", "e7", "e8", "e9", "e10", "e11", "e12"]}
    B = {k: v.n for k, v in sems.items()}
    bq1 = q1.n
    Q = ctx.big1.ap().rearrange("p (d t) -> p d t", d=NDT)
    K = ctx.big2.ap().rearrange("p (d t) -> p d t", d=NDT)
    V = ctx.big3.ap().rearrange("p (c d) -> p c d", c=NCK)
    O = ctx.big4.ap().rearrange("p (d t) -> p d t", d=NDT)

    NCKD = NCK if D_NCK is None else D_NCK

    @blk.vector
    def _(v):
        v.memset(ctx.S_F[:], 0.0)
        v.memset(ctx.S_B[:], 0.0)
        v.drain().then_inc(sems["e12"].h, 1)
    sems["e12"].n += 1
    B["e12"] = sems["e12"].n - 1

    @blk.sync
    def _(sy):
        for c in range(NCKD):
            if c >= 1:
                sy.wait_ge(sems["e1"].h, B["e1"] + c)
            sy.dma_start(out=ctx.T1A[0:1, :],
                         in_=ctx.CUM[0:4, c * 128:(c + 1) * 128]).then_inc(q1.h, 16)
            sy.dma_start(out=ctx.T1A[32:33, :],
                         in_=ctx.CUM[4:8, c * 128:(c + 1) * 128]).then_inc(q1.h, 16)
            sy.dma_start(out=ctx.T1A[64:65, :],
                         in_=ctx.CUM[8:12, c * 128:(c + 1) * 128]).then_inc(q1.h, 16)
            sy.dma_start(out=ctx.T1B[0:1, :],
                         in_=ctx.CUM[12:16, c * 128:(c + 1) * 128]).then_inc(q1.h, 16)
        q1.n += 64 * NCKD

    @blk.tensor
    def _(t):
        e = sems
        for c in range(NCKD):
            if D_SUB >= 1:
                t.wait_ge(q1.h, bq1 + 64 * (c + 1))
                if c >= 1:
                    t.wait_ge(e["e7"].h, B["e7"] + c)
                    t.wait_ge(e["e2"].h, B["e2"] + c)
                pairs = [(ctx.ones_f[0:1, :], ctx.T1A[0:1, :]),
                         (ctx.ones_f[32:33, :], ctx.T1A[32:33, :]),
                         (ctx.ones_f[64:65, :], ctx.T1A[64:65, :]),
                         (ctx.ones_f[0:1, :], ctx.T1B[0:1, :])]
                tps = [(0, 0), (32, 0), (64, 0), (0, 0)]
                for q in range(4):
                    t.matmul(ps[:, 2048 + q * 512:2048 + (q + 1) * 512],
                             pairs[q][0], pairs[q][1], start=True, stop=True,
                             tile_position=tps[q])
                t.drain().then_inc(e["e1"].h, 1)
            if D_SUB >= 4:
                t.wait_ge(e["e2"].h, B["e2"] + c + 1)
                for par in range(2):
                    for h in range(par, H, 2):
                        pb = par * 64
                        dt = h // 2
                        t.matmul(ps[:, 2048 + h * 128:2048 + (h + 1) * 128],
                                 K[pb:pb + 64, dt, c * 128:(c + 1) * 128],
                                 Q[pb:pb + 64, dt, c * 128:(c + 1) * 128],
                                 start=True, stop=True,
                                 tile_position=(pb, 0))
                    t.drain()
                t.drain().then_inc(e["e4"].h, 1)
            if D_SUB >= 5:
                if c >= 1:
                    t.wait_ge(e["e9"].h, B["e9"] + c)
                for dt in range(NDT):
                    t.transpose(psb[:, 3072 + dt * 128:3072 + (dt + 1) * 128],
                                K[:, dt, c * 128:(c + 1) * 128], ctx.ident_b[:])
                t.drain().then_inc(e["e8"].h, 1)
            if D_SUB >= 7:
                t.wait_ge(e["e5"].h, B["e5"] + c + 1)
                for dt in range(NDT):
                    t.matmul(ps[:, 2048 + dt * 128:2048 + (dt + 1) * 128],
                             ctx.e16[:, dt, :], ctx.EBE[:, c * 128:(c + 1) * 128],
                             start=True, stop=True)
                t.drain().then_inc(e["e6"].h, 1)
            if D_SUB >= 10:
                t.wait_ge(e["e7"].h, B["e7"] + c + 1)
                t.wait_ge(e["e9"].h, B["e9"] + c + 1)
                t.wait_ge(e["e12"].h, B["e12"] + c + 1)
                if c >= 1:
                    t.wait_ge(e["e11"].h, B["e11"] + 2 * c)
                for h in range(H):
                    dt = h // 2
                    pb = (h % 2) * 64
                    if D_MMA:
                        t.matmul(ps[pb:pb + 64, dt * 128:(dt + 1) * 128],
                                 V[:, c, h * 64:(h + 1) * 64],
                                 ctx.A_SB[:, h * 128:(h + 1) * 128],
                                 start=True, stop=(not D_MMB), tile_position=(0, pb))
                    if D_MMB:
                        t.matmul(ps[pb:pb + 64, dt * 128:(dt + 1) * 128],
                                 ctx.S_B[pb:pb + 64, dt * 64:(dt + 1) * 64],
                                 ctx.PA[pb:pb + 64, dt, :],
                                 start=(not D_MMA), stop=True, tile_position=(pb, pb))
                for h in range(H):
                    dt = h // 2
                    pb = (h % 2) * 64
                    t.matmul(ps[pb:pb + 64, 1024 + dt * 64:1024 + (dt + 1) * 64],
                             ctx.KDEC.rearrange("p (h d) -> p h d", h=H)[:, h, :],
                             V[:, c, h * 64:(h + 1) * 64],
                             start=True, stop=True, tile_position=(0, pb))
                t.drain().then_inc(e["e10"].h, 1)
        for k, lvl in [("e1", 1), ("e4", 4), ("e8", 5), ("e6", 7), ("e10", 10)]:
            if D_SUB >= lvl:
                sems[k].n += NCKD

    @blk.scalar
    def _(sc):
        e = sems
        for c in range(NCKD):
            if D_SUB >= 3:
                sc.wait_ge(e["e2"].h, B["e2"] + c + 1)
                sc.activation(out=ctx.EXPD[:], in_=ctx.EXPD[:], func=AF.Exp)
                sc.drain().then_inc(e["e3"].h, 1)
        if D_SUB >= 3:
            sems["e3"].n += NCKD

    @blk.vector
    def _(v):
        e = sems
        for c in range(NCKD):
            if D_SUB >= 2:
                v.wait_ge(e["e1"].h, B["e1"] + c + 1)
                v.tensor_tensor(ctx.EXPD.rearrange("p (h i) -> p h i", h=H),
                                ctx.CUMT[:, c, :].unsqueeze(2).to_broadcast([128, H, 128]),
                                ps[:, 2048:4096].rearrange("p (h i) -> p h i", h=H),
                                ALU.subtract)
                v.tensor_tensor(ctx.EXPD.rearrange("p (h i) -> p h i", h=H),
                                ctx.EXPD.rearrange("p (h i) -> p h i", h=H),
                                ctx.maskadd[:].unsqueeze(1).to_broadcast([128, H, 128]),
                                ALU.add)
                v.drain().then_inc(e["e2"].h, 1)
            if D_SUB >= 6:
                v.wait_ge(e["e4"].h, B["e4"] + c + 1)
                v.wait_ge(e["e3"].h, B["e3"] + c + 1)
                if c >= 1:
                    v.wait_ge(e["e10"].h, B["e10"] + c)
                v.tensor_tensor(ctx.A_SB[:], ps[:, 2048:4096], ctx.EXPD[:],
                                ALU.mult).then_inc(e["e5"].h, 1)
            if D_SUB >= 8:
                v.wait_ge(e["e6"].h, B["e6"] + c + 1)
                for dt in range(NDT):
                    v.tensor_tensor(ctx.PA[:, dt, :], Q[:, dt, c * 128:(c + 1) * 128],
                                    ps[:, 2048 + dt * 128:2048 + (dt + 1) * 128],
                                    ALU.mult)
                v.drain().then_inc(e["e7"].h, 1)
            if D_SUB >= 9:
                v.wait_ge(e["e8"].h, B["e8"] + c + 1)
                v.tensor_tensor(ctx.KDEC.rearrange("p (h d) -> p h d", h=H),
                                psb[:, 3072:4096].rearrange("p (h d) -> p h d", h=H),
                                ctx.EDET[:, c, :].unsqueeze(2).to_broadcast([128, H, 64]),
                                ALU.mult).then_inc(e["e9"].h, 1)
            if D_SUB >= 11:
                v.wait_ge(e["e10"].h, B["e10"] + c + 1)
                v.tensor_copy(O[:, 0:4, c * 128:(c + 1) * 128], ps[:, 0:512])
                v.tensor_copy(O[:, 4:8, c * 128:(c + 1) * 128], ps[:, 512:1024])
                v.drain().then_inc(e["e11"].h, 1)
                gv = ctx.GAM2[:, c * 8:(c + 1) * 8].unsqueeze(2).to_broadcast([128, NDT, 64])
                v.tensor_tensor(ctx.S_F.rearrange("p (d v) -> p d v", d=NDT),
                                ctx.S_F.rearrange("p (d v) -> p d v", d=NDT), gv, ALU.mult)
                v.drain()
                v.tensor_tensor(ctx.S_F[:], ctx.S_F[:], ps[:, 1024:1536], ALU.add)
                v.drain().then_inc(e["e11"].h, 1)
                v.tensor_copy(ctx.S_B[:], ctx.S_F[:])
                v.drain().then_inc(e["e12"].h, 1)
        for k, amt, lvl in [("e2", NCKD, 2), ("e5", NCKD, 6), ("e7", NCKD, 8),
                            ("e9", NCKD, 9), ("e11", 2 * NCKD, 11), ("e12", NCKD, 11)]:
            if D_SUB >= lvl:
                sems[k].n += amt


def emit_E(ctx, s):
    blk, ps = ctx.blk, ctx.ps
    e1, e2, e3, q1 = ctx.e1, ctx.e2, ctx.e3, ctx.q1
    O = ctx.big4.ap().rearrange("p (d t) -> p d t", d=NDT)

    @blk.vector
    def _(v):
        for dt in range(NDT):
            od = ctx.big4.ap()[:, dt * 2048:(dt + 1) * 2048]
            v.tensor_reduce(ctx.ECOLS[:, dt:dt + 1], od, AX.X, ALU.add)
            v.tensor_tensor(ctx.EXPD[:], od, od, ALU.mult)
            v.tensor_reduce(ctx.ECOLS[:, 8 + dt:9 + dt], ctx.EXPD[:], AX.X, ALU.add)
        v.drain().then_inc(e1.h, 1)
        e1.n += 1

    @blk.tensor
    def _(t):
        t.wait_ge(e1.h, e1.n)
        t.matmul(ps[0:2, 0:16], ctx.e2t[:], ctx.ECOLS[:, 0:16],
                 start=True, stop=True).then_inc(e1.h, 1)
        e1.n += 1

    @blk.vector
    def _(v):
        v.wait_ge(e1.h, e1.n)
        st = ctx.GAMS
        NI = 1.0 / (S * DH)
        v.tensor_scalar(st[0:2, 0:16], ps[0:2, 0:16], NI, None, ALU.mult)
        v.drain()
        v.tensor_tensor(ctx.CB[0:2, 0:8], st[0:2, 0:8], st[0:2, 0:8], ALU.mult)
        v.drain()
        v.tensor_tensor(st[0:2, 8:16], st[0:2, 8:16], ctx.CB[0:2, 0:8], ALU.subtract)
        v.drain().then_inc(e2.h, 1)
        e2.n += 1

    @blk.scalar
    def _(sc):
        sc.wait_ge(e2.h, e2.n)
        sc.activation(out=ctx.GAMS[0:2, 8:16], in_=ctx.GAMS[0:2, 8:16], func=AF.Ln,
                      bias=ctx.cst[0:2, 0:1])
        sc.drain()
        sc.activation(out=ctx.GAMS[0:2, 8:16], in_=ctx.GAMS[0:2, 8:16], func=AF.Exp,
                      scale=-0.5)
        sc.drain().then_inc(e2.h, 1)
        e2.n += 1

    @blk.tensor
    def _(t):
        t.wait_ge(e2.h, e2.n)
        t.matmul(ps[0:128, 512:520], ctx.exp2_f[:], ctx.GAMS[0:2, 0:8],
                 start=True, stop=True)
        t.matmul(ps[0:128, 520:528], ctx.exp2_f[:], ctx.GAMS[0:2, 8:16],
                 start=True, stop=True).then_inc(e2.h, 1)
        e2.n += 1

    @blk.vector
    def _(v):
        v.wait_ge(e2.h, e2.n)
        v.tensor_tensor(ctx.ecol[:], ps[0:128, 520:528], ctx.gnw_t[:], ALU.mult)
        v.drain()
        v.tensor_tensor(ctx.ecol2[:], ps[0:128, 512:520], ctx.ecol[:], ALU.mult)
        v.drain()
        v.tensor_tensor(ctx.ecol2[:], ctx.gnb_t[:], ctx.ecol2[:], ALU.subtract)
        v.drain().then_inc(e3.h, 1)
        e3.n += 1

    @blk.vector
    def _(v):
        v.wait_ge(e3.h, e3.n)
        for dt in range(NDT):
            od = ctx.big4.ap()[:, dt * 2048:(dt + 1) * 2048]
            v.tensor_scalar(od, od, ctx.ecol[:, dt:dt + 1], ctx.ecol2[:, dt:dt + 1],
                            ALU.mult, ALU.add)
        v.drain().then_inc(e3.h, 1)
        e3.n += 1

    wbufe = ctx.big2.ap()[:, 0:8192].rearrange("p (k o c) -> p k o c", k=NCH, o=NDT)
    xnv = ctx.big0.ap().rearrange("p (c t) -> p c t", c=NCH)
    bq1 = q1.n
    be4 = ctx.e4.n

    @blk.sync
    def _(sy):
        for k in range(NCH):
            if k >= 1:
                sy.wait_ge(ctx.e4.h, be4 + k)
            else:
                sy.wait_ge(e3.h, e3.n)
            sy.dma_start(out=ctx.WST2f[:],
                         in_=ctx.wga[k * 128:(k + 1) * 128, :]).then_inc(q1.h, 16)
        q1.n += 16 * NCH

    @blk.vector
    def _(v):
        for k in range(NCH):
            v.wait_ge(q1.h, bq1 + 16 * (k + 1))
            v.tensor_copy(wbufe[:, k, :, :], ctx.WST2f[:]).then_inc(ctx.e4.h, 1)
        ctx.e4.n += NCH

    be5 = ctx.e5.n

    @blk.tensor
    def _(t):
        t.wait_ge(ctx.e4.h, ctx.e4.n)
        for oc in range(NDT):
            for t4 in range(4):
                if oc * 4 + t4 >= 2:
                    t.wait_ge(ctx.e5.h, be5 + 2 * (oc * 4 + t4) - 2)
                for k in range(NCH):
                    t.matmul(ps[:, ((oc * 4 + t4) % 2) * 512:
                                ((oc * 4 + t4) % 2) * 512 + 512],
                             wbufe[:, k, oc, :], xnv[:, k, t4 * 512:(t4 + 1) * 512],
                             start=(k == 0), stop=(k == NCH - 1))
                t.drain().then_inc(ctx.e5.h, 1)
        ctx.e5.n += 32

    be6 = ctx.e6.n

    @blk.scalar
    def _(sc):
        for i in range(32):
            sc.wait_ge(ctx.e5.h, be5 + 2 * i + 1)
            if i >= 1:
                sc.wait_ge(ctx.e6.h, be6 + i)
            sc.activation(out=ctx.GT[:], in_=ps[:, (i % 2) * 512:(i % 2) * 512 + 512],
                          func=AF.Sigmoid)
            sc.drain().then_inc(ctx.e5.h, 1)
        ctx.e5.n += 32

    @blk.vector
    def _(v):
        for i in range(32):
            oc, t4 = i // 4, i % 4
            v.wait_ge(ctx.e5.h, be5 + 2 * i + 2)
            sl = slice(oc * 2048 + t4 * 512, oc * 2048 + (t4 + 1) * 512)
            v.tensor_tensor(ctx.big1.ap()[:, sl], ctx.big4.ap()[:, sl], ctx.GT[:],
                            ALU.mult).then_inc(ctx.e6.h, 1)
        ctx.e6.n += 32


def emit_F(ctx, s):
    blk, ps = ctx.blk, ctx.ps
    q1, q3 = ctx.q1, ctx.q3
    f1, f2, f3, f4 = ctx.f1, ctx.f2, ctx.f3, ctx.f4
    e1 = ctx.e1
    xnv = ctx.big0.ap().rearrange("p (c t) -> p c t", c=NCH)
    W1 = ctx.big3.ap().rearrange("p (k f) -> p k f", k=NCH)
    W2 = ctx.big4.ap().rearrange("p (f d) -> p f d", f=NCK)
    gfm = ctx.G_FM
    acc = ctx.big2.ap()
    bf1, bf2, bf3, bf4 = f1.n, f2.n, f3.n, f4.n

    def w_chunks(e):
        w1f = [ctx.w1[e][(k // 2) * 128:(k // 2 + 1) * 128,
                         (k % 2) * 1024:(k % 2 + 1) * 1024] for k in range(16)]
        w1b = [ctx.w1_b[e][(k // 2) * 128:(k // 2 + 1) * 128,
                           (k % 2) * 1024:(k % 2 + 1) * 1024] for k in range(16)]
        w1d = [W1[:, k // 2, (k % 2) * 1024:(k % 2 + 1) * 1024] for k in range(16)]
        w2f = [ctx.w2[e][k * 128:(k + 1) * 128, :] for k in range(NCK)]
        w2b = [ctx.w2_b[e][k * 128:(k + 1) * 128, :] for k in range(NCK)]
        w2d = [W2[:, k, :] for k in range(NCK)]
        return w1f, w1b, w1d, w2f, w2b, w2d

    fw = ctx.fw
    bfw = fw.n

    for e in range(E):
        w1f, w1b, w1d, w2f, w2b, w2d = w_chunks(e)
        if s == 0:
            @blk.gpsimd
            def _(g, e=e, w1f=w1f, w2f=w2f, w1b=w1b, w2b=w2b, w1d=w1d, w2d=w2d):
                bq = bfw + 16 * 32 * e
                g.wait_ge(f1.h, bf1 + 16 * e)
                for k in range(16):
                    g.dma_start(out=w1d[k], in_=w1f[k]).then_inc(fw.h, 16)
                g.wait_ge(f3.h, bf3 + 16 * e)
                for k in range(NCK):
                    g.dma_start(out=w2d[k], in_=w2f[k]).then_inc(fw.h, 16)
                for k in range(32):
                    g.wait_ge(fw.h, bq + 16 * (k + 1))
                    src = w1d[k] if k < 16 else w2d[k - 16]
                    dstc = w1b[k] if k < 16 else w2b[k - 16]
                    g.dma_start(out=dstc, in_=src).then_inc(q3.h, 16)
                q3.n += 16 * 32
            fw.n += 16 * 32
        else:
            @blk.sync
            def _(sy, e=e, w1b=w1b, w2b=w2b, w1d=w1d, w2d=w2d):
                for k in range(32):
                    if k == 0:
                        sy.wait_ge(f1.h, bf1 + 16 * e)
                    if k == 16:
                        sy.wait_ge(f3.h, bf3 + 16 * e)
                    dst = w1d[k] if k < 16 else w2d[k - 16]
                    src = w1b[k] if k < 16 else w2b[k - 16]
                    sy.dma_start(out=dst, in_=src).then_inc(fw.h, 16)
                q1.n += 0
            fw.n += 16 * 32

        @blk.tensor
        def _(t, e=e):
            t.wait_ge(fw.h, bfw + 512 * e + 256)
            for t4 in range(4):
                for j in range(4):
                    qidx = (e * 4 + t4) * 4 + j
                    hb = (j + 1) % 2
                    if qidx >= 2:
                        t.wait_ge(f2.h, bf2 + qidx - 1)
                    if j == 1:
                        t.wait_ge(f4.h, bf4 + 4 * (e * 4 + t4))
                    for i in range(4):
                        fcg = j * 4 + i
                        for k in range(NCH):
                            t.matmul(ps[:, hb * 2048 + i * 512:hb * 2048 + (i + 1) * 512],
                                     W1[:, k, fcg * 128:(fcg + 1) * 128],
                                     xnv[:, k, t4 * 512:(t4 + 1) * 512],
                                     start=(k == 0), stop=(k == NCH - 1))
                    t.drain().then_inc(f1.h, 1)
                t.wait_ge(f2.h, bf2 + (e * 4 + t4) * 4 + 4)
                t.wait_ge(fw.h, bfw + 512 * (e + 1))
                for ts in range(4):
                    tc = t4 * 4 + ts
                    i2 = e * 16 + tc
                    if i2 >= 2:
                        t.wait_ge(f4.h, bf4 + i2 - 1)
                    pbase = (tc % 2) * 1024
                    for fc in range(NCK):
                        t.matmul(ps[:, pbase:pbase + 512],
                                 gfm[:, fc, ts * 128:(ts + 1) * 128],
                                 W2[:, fc, 0:512], start=(fc == 0), stop=(fc == NCK - 1))
                        t.matmul(ps[:, pbase + 512:pbase + 1024],
                                 gfm[:, fc, ts * 128:(ts + 1) * 128],
                                 W2[:, fc, 512:1024], start=(fc == 0), stop=(fc == NCK - 1))
                    t.drain().then_inc(f3.h, 1)

        @blk.scalar
        def _(sc, e=e):
            for t4 in range(4):
                for j in range(4):
                    qidx = (e * 4 + t4) * 4 + j
                    hb = (j + 1) % 2
                    sc.wait_ge(f1.h, bf1 + qidx + 1)
                    if e * 4 + t4 >= 1 and j == 0:
                        sc.wait_ge(f3.h, bf3 + 4 * (e * 4 + t4))
                    for i in range(4):
                        sc.activation(out=gfm[:, j * 4 + i, :],
                                      in_=ps[:, hb * 2048 + i * 512:
                                             hb * 2048 + (i + 1) * 512], func=AF.Gelu)
                    sc.drain().then_inc(f2.h, 1)

        @blk.vector
        def _(v, e=e):
            for t4 in range(4):
                for ts in range(4):
                    tc = t4 * 4 + ts
                    i2 = e * 16 + tc
                    v.wait_ge(f3.h, bf3 + i2 + 1)
                    pbase = (tc % 2) * 1024
                    gate = ctx.gates_own[:, e, s * NCK + tc:s * NCK + tc + 1]
                    asl = acc[:, tc * 1024:(tc + 1) * 1024]
                    if e == 0:
                        v.tensor_scalar(asl[:, 0:512], ps[:, pbase:pbase + 512],
                                        gate, None, ALU.mult)
                        v.tensor_scalar(asl[:, 512:1024], ps[:, pbase + 512:pbase + 1024],
                                        gate, None, ALU.mult).then_inc(f4.h, 1)
                    else:
                        v.scalar_tensor_tensor(asl[:, 0:512], ps[:, pbase:pbase + 512],
                                               gate, asl[:, 0:512], ALU.mult, ALU.add)
                        v.scalar_tensor_tensor(asl[:, 512:1024],
                                               ps[:, pbase + 512:pbase + 1024],
                                               gate, asl[:, 512:1024],
                                               ALU.mult, ALU.add).then_inc(f4.h, 1)
        f1.n += 16
        f2.n += 16
        f3.n += 16
        f4.n += 16


def emit_G(ctx, s):
    blk, ps = ctx.blk, ctx.ps
    q1, q2, q3 = ctx.q1, ctx.q2, ctx.q3
    g1, g2, g3 = ctx.e7, ctx.e8, ctx.e9
    og = ctx.big1.ap().rearrange("p (k t) -> p k t", k=NCH)
    acc = ctx.big2.ap()
    wbufg = ctx.big3.ap()[:, 0:8192].rearrange("p (k d) -> p k d", k=NCH)
    bg1, bg2, bg3 = g1.n, g2.n, g3.n
    bq1, bq2, bq3 = q1.n, q2.n, q3.n
    xo = ctx.x_own
    outv = ctx.out

    @blk.sync
    def _(sy):
        for k in range(NCH):
            if k >= 1:
                sy.wait_ge(g1.h, bg1 + k)
            sy.dma_start(out=ctx.WST2f[:],
                         in_=ctx.wo[k * 128:(k + 1) * 128, :]).then_inc(q1.h, 16)
        q1.n += 16 * NCH

    @blk.vector
    def _(v):
        for k in range(NCH):
            v.wait_ge(q1.h, bq1 + 16 * (k + 1))
            v.tensor_copy(wbufg[:, k, :], ctx.WST2f[:]).then_inc(g1.h, 1)
        g1.n += NCH

    @blk.scalar
    def _(sc):
        for tc in range(NCK):
            if tc >= 2:
                sc.wait_ge(g3.h, bg3 + tc - 1)
            sc.dma_start(out=ctx.XG[:, tc % 2],
                         in_=xo[s * S + tc * 128:s * S + (tc + 1) * 128, :]
                         ).then_inc(q2.h, 16)
        q2.n += 16 * NCK

    @blk.tensor
    def _(t):
        t.wait_ge(g1.h, g1.n)
        for tc in range(NCK):
            if tc >= 2:
                t.wait_ge(g3.h, bg3 + tc - 1)
            pbase = (tc % 2) * 1024
            for k in range(NCH):
                t.matmul(ps[:, pbase:pbase + 512], og[:, k, tc * 128:(tc + 1) * 128],
                         wbufg[:, k, 0:512], start=(k == 0), stop=(k == NCH - 1))
                t.matmul(ps[:, pbase + 512:pbase + 1024],
                         og[:, k, tc * 128:(tc + 1) * 128],
                         wbufg[:, k, 512:1024], start=(k == 0), stop=(k == NCH - 1))
            t.drain().then_inc(g2.h, 1)
        g2.n += NCK

    @blk.vector
    def _(v):
        for tc in range(NCK):
            v.wait_ge(g2.h, bg2 + tc + 1)
            v.wait_ge(q2.h, bq2 + 16 * (tc + 1))
            if tc >= 2:
                v.wait_ge(q3.h, bq3 + 16 * (tc - 1))
            pbase = (tc % 2) * 1024
            o1 = ctx.OUTST[:, tc % 2]
            for hf in range(2):
                v.tensor_tensor(o1[:, hf * 512:(hf + 1) * 512],
                                ps[:, pbase + hf * 512:pbase + (hf + 1) * 512],
                                ctx.XG[:, tc % 2][:, hf * 512:(hf + 1) * 512], ALU.add)
                v.tensor_tensor(o1[:, hf * 512:(hf + 1) * 512],
                                o1[:, hf * 512:(hf + 1) * 512],
                                acc[:, tc * 1024 + hf * 512:tc * 1024 + (hf + 1) * 512],
                                ALU.add)
            v.drain().then_inc(g3.h, 1)
        g3.n += NCK

    @blk.gpsimd
    def _(g):
        for tc in range(NCK):
            g.wait_ge(g3.h, bg3 + tc + 1)
            g.dma_start(out=outv[s * S + tc * 128:s * S + (tc + 1) * 128, :],
                        in_=ctx.OUTST[:, tc % 2]).then_inc(q3.h, 16)
        q3.n += 16 * NCK


def kernel(L0_attn_x, L0_attn_Wq, L0_attn_Wk, L0_attn_Wv, L0_attn_Wg,
           L0_attn_Wa, L0_attn_ba, L0_attn_Wo, L0_attn_gn_w, L0_attn_gn_b,
           L0_ffn_Wg, L0_ffn_W1, L0_ffn_W2):
    x = np.ascontiguousarray(np.asarray(L0_attn_x, dtype=np.float32)).reshape(T, DM)
    base = {
        "x_all": x,
        "wq": np.ascontiguousarray(L0_attn_Wq, dtype=np.float32),
        "wk": np.ascontiguousarray(L0_attn_Wk, dtype=np.float32),
        "wv": np.ascontiguousarray(L0_attn_Wv, dtype=np.float32),
        "wga": np.ascontiguousarray(L0_attn_Wg, dtype=np.float32),
        "wa": np.ascontiguousarray(L0_attn_Wa, dtype=np.float32),
        "ba": np.ascontiguousarray(L0_attn_ba, dtype=np.float32),
        "wo": np.ascontiguousarray(L0_attn_Wo, dtype=np.float32),
        "gnw": np.ascontiguousarray(L0_attn_gn_w, dtype=np.float32),
        "gnb": np.ascontiguousarray(L0_attn_gn_b, dtype=np.float32),
        "wgf": np.ascontiguousarray(L0_ffn_Wg, dtype=np.float32),
        "w1": np.ascontiguousarray(L0_ffn_W1, dtype=np.float32),
        "w2": np.ascontiguousarray(L0_ffn_W2, dtype=np.float32),
    }
    in_maps = []
    for c in range(NC_):
        m = dict(base)
        m["x_own"] = np.ascontiguousarray(x[c * TOWN:(c + 1) * TOWN])
        m["csel"] = np.array([c], dtype=np.int32)
        in_maps.append(m)
    nc, _ctx = build_nc()
    res = run_bass_kernel_spmd(nc, in_maps, core_ids=list(range(NC_)))
    out = np.concatenate([r["out"] for r in res.results], axis=0)
    return out.reshape(B, S, DM)


# version 05
# speedup vs baseline: 1.2660x; 1.0308x over previous
import contextlib
import numpy as np
import ml_dtypes
import concourse.bass as bass
import concourse.mybir as mybir
from concourse.bass_utils import run_bass_kernel_spmd

F32 = mybir.dt.float32
BF16 = mybir.dt.bfloat16
I32 = mybir.dt.int32
AF = mybir.ActivationFunctionType
ALU = mybir.AluOpType
AX = mybir.AxisListType
bf16 = ml_dtypes.bfloat16
ds = bass.ds

NC_ = 8
B, S, DM = 16, 2048, 1024
H, DH, DI, FF, E = 16, 64, 1024, 2048, 8
T = B * S
TOWN = T // NC_
NSEQ = 2
NCH = DM // 128
NDT = DI // 128
NBLK = T // 128
NOWN = TOWN // 128
NCK = S // 128
CAP = T // E
EPS = 1e-5
NITER = 64
MEGA = E + H + 1

DEBUG_OUTS = []
STAGES = {"0", "A", "B", "P", "S"}
SEQ_PHASES = "DEFG"
D_MMB = True
D_MMA = True
D_NCK = None
D_SUB = 99
P_PARTS = {1, 2, 3}
DEBUG_BIS = False
B_PARTS = 3
B_NT = None
B_PLAIN = False


class Sem:
    __slots__ = ("h", "n")

    def __init__(self, es, nc, name):
        self.h = es.enter_context(nc.semaphore(name))
        self.n = 0


class Ctx:
    pass


def _sb(ctx, name, shape, dt):
    t = ctx.es.enter_context(ctx.nc.sbuf_tensor(name, list(shape), dt))
    setattr(ctx, name, t)
    return t


def build_nc():
    nc = bass.Bass("TRN2", target_bir_lowering=False, detect_race_conditions=False)
    ctx = Ctx()
    ctx.nc = nc
    es = contextlib.ExitStack()
    ctx.es = es

    def din(name, shape, dt=F32):
        ap = nc.dram_tensor(name, list(shape), dt, kind="ExternalInput").ap()
        setattr(ctx, name, ap)

    din("x_own", [TOWN, DM])
    din("x_all", [T, DM])
    din("wq", [DM, DI]); din("wk", [DM, DI]); din("wv", [DM, DI])
    din("wga", [DM, DI]); din("wa", [DM, H]); din("ba", [H])
    din("wo", [DI, DM]); din("gnw", [DI]); din("gnb", [DI])
    din("wgf", [DM, E]); din("w1", [E, DM, FF]); din("w2", [E, FF, DM])
    din("csel", [1], I32)
    ctx.out = nc.dram_tensor("out", [TOWN, DM], F32, kind="ExternalOutput").ap()
    for name, shape, dt, _fn in DEBUG_OUTS:
        setattr(ctx, "dbg_" + name,
                nc.dram_tensor("dbg_" + name, list(shape), dt, kind="ExternalOutput").ap())

    def scr(name, shape, dt):
        ap = nc.dram_tensor(name, list(shape), dt, kind="Internal").ap()
        setattr(ctx, name, ap)

    scr("xhiT_own", [DM, TOWN], BF16)
    scr("w1_b", [E, DM, FF], BF16); scr("w2_b", [E, FF, DM], BF16)
    scr("msrow_d", [2, TOWN], F32)

    ctx.ps = es.enter_context(nc.psum_tensor("ps", [128, 4096], F32))

    _sb(ctx, "ones_f", [128, 128], F32)
    _sb(ctx, "ones_b", [128, 128], BF16)
    _sb(ctx, "ident_f", [128, 128], F32)
    _sb(ctx, "ident_b", [128, 128], BF16)
    _sb(ctx, "maskadd", [128, 128], F32)
    _sb(ctx, "exp2_f", [2, 128], F32)
    _sb(ctx, "exp2_b", [2, 128], BF16)
    _sb(ctx, "mega_hi", [128, NCH, 32], BF16)
    _sb(ctx, "mega_lo", [128, NCH, 32], BF16)
    _sb(ctx, "corr", [128, MEGA], F32)
    _sb(ctx, "ba_t", [128, H], F32)
    _sb(ctx, "gnw_t", [128, NDT], F32)
    _sb(ctx, "gnb_t", [128, NDT], F32)
    _sb(ctx, "csel_sb", [1, 1], I32)
    _sb(ctx, "cst", [128, 4], F32)
    _sb(ctx, "e16", [16, NDT, 128], BF16)
    _sb(ctx, "e2t", [128, 2], F32)
    _sb(ctx, "ecol", [128, 8], F32)
    _sb(ctx, "ecol2", [128, 8], F32)
    _sb(ctx, "tiny", [2, 4, 64], F32)

    _sb(ctx, "gates_own", [128, E, NOWN], F32)
    _sb(ctx, "awa_own", [128, H, NOWN], F32)
    _sb(ctx, "tau", [128, E], F32)
    _sb(ctx, "m_own", [128, NOWN], F32)
    _sb(ctx, "sinv_own", [128, NOWN], F32)
    _sb(ctx, "bis", [1, 32], F32)
    _sb(ctx, "bis2", [128, 4, E], F32)
    _sb(ctx, "cnt8", [128, E], F32)

    _sb(ctx, "big0", [128, 16384], BF16)
    _sb(ctx, "big1", [128, 16384], BF16)
    _sb(ctx, "big2", [128, 16384], BF16)
    _sb(ctx, "big3", [128, 16384], BF16)
    _sb(ctx, "big4", [128, 16384], BF16)
    _sb(ctx, "arena", [128, 18528], BF16)

    b0f = ctx.big0.ap().bitcast(F32)
    b1f = ctx.big1.ap().bitcast(F32)
    b2f = ctx.big2.ap().bitcast(F32)
    arf = ctx.arena.ap().bitcast(F32)
    arb = ctx.arena.ap()
    ctx.ALL = b0f[:, 0:MEGA * NBLK].rearrange("p (a b) -> p a b", a=MEGA)
    ctx.cmp_f = b1f[:, 0:E * NBLK].rearrange("p (a b) -> p a b", a=E)
    ctx.gates_all = b1f[:, E * NBLK:2 * E * NBLK].rearrange("p (a b) -> p a b", a=E)
    ctx.val_mx = b1f[:, 2 * E * NBLK:2 * E * NBLK + 4 * NBLK].rearrange(
        "p (a b) -> p a b", a=4)
    ctx.sumsq = b2f[:, 0:NBLK]
    ctx.m_all = b2f[:, NBLK:2 * NBLK]
    ctx.sinv_all = b2f[:, 2 * NBLK:3 * NBLK]
    ctx.ALLo = b2f[:, 3 * NBLK:3 * NBLK + MEGA * NOWN].rearrange(
        "p (a b) -> p a b", a=MEGA)
    ctx.sumsq_o = b2f[:, 3 * NBLK + MEGA * NOWN:3 * NBLK + MEGA * NOWN + NOWN]
    ctx.m_allo = b2f[:, 3 * NBLK + MEGA * NOWN + NOWN:3 * NBLK + MEGA * NOWN + 2 * NOWN]
    ctx.sinv_allo = b2f[:, 3 * NBLK + MEGA * NOWN + 2 * NOWN:3 * NBLK + MEGA * NOWN + 3 * NOWN]
    ctx.cmp_o = b1f[:, 2 * E * NBLK + 4 * NBLK:2 * E * NBLK + 4 * NBLK + E * NOWN].rearrange(
        "p (a b) -> p a b", a=E)
    ctx.val_mxo = b1f[:, 2 * E * NBLK + 4 * NBLK + E * NOWN:
                      2 * E * NBLK + 4 * NBLK + E * NOWN + 4 * NOWN].rearrange(
        "p (a b) -> p a b", a=4)
    ctx.xstage = arf[:, 0:2048].rearrange("p (a b) -> p a b", a=2)
    ctx.xhi_st = arb[:, 4096:6144].rearrange("p (a b) -> p a b", a=2)
    ctx.xfmstage = arb[:, 6144:10240].rearrange("p (a b) -> p a b", a=NCH)
    ctx.wstage = arf[:, 4096:5120]
    ctx.ost = arf[:, 5120:5632][0:MEGA, :]

    ctx.XFM = arb[:, 0:1024].rearrange("p (a b) -> p a b", a=2)
    ctx.LN_TMP = arf[:, 512:1024]
    ctx.MSLm = arf[0:1, 1024:2048]
    ctx.MSLs = arf[0:1, 2048:3072]
    ctx.WST2 = arf[:, 1024:2048]
    ctx.CUM = arf[0:16, 3072:5120]
    ctx.EBUF = arf[0:16, 5120:7168]
    ctx.EBE = arb[0:16, 28672 // 2:32768 // 2]
    ctx.CUMT = arf[:, 8192:8448].rearrange("p (a b) -> p a b", a=NCK)
    ctx.EDET = arf[:, 8448:8704].rearrange("p (a b) -> p a b", a=NCK)
    ctx.GAM2 = arf[:, 8704:8832]
    ctx.G2 = arf[0:2, 8832:8960]
    ctx.GAMST = arf[0:16, 6144:6160]
    ctx.SP_T = arf[:, 8960:9216].rearrange("p (a b) -> p a b", a=H)
    ctx.CB = arf[0:16, 9216:9232]
    ctx.CE = arf[0:16, 9232:9248]
    ctx.GAMS = arf[0:16, 9248:9264]
    ctx.GFLAT = arf[0:1, 6144:6400]
    ctx.EXPD = arb[:, 0:2048]
    ctx.A_SB = arb[:, 2048:4096]
    ctx.KDEC = arb[:, 4096:5120]
    ctx.S_B = arb[:, 5120:5632]
    ctx.S_F = arf[:, 5120:5632]
    ctx.PA = arb[:, 12288:13312].rearrange("p (a b) -> p a b", a=NDT)
    ctx.ECOLS = arf[:, 6912:6944]
    ctx.GT = arb[:, 13312:13824]
    ctx.G_FM = arb[:, 0:8192].rearrange("p (a b) -> p a b", a=16)
    ctx.WST2f = arf[:, 4096:5120]
    ctx.XG = arf[:, 0:2048].rearrange("p (a b) -> p a b", a=2)
    ctx.OUTST = arf[:, 2048:4096].rearrange("p (a b) -> p a b", a=2)
    ctx.T1A = arf[:, 5632:6144]
    ctx.T1B = arf[:, 6656:7168]

    ctx.sems = {}
    for s in ["dS", "dG", "dT", "dW", "dO", "cA", "cB", "cC", "cD", "cE", "cF", "BAR",
              "q1", "q2", "q3", "e1", "e2", "e3", "e4", "e5", "e6", "e7", "e8",
              "e9", "e10", "e11", "e12", "f1", "f2", "f3", "f4", "fw",
              "a1", "a2", "a3", "a4", "a5", "a6"]:
        setattr(ctx, s, Sem(es, nc, s))
    ctx.bar_n = 0

    with nc.Block() as blk:
        ctx.blk = blk
        emit_all(ctx)

    es.close()
    return nc, ctx


def barrier(ctx):
    blk = ctx.blk
    ctx.bar_n += 1
    target = 5 * ctx.bar_n
    bar = ctx.BAR.h
    for name in ["sync", "tensor", "vector", "scalar", "gpsimd"]:
        def f(eng):
            eng.drain().then_inc(bar, 1)
            eng.wait_ge(bar, target)
        getattr(blk, name)(f)


def emit_all(ctx):
    nc = ctx.nc
    if "0" in STAGES:
        with nc.named_scope("s0"):
            emit_stage0(ctx)
            barrier(ctx)
            emit_stage0b(ctx)
    if "A" in STAGES:
        with nc.named_scope("sA"):
            emit_stageA(ctx)
    if "B" in STAGES:
        with nc.named_scope("sB"):
            emit_stageB(ctx)
            barrier(ctx)
    if "P" in STAGES:
        with nc.named_scope("sP"):
            emit_post_routing(ctx)
            barrier(ctx)
    if "S" in STAGES:
        with nc.named_scope("sM"):
            emit_msrow(ctx)
            barrier(ctx)
        for s in range(NSEQ):
            emit_seq(ctx, s)
    for name, _s, _d, fn in DEBUG_OUTS:
        emit_debug_out(ctx, name, fn)
    finalize(ctx)


def emit_debug_out(ctx, name, fn):
    blk = ctx.blk
    ap = fn(ctx)

    @blk.sync
    def _(sy):
        sy.dma_start(out=getattr(ctx, "dbg_" + name), in_=ap).then_inc(ctx.dO.h, 16)
    ctx.dO.n += 16


def finalize(ctx):
    blk = ctx.blk

    @blk.sync
    def _(sy):
        if ctx.dO.n:
            sy.wait_ge(ctx.dO.h, ctx.dO.n)

    @blk.gpsimd
    def _(g):
        g.wait_ge(ctx.q3.h, ctx.q3.n)


def emit_stage0(ctx):
    blk = ctx.blk
    dW, dS = ctx.dW, ctx.dS

    @blk.gpsimd
    def _(g):
        g.memset(ctx.ones_f[:], 1.0)
        g.memset(ctx.ones_b[:], 1.0)
        g.memset(ctx.cst[:, 0:1], EPS)
        g.memset(ctx.cst[:, 1:2], 1.0)
        g.memset(ctx.cst[:, 2:3], 0.0)
        g.affine_select(ctx.ident_f[:], ctx.ones_f[:], compare_op=ALU.is_equal,
                        base=0, pattern=[[1, 128]], channel_multiplier=-1, fill=0.0)
        g.affine_select(ctx.ident_b[:], ctx.ones_b[:], compare_op=ALU.is_equal,
                        base=0, pattern=[[1, 128]], channel_multiplier=-1, fill=0.0)
        g.memset(ctx.maskadd[:], 0.0)
        g.affine_select(ctx.maskadd[:], ctx.maskadd[:], compare_op=ALU.is_ge,
                        base=0, pattern=[[1, 128]], channel_multiplier=-1, fill=-1e30)
        g.memset(ctx.exp2_f[:], 1.0)
        g.affine_select(ctx.exp2_f[:], ctx.exp2_f[:], compare_op=ALU.is_ge,
                        base=0, pattern=[[1, 128]], channel_multiplier=-64, fill=0.0)
        g.affine_select(ctx.exp2_f[:], ctx.exp2_f[:], compare_op=ALU.is_ge,
                        base=63, pattern=[[-1, 128]], channel_multiplier=64, fill=0.0)
        g.tensor_copy(ctx.exp2_b[:], ctx.exp2_f[:])
        g.memset(ctx.e16[:], 1.0)
        for dt in range(NDT):
            g.affine_select(ctx.e16[:, dt, :], ctx.e16[:, dt, :], compare_op=ALU.is_ge,
                            base=128 * dt, pattern=[[1, 128]], channel_multiplier=-64,
                            fill=0.0)
            g.affine_select(ctx.e16[:, dt, :], ctx.e16[:, dt, :], compare_op=ALU.is_ge,
                            base=63 - 128 * dt, pattern=[[-1, 128]], channel_multiplier=64,
                            fill=0.0)
        g.memset(ctx.e2t[:], 1.0)
        g.affine_select(ctx.e2t[:], ctx.e2t[:], compare_op=ALU.is_ge,
                        base=0, pattern=[[-64, 2]], channel_multiplier=1, fill=0.0)
        g.affine_select(ctx.e2t[:], ctx.e2t[:], compare_op=ALU.is_ge,
                        base=63, pattern=[[64, 2]], channel_multiplier=-1, fill=0.0)

    @blk.sync
    def _(sy):
        sy.dma_start(out=ctx.wstage[:, 0:NCH * E],
                     in_=ctx.wgf.rearrange("(c p) e -> p c e", p=128)).then_inc(dS.h, 16)
        sy.dma_start(out=ctx.wstage[:, NCH * E:NCH * E + NCH * H],
                     in_=ctx.wa.rearrange("(c p) h -> p c h", p=128)).then_inc(dS.h, 16)
        sy.dma_start(out=ctx.tiny[0:1, 0, 0:H], in_=ctx.ba.unsqueeze(0)).then_inc(dS.h, 16)
        sy.dma_start(out=ctx.gnw_t[:],
                     in_=ctx.gnw.rearrange("(c p) -> p c", p=128),
                     allow_slow_non_contiguous=True).then_inc(dS.h, 16)
        sy.dma_start(out=ctx.gnb_t[:],
                     in_=ctx.gnb.rearrange("(c p) -> p c", p=128),
                     allow_slow_non_contiguous=True).then_inc(dS.h, 16)
        sy.dma_start(out=ctx.csel_sb[:], in_=ctx.csel.unsqueeze(0)).then_inc(dS.h, 16)
        dS.n += 16 * 6
        sy.wait_ge(dS.h, dS.n)


def emit_stage0b(ctx):
    blk, ps = ctx.blk, ctx.ps
    cB, cC = ctx.cB, ctx.cC

    @blk.vector
    def _(v):
        v.memset(ctx.mega_hi[:], 0.0)
        v.memset(ctx.mega_lo[:], 0.0)
        for c in range(NCH):
            src = ctx.wstage[:, c * E:(c + 1) * E]
            wa_src = ctx.wstage[:, NCH * E + c * H:NCH * E + (c + 1) * H]
            v.tensor_copy(ctx.mega_hi[:, c, 0:E], src)
            v.tensor_copy(ctx.mega_hi[:, c, E:E + H], wa_src)
            v.tensor_copy(ctx.mega_hi[:, c, E + H:MEGA], ctx.ones_f[:, 0:1])
            v.drain()
            v.scalar_tensor_tensor(ctx.mega_lo[:, c, 0:E], ctx.mega_hi[:, c, 0:E],
                                   -1.0, src, ALU.mult, ALU.add)
            v.scalar_tensor_tensor(ctx.mega_lo[:, c, E:E + H], ctx.mega_hi[:, c, E:E + H],
                                   -1.0, wa_src, ALU.mult, ALU.add)
        v.drain().then_inc(cB.h, 1)
        cB.n += 1

    @blk.tensor
    def _(t):
        for c in range(NCH):
            t.matmul(ps[0:1, 0:E], ctx.ones_f[:, 0:1], ctx.wstage[:, c * E:(c + 1) * E],
                     start=(c == 0), stop=(c == NCH - 1))
        for c in range(NCH):
            t.matmul(ps[0:1, 32:32 + H], ctx.ones_f[:, 0:1],
                     ctx.wstage[:, NCH * E + c * H:NCH * E + (c + 1) * H],
                     start=(c == 0), stop=(c == NCH - 1))
        t.drain().then_inc(cC.h, 1)
        cC.n += 1

    @blk.vector
    def _(v):
        v.wait_ge(cC.h, cC.n)
        v.tensor_copy(ctx.tiny[0:1, 1, 0:E], ps[0:1, 0:E])
        v.tensor_copy(ctx.tiny[0:1, 1, E:E + H], ps[0:1, 32:32 + H])
        v.memset(ctx.tiny[0:1, 1, E + H:MEGA], float(DM))
        v.drain().then_inc(cC.h, 1)
        cC.n += 1

    @blk.tensor
    def _(t):
        t.wait_ge(cC.h, cC.n)
        t.matmul(ps[:, 0:32][:, 0:MEGA], ctx.ones_f[0:1, :], ctx.tiny[0:1, 1, 0:MEGA],
                 start=True, stop=True)
        t.matmul(ps[:, 32:32 + H], ctx.ones_f[0:1, :], ctx.tiny[0:1, 0, 0:H],
                 start=True, stop=True)
        t.drain().then_inc(cC.h, 1)
        cC.n += 1

    @blk.vector
    def _(v):
        v.wait_ge(cC.h, cC.n)
        v.tensor_copy(ctx.corr[:], ps[:, 0:MEGA])
        v.tensor_copy(ctx.ba_t[:], ps[:, 32:32 + H])
        v.drain().then_inc(cC.h, 1)
        cC.n += 1


def emit_stageA(ctx):
    blk, ps = ctx.blk, ctx.ps
    psb = ctx.ps.ap().bitcast(BF16)
    dS, dW = ctx.dS, ctx.dW
    a1, a2, a3, a4, a5, a6 = (ctx.a1, ctx.a2, ctx.a3, ctx.a4, ctx.a5, ctx.a6)
    xa = ctx.x_all.rearrange("(n p) d -> n p d", p=128)
    xo = ctx.x_own.rearrange("(n p) d -> n p d", p=128)
    ntile = NBLK + NOWN
    ngrp = ntile // 4
    b1, b2, b3, b4, b5, b6 = a1.n, a2.n, a3.n, a4.n, a5.n, a6.n
    bS, bW = dS.n, dW.n
    mega_done = ctx.cB.n
    ps_free = ctx.cC.n

    @blk.sync
    def _(sy):
        for i in range(ntile):
            if i >= 2:
                sy.wait_ge(a1.h, b1 + (i - 2) + 1)
            src = xa[i] if i < NBLK else xo[i - NBLK]
            sy.dma_start(out=ctx.xstage[:, i % 2], in_=src).then_inc(dS.h, 16)
        dS.n += 16 * ntile

    @blk.vector
    def _(v):
        for i in range(ntile):
            b = i % 2
            g = i // 4
            v.wait_ge(dS.h, bS + 16 * (i + 1))
            if i >= 2:
                v.wait_ge(a2.h, b2 + (i - 2) + 1)
            v.tensor_copy(ctx.xhi_st[:, b], ctx.xstage[:, b])
            acc = (ctx.sumsq[:, i:i + 1] if i < NBLK
                   else ctx.sumsq_o[:, i - NBLK:i - NBLK + 1])
            v.scalar_tensor_tensor(ctx.xstage[:, b], ctx.xhi_st[:, b], 1.0,
                                   ctx.xhi_st[:, b], ALU.bypass, ALU.mult,
                                   accum_out=acc)
            v.drain().then_inc(a1.h, 1)
            v.wait_ge(a2.h, b2 + i + 1)
            if i % 4 == 0:
                if i >= 4:
                    v.wait_ge(a4.h, b4 + i - 3)
                if 65 <= g <= 72:
                    v.wait_ge(dW.h, bW + 128 * (g - 64))
            for c in range(NCH):
                v.tensor_copy(ctx.xfmstage[:, c, (i % 4) * 128:(i % 4 + 1) * 128],
                              psb[:, (4 + i % 2) * 1024 + c * 128:
                                  (4 + i % 2) * 1024 + (c + 1) * 128])
            v.drain().then_inc(a3.h, 1)
            if i % 4 == 3:
                v.wait_ge(a4.h, b4 + i + 1)
                v.tensor_copy(ctx.ost[:], ps[0:MEGA, 0:512])
                v.drain().then_inc(a5.h, 1)
                v.wait_ge(a6.h, b6 + g + 1)
                for j in range(4):
                    if g < NBLK // 4:
                        dst = ctx.ALL[:, :, 4 * g + j]
                    else:
                        dst = ctx.ALLo[:, :, 4 * (g - NBLK // 4) + j]
                    v.tensor_copy(dst, ps[:, 512 + j * 32:512 + j * 32 + MEGA])
                v.drain().then_inc(a5.h, 1)
        a1.n += ntile
        a3.n += ntile
        a5.n += 2 * ngrp

    @blk.tensor
    def _(t):
        t.wait_ge(ctx.cB.h, mega_done)
        t.wait_ge(ctx.cC.h, ps_free)
        for i in range(ntile):
            g = i // 4
            t.wait_ge(a1.h, b1 + i + 1)
            if i >= 2:
                t.wait_ge(a3.h, b3 + i - 1)
            for c in range(NCH):
                t.transpose(psb[:, (4 + i % 2) * 1024 + c * 128:
                                (4 + i % 2) * 1024 + (c + 1) * 128],
                            ctx.xhi_st[:, i % 2, c * 128:(c + 1) * 128],
                            ctx.ident_b[:])
            t.drain().then_inc(a2.h, 1)
            t.wait_ge(a3.h, b3 + i + 1)
            if i % 4 == 0 and g >= 1:
                t.wait_ge(a5.h, b5 + 2 * g - 1)
            for c in range(NCH):
                t.matmul(ps[0:MEGA, (i % 4) * 128:(i % 4 + 1) * 128],
                         ctx.mega_hi[:, c, 0:MEGA],
                         ctx.xfmstage[:, c, (i % 4) * 128:(i % 4 + 1) * 128],
                         start=(c == 0), stop=False)
                t.matmul(ps[0:MEGA, (i % 4) * 128:(i % 4 + 1) * 128],
                         ctx.mega_lo[:, c, 0:MEGA],
                         ctx.xfmstage[:, c, (i % 4) * 128:(i % 4 + 1) * 128],
                         start=False, stop=(c == NCH - 1))
            t.drain().then_inc(a4.h, 1)
            if i % 4 == 3:
                t.wait_ge(a5.h, b5 + 2 * g + 1)
                for j in range(4):
                    t.transpose(ps[:, 512 + j * 32:512 + j * 32 + MEGA],
                                ctx.ost[:, j * 128:(j + 1) * 128],
                                ctx.ident_f[0:MEGA, 0:MEGA])
                t.drain().then_inc(a6.h, 1)
        a2.n += ntile
        a4.n += ntile
        a6.n += ngrp

    @blk.gpsimd
    def _(g_):
        for g in range(NBLK // 4, ngrp):
            go = g - NBLK // 4
            g_.wait_ge(a3.h, b3 + 4 * (g + 1))
            for c in range(NCH):
                g_.dma_start(out=ctx.xhiT_own[c * 128:(c + 1) * 128,
                                              go * 512:(go + 1) * 512],
                             in_=ctx.xfmstage[:, c, :]).then_inc(dW.h, 16)
        dW.n += 16 * NCH * NOWN // 4


def emit_stageB(ctx):
    pass


def emit_post_routing(ctx):
    blk, ps = ctx.blk, ctx.ps
    cC, cD = ctx.cC, ctx.cD
    Bp = ctx.ALL[:][:, 0:E, :]
    sums = ctx.ALL[:][:, E + H, :]
    Bpo = ctx.ALLo[:][:, 0:E, :]
    awao = ctx.ALLo[:][:, E:E + H, :]
    sumso = ctx.ALLo[:][:, E + H, :]
    if 1 not in P_PARTS:
        return

    @blk.vector
    def _(v):
        v.tensor_scalar(ctx.m_all[:], sums, 1.0 / DM, None, ALU.mult)
        v.tensor_tensor(ctx.sinv_all[:], ctx.m_all[:], ctx.m_all[:], ALU.mult)
        v.scalar_tensor_tensor(ctx.sinv_all[:], ctx.sumsq[:], 1.0 / DM, ctx.sinv_all[:],
                               ALU.mult, ALU.subtract)
        v.tensor_scalar(ctx.m_allo[:], sumso, 1.0 / DM, None, ALU.mult)
        v.drain()
        v.tensor_tensor(ctx.sinv_allo[:], ctx.m_allo[:], ctx.m_allo[:], ALU.mult)
        v.drain()
        v.scalar_tensor_tensor(ctx.sinv_allo[:], ctx.sumsq_o[:], 1.0 / DM,
                               ctx.sinv_allo[:], ALU.mult, ALU.subtract)
        v.drain().then_inc(cC.h, 1)
        cC.n += 1

    @blk.scalar
    def _(s):
        s.wait_ge(cC.h, cC.n)
        s.activation(out=ctx.sinv_all[:], in_=ctx.sinv_all[:], func=AF.Ln, bias=ctx.cst[:, 0:1])
        s.activation(out=ctx.sinv_all[:], in_=ctx.sinv_all[:], func=AF.Exp, scale=-0.5)
        s.activation(out=ctx.sinv_allo[:], in_=ctx.sinv_allo[:], func=AF.Ln, bias=ctx.cst[:, 0:1])
        s.drain()
        s.activation(out=ctx.sinv_allo[:], in_=ctx.sinv_allo[:], func=AF.Exp, scale=-0.5)
        s.drain()
        s.drain().then_inc(cC.h, 1)
        cC.n += 1

    @blk.vector
    def _(v):
        v.wait_ge(cC.h, cC.n)
        m_v = ctx.m_all[:].unsqueeze(1).to_broadcast([128, E, NBLK])
        s_v = ctx.corr[:, 0:E].unsqueeze(2).to_broadcast([128, E, NBLK])
        sinv_v = ctx.sinv_all[:].unsqueeze(1).to_broadcast([128, E, NBLK])
        v.tensor_tensor(ctx.cmp_f[:], m_v, s_v, ALU.mult)
        v.tensor_tensor(Bp, Bp, ctx.cmp_f[:], ALU.subtract)
        v.tensor_tensor(Bp, Bp, sinv_v, ALU.mult)
        mx = ctx.val_mx[:]
        v.tensor_tensor(mx[:, 0:4, :], Bp[:, 0:4, :], Bp[:, 4:8, :], ALU.max)
        v.tensor_tensor(mx[:, 0:2, :], mx[:, 0:2, :], mx[:, 2:4, :], ALU.max)
        v.tensor_tensor(mx[:, 0:1, :], mx[:, 0:1, :], mx[:, 1:2, :], ALU.max)
        v.tensor_tensor(Bp, Bp, mx[:, 0:1, :].to_broadcast([128, E, NBLK]), ALU.subtract)
        m_vo = ctx.m_allo[:].unsqueeze(1).to_broadcast([128, E, NOWN])
        s_vo = ctx.corr[:, 0:E].unsqueeze(2).to_broadcast([128, E, NOWN])
        sinv_vo = ctx.sinv_allo[:].unsqueeze(1).to_broadcast([128, E, NOWN])
        v.tensor_tensor(ctx.cmp_o[:], m_vo, s_vo, ALU.mult)
        v.tensor_tensor(Bpo, Bpo, ctx.cmp_o[:], ALU.subtract)
        v.tensor_tensor(Bpo, Bpo, sinv_vo, ALU.mult)
        mo = ctx.val_mxo[:]
        v.tensor_tensor(mo[:, 0:4, :], Bpo[:, 0:4, :], Bpo[:, 4:8, :], ALU.max)
        v.tensor_tensor(mo[:, 0:2, :], mo[:, 0:2, :], mo[:, 2:4, :], ALU.max)
        v.drain()
        v.tensor_tensor(mo[:, 0:1, :], mo[:, 0:1, :], mo[:, 1:2, :], ALU.max)
        v.drain()
        v.tensor_tensor(Bpo, Bpo, mo[:, 0:1, :].to_broadcast([128, E, NOWN]), ALU.subtract)
        sw_vo = ctx.corr[:, E:E + H].unsqueeze(2).to_broadcast([128, H, NOWN])
        m_vo16 = ctx.m_allo[:].unsqueeze(1).to_broadcast([128, H, NOWN])
        sinv_vo16 = ctx.sinv_allo[:].unsqueeze(1).to_broadcast([128, H, NOWN])
        v.tensor_tensor(ctx.awa_own[:], m_vo16, sw_vo, ALU.mult)
        v.drain()
        v.tensor_tensor(ctx.awa_own[:], awao, ctx.awa_own[:], ALU.subtract)
        v.drain()
        v.tensor_tensor(ctx.awa_own[:], ctx.awa_own[:], sinv_vo16, ALU.mult)
        v.drain()
        v.tensor_copy(ctx.m_own[:], ctx.m_allo[:])
        v.tensor_copy(ctx.sinv_own[:], ctx.sinv_allo[:])
        v.drain().then_inc(cC.h, 1)
        cC.n += 1

    @blk.scalar
    def _(s):
        s.wait_ge(cC.h, cC.n)
        s.activation(out=Bp, in_=Bp, func=AF.Exp)
        s.activation(out=Bpo, in_=Bpo, func=AF.Exp)
        s.drain().then_inc(cC.h, 1)
        cC.n += 1

    @blk.vector
    def _(v):
        v.wait_ge(cC.h, cC.n)
        mx = ctx.val_mx[:]
        v.tensor_tensor(mx[:, 0:4, :], Bp[:, 0:4, :], Bp[:, 4:8, :], ALU.add)
        v.tensor_tensor(mx[:, 0:2, :], mx[:, 0:2, :], mx[:, 2:4, :], ALU.add)
        v.tensor_tensor(mx[:, 0:1, :], mx[:, 0:1, :], mx[:, 1:2, :], ALU.add)
        v.reciprocal(mx[:, 1:2, :], mx[:, 0:1, :])
        v.tensor_tensor(Bp, Bp, mx[:, 1:2, :].to_broadcast([128, E, NBLK]), ALU.mult)
        mo = ctx.val_mxo[:]
        v.tensor_tensor(mo[:, 0:4, :], Bpo[:, 0:4, :], Bpo[:, 4:8, :], ALU.add)
        v.tensor_tensor(mo[:, 0:2, :], mo[:, 0:2, :], mo[:, 2:4, :], ALU.add)
        v.drain()
        v.tensor_tensor(mo[:, 0:1, :], mo[:, 0:1, :], mo[:, 1:2, :], ALU.add)
        v.drain()
        v.reciprocal(mo[:, 1:2, :], mo[:, 0:1, :])
        v.drain()
        v.tensor_tensor(Bpo, Bpo, mo[:, 1:2, :].to_broadcast([128, E, NOWN]), ALU.mult)
        v.memset(ctx.bis2[:, 0, :], 0.0)
        v.memset(ctx.bis2[:, 1, :], 1.0)
        v.memset(ctx.tau[:], 0.5)
        v.drain()
        v.drain().then_inc(cC.h, 1)
        cC.n += 1

    if 2 not in P_PARTS:
        return
    for it in range(NITER):
        @blk.vector
        def _(v):
            v.wait_ge(cC.h, cC.n)
            tau_v = ctx.tau[:].unsqueeze(2).to_broadcast([128, E, NBLK])
            v.tensor_tensor(ctx.cmp_f[:], Bp, tau_v, ALU.is_ge)
            v.tensor_reduce(ctx.cnt8[:], ctx.cmp_f[:], AX.X, ALU.add).then_inc(cD.h, 1)
            cD.n += 1

        @blk.tensor
        def _(t):
            t.wait_ge(cD.h, cD.n)
            t.matmul(ps[0:128, 3584:3584 + E], ctx.ones_f[:, 0:128], ctx.cnt8[:],
                     start=True, stop=True).then_inc(cD.h, 1)
            cD.n += 1

        @blk.vector
        def _(v):
            v.wait_ge(cD.h, cD.n)
            lo = ctx.bis2[:, 0, :]
            hi = ctx.bis2[:, 1, :]
            d1 = ctx.bis2[:, 2, :]
            g = ctx.bis2[:, 3, :]
            cntR = ctx.cnt8[:]
            mid = ctx.tau[:]
            v.tensor_copy(cntR, ps[0:128, 3584:3584 + E])
            v.drain()
            v.tensor_scalar(g, cntR, float(CAP), None, ALU.is_ge)
            v.tensor_tensor(d1, mid, lo, ALU.subtract)
            v.drain()
            v.tensor_tensor(d1, d1, g, ALU.mult)
            v.drain()
            v.tensor_tensor(lo, lo, d1, ALU.add)
            v.tensor_tensor(d1, hi, mid, ALU.subtract)
            v.drain()
            v.tensor_tensor(d1, d1, g, ALU.mult)
            v.drain()
            v.tensor_tensor(hi, mid, d1, ALU.add)
            v.drain()
            v.tensor_tensor(d1, lo, hi, ALU.add)
            v.drain()
            v.tensor_scalar(ctx.tau[:], d1, 0.5, None, ALU.mult).then_inc(cC.h, 1)
            cC.n += 1
    @blk.vector
    def _(v):
        v.wait_ge(cC.h, cC.n)
        v.drain()
        v.tensor_copy(ctx.tau[:], ctx.bis2[:, 0, :])
        v.drain().then_inc(cC.h, 1)
        cC.n += 1

    if 3 not in P_PARTS:
        return

    @blk.vector
    def _(v):
        v.wait_ge(cC.h, cC.n)
        tau_v = ctx.tau[:].unsqueeze(2).to_broadcast([128, E, NOWN])
        sel = ctx.cmp_o[:]
        mx = ctx.val_mxo[:]
        v.tensor_tensor(sel, Bpo, tau_v, ALU.is_ge)
        v.scalar_tensor_tensor(sel, sel, 2.0, Bpo, ALU.mult, ALU.add)
        v.tensor_tensor(mx[:, 0:4, :], sel[:, 0:4, :], sel[:, 4:8, :], ALU.max)
        v.tensor_tensor(mx[:, 0:2, :], mx[:, 0:2, :], mx[:, 2:4, :], ALU.max)
        v.tensor_tensor(mx[:, 0:1, :], mx[:, 0:1, :], mx[:, 1:2, :], ALU.max)
        seen = mx[:, 1:2, :]
        hit = mx[:, 2:3, :]
        v.memset(seen, 0.0)
        v.drain()
        for e in range(E):
            ga = ctx.gates_own[:, e:e + 1, :]
            v.tensor_tensor(hit, sel[:, e:e + 1, :], mx[:, 0:1, :], ALU.is_ge)
            v.drain()
            v.tensor_tensor(ga, hit, seen, ALU.subtract)
            v.drain()
            v.tensor_scalar_max(ga, ga, 0.0)
            v.tensor_tensor(seen, seen, hit, ALU.max)
            v.drain()
            v.tensor_tensor(ga, ga, Bpo[:, e:e + 1, :], ALU.mult)
            v.drain()
        v.drain().then_inc(cC.h, 1)
        cC.n += 1



def emit_seq(ctx, s):
    nc = ctx.nc
    with nc.named_scope(f"C{s}"):
        emit_C(ctx, s)
        barrier(ctx)
    if "D" not in SEQ_PHASES:
        return
    with nc.named_scope(f"D{s}"):
        emit_D(ctx, s)
        barrier(ctx)
    if "E" not in SEQ_PHASES:
        return
    with nc.named_scope(f"E{s}"):
        emit_E(ctx, s)
        barrier(ctx)
    if "F" not in SEQ_PHASES:
        return
    with nc.named_scope(f"F{s}"):
        emit_F(ctx, s)
        barrier(ctx)
    with nc.named_scope(f"G{s}"):
        emit_G(ctx, s)
        barrier(ctx)


def emit_msrow(ctx):
    blk, ps = ctx.blk, ctx.ps

    @blk.tensor
    def _(t):
        t.transpose(ps[0:32, 0:128], ctx.m_own[:], ctx.ident_f[:])
        t.transpose(ps[0:32, 128:256], ctx.sinv_own[:], ctx.ident_f[:]).then_inc(ctx.e1.h, 1)
        ctx.e1.n += 1

    @blk.vector
    def _(v):
        v.wait_ge(ctx.e1.h, ctx.e1.n)
        v.tensor_copy(ctx.wstage[0:32, 0:128], ps[0:32, 0:128])
        v.tensor_copy(ctx.wstage[0:32, 128:256], ps[0:32, 128:256]).then_inc(ctx.e2.h, 1)
        ctx.e2.n += 1

    @blk.sync
    def _(sy):
        sy.wait_ge(ctx.e2.h, ctx.e2.n)
        sy.dma_start(out=ctx.msrow_d[0:1, :], in_=ctx.wstage[0:32, 0:128]).then_inc(ctx.q1.h, 16)
        sy.dma_start(out=ctx.msrow_d[1:2, :], in_=ctx.wstage[0:32, 128:256]).then_inc(ctx.q1.h, 16)
        ctx.q1.n += 32
        sy.wait_ge(ctx.q1.h, ctx.q1.n)


def emit_C(ctx, s):
    blk, ps = ctx.blk, ctx.ps
    nc = ctx.nc
    q1, q2, e1, e2, e3, e4 = ctx.q1, ctx.q2, ctx.e1, ctx.e2, ctx.e3, ctx.e4

    @blk.vector
    def _(v):
        v.tensor_tensor(ctx.SP_T[:], ctx.awa_own[:, :, s * NCK:(s + 1) * NCK],
                        ctx.ba_t[:].unsqueeze(2).to_broadcast([128, H, NCK]),
                        ALU.add).then_inc(e1.h, 1)
        e1.n += 1

    @blk.scalar
    def _(sc):
        sc.wait_ge(e1.h, e1.n)
        sc.activation(out=ctx.SP_T[:], in_=ctx.SP_T[:], func=AF.Exp)
        sc.drain()
        sc.activation(out=ctx.SP_T[:], in_=ctx.SP_T[:], func=AF.Ln,
                      bias=ctx.cst[:, 1:2])
        sc.drain().then_inc(e1.h, 1)
        e1.n += 1

    @blk.tensor
    def _(t):
        t.wait_ge(e1.h, e1.n)
        for tt in range(NCK):
            t.transpose(ps[0:16, tt * 128:tt * 128 + 128],
                        ctx.SP_T[:, :, tt], ctx.ident_f[:])
        t.drain().then_inc(e2.h, 1)
        e2.n += 1

    @blk.vector
    def _(v):
        v.wait_ge(e2.h, e2.n)
        v.tensor_copy(ctx.EBUF[:], ps[0:16, 0:2048])
        v.drain()
        v.tensor_tensor_scan(ctx.CUM[:], ctx.EBUF[:],
                             ctx.cst[0:16, 2:3].to_broadcast([16, 2048]),
                             0.0, ALU.add, ALU.add)
        v.drain()
        v.memset(ctx.CB[:], 0.0)
        v.drain()
        v.tensor_copy(ctx.CB[:, 1:16], ctx.CUM[:, 127:2047:128])
        v.tensor_copy(ctx.CE[:], ctx.CUM[:, 127:2048:128])
        v.drain()
        v.tensor_tensor(ctx.EBUF[:],
                        ctx.CB[:].unsqueeze(2).to_broadcast([16, NCK, 128]),
                        ctx.CUM[:].rearrange("p (c i) -> p c i", c=NCK),
                        ALU.subtract)
        v.tensor_tensor(ctx.GAMS[:], ctx.CB[:], ctx.CE[:], ALU.subtract)
        v.drain().then_inc(e2.h, 1)
        e2.n += 1

    @blk.scalar
    def _(sc):
        sc.wait_ge(e2.h, e2.n)
        sc.activation(out=ctx.EBE[:], in_=ctx.EBUF[:], func=AF.Exp)
        sc.drain()
        sc.activation(out=ctx.GAMS[:], in_=ctx.GAMS[:], func=AF.Exp)
        sc.drain().then_inc(e3.h, 1)
        e3.n += 1

    @blk.vector
    def _(v):
        v.wait_ge(e3.h, e3.n)
        v.tensor_tensor(ctx.EBUF[:],
                        ctx.CUM[:].rearrange("p (c i) -> p c i", c=NCK),
                        ctx.CE[:].unsqueeze(2).to_broadcast([16, NCK, 128]),
                        ALU.subtract).then_inc(e3.h, 1)
        e3.n += 1

    @blk.scalar
    def _(sc):
        sc.wait_ge(e3.h, e3.n)
        sc.activation(out=ctx.EBUF[:], in_=ctx.EBUF[:], func=AF.Exp)
        sc.drain().then_inc(e3.h, 1)
        e3.n += 1

    @blk.tensor
    def _(t):
        t.wait_ge(e3.h, e3.n)
        t.transpose(ps[0:16, 512:528], ctx.GAMS[:], ctx.ident_f[0:16, 0:16]
                    ).then_inc(e3.h, 1)
        e3.n += 1

    @blk.vector
    def _(v):
        v.wait_ge(e3.h, e3.n)
        v.tensor_copy(ctx.GAMST[:], ps[0:16, 512:528]).then_inc(e3.h, 1)
        e3.n += 1

    @blk.sync
    def _(sy):
        sy.wait_ge(e3.h, e3.n)
        for j in range(2):
            sy.dma_start(out=ctx.G2[j:j + 1, :],
                         in_=ctx.GAMST[:, j:16:2],
                         allow_slow_non_contiguous=True).then_inc(q1.h, 16)
        q1.n += 32

    @blk.tensor
    def _(t):
        t.wait_ge(e3.h, e3.n)
        for c in range(NCK):
            t.transpose(ps[0:128, 2048 + c * 32:2048 + c * 32 + 16],
                        ctx.CUM[:, c * 128:(c + 1) * 128], ctx.ident_f[0:16, 0:16])
            t.transpose(ps[0:128, 3072 + c * 32:3072 + c * 32 + 16],
                        ctx.EBUF[:, c * 128:(c + 1) * 128], ctx.ident_f[0:16, 0:16])
        t.drain().then_inc(e4.h, 1)
        e4.n += 1
        t.wait_ge(q1.h, q1.n)
        t.matmul(ps[0:128, 1536:1664], ctx.exp2_f[:], ctx.G2[:],
                 start=True, stop=True).then_inc(e4.h, 1)
        e4.n += 1

    @blk.vector
    def _(v):
        v.wait_ge(e4.h, e4.n)
        for c in range(NCK):
            v.tensor_copy(ctx.CUMT[:, c, :], ps[0:128, 2048 + c * 32:2048 + c * 32 + 16])
            v.tensor_copy(ctx.EDET[:, c, :], ps[0:128, 3072 + c * 32:3072 + c * 32 + 16])
        v.tensor_copy(ctx.GAM2[:], ps[0:128, 1536:1664])
        v.drain().then_inc(e4.h, 1)
        e4.n += 1

    xho = ctx.xhiT_own
    base_q2 = q2.n
    base_e5 = ctx.e5.n
    base_e6 = ctx.e6.n

    @blk.scalar
    def _(sc):
        for t4 in range(4):
            if t4 >= 1:
                sc.wait_ge(ctx.e5.h, base_e5 + t4)
            tok0 = s * S + t4 * 512
            sc.dma_start(out=ctx.MSLm[:, (t4 % 2) * 512:(t4 % 2) * 512 + 512],
                         in_=ctx.msrow_d[0:1, s * S + t4 * 512: s * S + (t4 + 1) * 512]
                         ).then_inc(q2.h, 16)
            sc.dma_start(out=ctx.MSLs[:, (t4 % 2) * 512:(t4 % 2) * 512 + 512],
                         in_=ctx.msrow_d[1:2, s * S + t4 * 512: s * S + (t4 + 1) * 512]
                         ).then_inc(q2.h, 16)
            for ch in range(NCH):
                k = t4 * NCH + ch
                if k >= 2:
                    sc.wait_ge(ctx.e6.h, base_e6 + (k - 2) + 1)
                sc.dma_start(
                    out=ctx.XFM[:, k % 2],
                    in_=xho[ch * 128:(ch + 1) * 128,
                            s * S + t4 * 512:s * S + (t4 + 1) * 512]
                    ).then_inc(q2.h, 16)
        q2.n += 16 * 4 * (NCH + 2)

    @blk.tensor
    def _(t, _e4n=ctx.e4.n):
        t.wait_ge(ctx.e4.h, _e4n)
        for t4 in range(4):
            t.wait_ge(q2.h, base_q2 + 16 * (t4 * (NCH + 2) + 2))
            if t4 >= 1:
                t.wait_ge(ctx.e6.h, base_e6 + t4 * NCH)
            t.matmul(ps[:, 3072:3584], ctx.ones_f[0:1, :],
                     ctx.MSLm[:, (t4 % 2) * 512:(t4 % 2) * 512 + 512],
                     start=True, stop=True)
            t.matmul(ps[:, 3584:4096], ctx.ones_f[0:1, :],
                     ctx.MSLs[:, (t4 % 2) * 512:(t4 % 2) * 512 + 512],
                     start=True, stop=True).then_inc(ctx.e5.h, 1)
        ctx.e5.n += 4

    @blk.vector
    def _(v):
        for t4 in range(4):
            for ch in range(NCH):
                k = t4 * NCH + ch
                v.wait_ge(q2.h, base_q2 + 16 * (t4 * (NCH + 2) + 2 + ch + 1))
                v.wait_ge(ctx.e5.h, base_e5 + t4 + 1)
                v.tensor_tensor(ctx.LN_TMP[:], ctx.XFM[:, k % 2],
                                ps[:, 3072:3584], ALU.subtract)
                v.tensor_tensor(ctx.big0[:, ch * 2048 + t4 * 512:
                                         ch * 2048 + (t4 + 1) * 512],
                                ctx.LN_TMP[:], ps[:, 3584:4096],
                                ALU.mult).then_inc(ctx.e6.h, 1)
        ctx.e6.n += 4 * NCH

    emit_proj(ctx, s)


def emit_proj(ctx, s):
    blk, ps = ctx.blk, ctx.ps
    q1, e7, e8 = ctx.q1, ctx.e7, ctx.e8
    wbuf = ctx.big4.ap()[:, 0:8192].rearrange("p (k o c) -> p k o c", k=NCH, o=NDT)
    xnv = ctx.big0.ap().rearrange("p (c t) -> p c t", c=NCH)

    for wi, (wsrc, dstbig, act) in enumerate([
            (ctx.wq, ctx.big1, "silu"), (ctx.wk, ctx.big2, "silu"),
            (ctx.wv, ctx.big3, "copy")]):
        base_q1 = q1.n
        base_e7 = e7.n
        base_e8 = e8.n

        @blk.sync
        def _(sy, wsrc=wsrc, base_e8=base_e8, base_e7=base_e7, wi=wi):
            for k in range(NCH):
                if k >= 1:
                    sy.wait_ge(e8.h, base_e8 + k)
                elif wi == 0:
                    sy.wait_ge(ctx.e6.h, ctx.e6.n)
                else:
                    sy.wait_ge(e8.h, base_e8)
                    sy.wait_ge(e7.h, base_e7)
                sy.dma_start(out=ctx.WST2[:],
                             in_=wsrc[k * 128:(k + 1) * 128, :]).then_inc(q1.h, 16)
            q1.n += 16 * NCH

        @blk.vector
        def _(v, base_q1=base_q1, wi=wi):
            for k in range(NCH):
                v.wait_ge(q1.h, base_q1 + 16 * (k + 1))
                if wi > 0 or k > 0:
                    pass
                v.tensor_copy(wbuf[:, k, :, :], ctx.WST2[:]).then_inc(e8.h, 1)
            e8.n += NCH

        if act == "silu":
            @blk.tensor
            def _(t, base_e7=base_e7, dstbig=dstbig):
                t.wait_ge(e8.h, e8.n)
                t.wait_ge(e7.h, base_e7)
                for oc in range(NDT):
                    if oc >= 1:
                        t.wait_ge(e7.h, base_e7 + 2 * oc)
                    for k in range(NCH):
                        for t4 in range(4):
                            t.matmul(ps[:, (t4 % 4) * 512:(t4 % 4) * 512 + 512],
                                     wbuf[:, k, oc, :], xnv[:, k, t4 * 512:(t4 + 1) * 512],
                                     start=(k == 0), stop=(k == NCH - 1))
                    t.drain().then_inc(e7.h, 1)

            @blk.scalar
            def _(sc, base_e7=base_e7, dstbig=dstbig):
                for oc in range(NDT):
                    sc.wait_ge(e7.h, base_e7 + 2 * oc + 1)
                    for t4 in range(4):
                        sc.activation(out=dstbig[:, oc * 2048 + t4 * 512:
                                                 oc * 2048 + (t4 + 1) * 512],
                                      in_=ps[:, (t4 % 4) * 512:(t4 % 4) * 512 + 512],
                                      func=AF.Silu)
                    sc.drain().then_inc(e7.h, 1)
            e7.n += 2 * NDT
        else:
            @blk.tensor
            def _(t, base_e7=base_e7, _e4n=ctx.e4.n):
                t.wait_ge(e8.h, e8.n)
                t.wait_ge(e7.h, base_e7)
                t.wait_ge(ctx.e4.h, _e4n)
                wvk = ctx.big4.ap()[:, 0:8192].rearrange("p (k d) -> p k d", k=NCH)
                for tc in range(NCK):
                    if tc >= 1:
                        t.wait_ge(e7.h, base_e7 + 2 * tc)
                    for k in range(NCH):
                        t.matmul(ps[:, 2048:2560], xnv[:, k, tc * 128:(tc + 1) * 128],
                                 wvk[:, k, 0:512], start=(k == 0), stop=(k == NCH - 1))
                        t.matmul(ps[:, 2560:3072], xnv[:, k, tc * 128:(tc + 1) * 128],
                                 wvk[:, k, 512:1024], start=(k == 0), stop=(k == NCH - 1))
                    t.drain().then_inc(e7.h, 1)

            @blk.vector
            def _(v, base_e7=base_e7):
                for tc in range(NCK):
                    v.wait_ge(e7.h, base_e7 + 2 * tc + 1)
                    v.tensor_copy(ctx.big3[:, tc * 1024:(tc + 1) * 1024],
                                  ps[:, 2048:3072]).then_inc(e7.h, 1)
            e7.n += 2 * NCK


def emit_D(ctx, s):
    blk, ps = ctx.blk, ctx.ps
    psb = ctx.ps.ap().bitcast(BF16)
    q1 = ctx.q1
    sems = {k: getattr(ctx, k) for k in
            ["e1", "e2", "e3", "e4", "e5", "e6", "e7", "e8", "e9", "e10", "e11", "e12"]}
    B = {k: v.n for k, v in sems.items()}
    bq1 = q1.n
    Q = ctx.big1.ap().rearrange("p (d t) -> p d t", d=NDT)
    K = ctx.big2.ap().rearrange("p (d t) -> p d t", d=NDT)
    V = ctx.big3.ap().rearrange("p (c d) -> p c d", c=NCK)
    O = ctx.big4.ap().rearrange("p (d t) -> p d t", d=NDT)

    NCKD = NCK if D_NCK is None else D_NCK

    @blk.vector
    def _(v):
        v.memset(ctx.S_F[:], 0.0)
        v.memset(ctx.S_B[:], 0.0)
        v.drain().then_inc(sems["e12"].h, 1)
    sems["e12"].n += 1
    B["e12"] = sems["e12"].n - 1

    @blk.sync
    def _(sy):
        for c in range(NCKD):
            if c >= 1:
                sy.wait_ge(sems["e1"].h, B["e1"] + c)
            sy.dma_start(out=ctx.T1A[0:1, :],
                         in_=ctx.CUM[0:4, c * 128:(c + 1) * 128]).then_inc(q1.h, 16)
            sy.dma_start(out=ctx.T1A[32:33, :],
                         in_=ctx.CUM[4:8, c * 128:(c + 1) * 128]).then_inc(q1.h, 16)
            sy.dma_start(out=ctx.T1A[64:65, :],
                         in_=ctx.CUM[8:12, c * 128:(c + 1) * 128]).then_inc(q1.h, 16)
            sy.dma_start(out=ctx.T1B[0:1, :],
                         in_=ctx.CUM[12:16, c * 128:(c + 1) * 128]).then_inc(q1.h, 16)
        q1.n += 64 * NCKD

    @blk.tensor
    def _(t):
        e = sems
        for c in range(NCKD):
            if D_SUB >= 1:
                t.wait_ge(q1.h, bq1 + 64 * (c + 1))
                if c >= 1:
                    t.wait_ge(e["e7"].h, B["e7"] + c)
                    t.wait_ge(e["e2"].h, B["e2"] + c)
                pairs = [(ctx.ones_f[0:1, :], ctx.T1A[0:1, :]),
                         (ctx.ones_f[32:33, :], ctx.T1A[32:33, :]),
                         (ctx.ones_f[64:65, :], ctx.T1A[64:65, :]),
                         (ctx.ones_f[0:1, :], ctx.T1B[0:1, :])]
                tps = [(0, 0), (32, 0), (64, 0), (0, 0)]
                for q in range(4):
                    t.matmul(ps[:, 2048 + q * 512:2048 + (q + 1) * 512],
                             pairs[q][0], pairs[q][1], start=True, stop=True,
                             tile_position=tps[q])
                t.drain().then_inc(e["e1"].h, 1)
            if D_SUB >= 4:
                t.wait_ge(e["e2"].h, B["e2"] + c + 1)
                for par in range(2):
                    for h in range(par, H, 2):
                        pb = par * 64
                        dt = h // 2
                        t.matmul(ps[:, 2048 + h * 128:2048 + (h + 1) * 128],
                                 K[pb:pb + 64, dt, c * 128:(c + 1) * 128],
                                 Q[pb:pb + 64, dt, c * 128:(c + 1) * 128],
                                 start=True, stop=True,
                                 tile_position=(pb, 0))
                    t.drain()
                t.drain().then_inc(e["e4"].h, 1)
            if D_SUB >= 5:
                if c >= 1:
                    t.wait_ge(e["e9"].h, B["e9"] + c)
                for dt in range(NDT):
                    t.transpose(psb[:, 3072 + dt * 128:3072 + (dt + 1) * 128],
                                K[:, dt, c * 128:(c + 1) * 128], ctx.ident_b[:])
                t.drain().then_inc(e["e8"].h, 1)
            if D_SUB >= 7:
                t.wait_ge(e["e5"].h, B["e5"] + c + 1)
                for dt in range(NDT):
                    t.matmul(ps[:, 2048 + dt * 128:2048 + (dt + 1) * 128],
                             ctx.e16[:, dt, :], ctx.EBE[:, c * 128:(c + 1) * 128],
                             start=True, stop=True)
                t.drain().then_inc(e["e6"].h, 1)
            if D_SUB >= 10:
                t.wait_ge(e["e7"].h, B["e7"] + c + 1)
                t.wait_ge(e["e9"].h, B["e9"] + c + 1)
                t.wait_ge(e["e12"].h, B["e12"] + c + 1)
                if c >= 1:
                    t.wait_ge(e["e11"].h, B["e11"] + 2 * c)
                for h in range(H):
                    dt = h // 2
                    pb = (h % 2) * 64
                    if D_MMA:
                        t.matmul(ps[pb:pb + 64, dt * 128:(dt + 1) * 128],
                                 V[:, c, h * 64:(h + 1) * 64],
                                 ctx.A_SB[:, h * 128:(h + 1) * 128],
                                 start=True, stop=(not D_MMB), tile_position=(0, pb))
                    if D_MMB:
                        t.matmul(ps[pb:pb + 64, dt * 128:(dt + 1) * 128],
                                 ctx.S_B[pb:pb + 64, dt * 64:(dt + 1) * 64],
                                 ctx.PA[pb:pb + 64, dt, :],
                                 start=(not D_MMA), stop=True, tile_position=(pb, pb))
                for h in range(H):
                    dt = h // 2
                    pb = (h % 2) * 64
                    t.matmul(ps[pb:pb + 64, 1024 + dt * 64:1024 + (dt + 1) * 64],
                             ctx.KDEC.rearrange("p (h d) -> p h d", h=H)[:, h, :],
                             V[:, c, h * 64:(h + 1) * 64],
                             start=True, stop=True, tile_position=(0, pb))
                t.drain().then_inc(e["e10"].h, 1)
        for k, lvl in [("e1", 1), ("e4", 4), ("e8", 5), ("e6", 7), ("e10", 10)]:
            if D_SUB >= lvl:
                sems[k].n += NCKD

    @blk.scalar
    def _(sc):
        e = sems
        for c in range(NCKD):
            if D_SUB >= 3:
                sc.wait_ge(e["e2"].h, B["e2"] + c + 1)
                sc.activation(out=ctx.EXPD[:], in_=ctx.EXPD[:], func=AF.Exp)
                sc.drain().then_inc(e["e3"].h, 1)
        if D_SUB >= 3:
            sems["e3"].n += NCKD

    @blk.vector
    def _(v):
        e = sems
        for c in range(NCKD):
            if D_SUB >= 2:
                v.wait_ge(e["e1"].h, B["e1"] + c + 1)
                v.tensor_tensor(ctx.EXPD.rearrange("p (h i) -> p h i", h=H),
                                ctx.CUMT[:, c, :].unsqueeze(2).to_broadcast([128, H, 128]),
                                ps[:, 2048:4096].rearrange("p (h i) -> p h i", h=H),
                                ALU.subtract)
                v.tensor_tensor(ctx.EXPD.rearrange("p (h i) -> p h i", h=H),
                                ctx.EXPD.rearrange("p (h i) -> p h i", h=H),
                                ctx.maskadd[:].unsqueeze(1).to_broadcast([128, H, 128]),
                                ALU.add)
                v.drain().then_inc(e["e2"].h, 1)
            if D_SUB >= 6:
                v.wait_ge(e["e4"].h, B["e4"] + c + 1)
                v.wait_ge(e["e3"].h, B["e3"] + c + 1)
                if c >= 1:
                    v.wait_ge(e["e10"].h, B["e10"] + c)
                v.tensor_tensor(ctx.A_SB[:], ps[:, 2048:4096], ctx.EXPD[:],
                                ALU.mult).then_inc(e["e5"].h, 1)
            if D_SUB >= 8:
                v.wait_ge(e["e6"].h, B["e6"] + c + 1)
                for dt in range(NDT):
                    v.tensor_tensor(ctx.PA[:, dt, :], Q[:, dt, c * 128:(c + 1) * 128],
                                    ps[:, 2048 + dt * 128:2048 + (dt + 1) * 128],
                                    ALU.mult)
                v.drain().then_inc(e["e7"].h, 1)
            if D_SUB >= 9:
                v.wait_ge(e["e8"].h, B["e8"] + c + 1)
                v.tensor_tensor(ctx.KDEC.rearrange("p (h d) -> p h d", h=H),
                                psb[:, 3072:4096].rearrange("p (h d) -> p h d", h=H),
                                ctx.EDET[:, c, :].unsqueeze(2).to_broadcast([128, H, 64]),
                                ALU.mult).then_inc(e["e9"].h, 1)
            if D_SUB >= 11:
                v.wait_ge(e["e10"].h, B["e10"] + c + 1)
                v.tensor_copy(O[:, 0:4, c * 128:(c + 1) * 128], ps[:, 0:512])
                v.tensor_copy(O[:, 4:8, c * 128:(c + 1) * 128], ps[:, 512:1024])
                v.drain().then_inc(e["e11"].h, 1)
                gv = ctx.GAM2[:, c * 8:(c + 1) * 8].unsqueeze(2).to_broadcast([128, NDT, 64])
                v.tensor_tensor(ctx.S_F.rearrange("p (d v) -> p d v", d=NDT),
                                ctx.S_F.rearrange("p (d v) -> p d v", d=NDT), gv, ALU.mult)
                v.drain()
                v.tensor_tensor(ctx.S_F[:], ctx.S_F[:], ps[:, 1024:1536], ALU.add)
                v.drain().then_inc(e["e11"].h, 1)
                v.tensor_copy(ctx.S_B[:], ctx.S_F[:])
                v.drain().then_inc(e["e12"].h, 1)
        for k, amt, lvl in [("e2", NCKD, 2), ("e5", NCKD, 6), ("e7", NCKD, 8),
                            ("e9", NCKD, 9), ("e11", 2 * NCKD, 11), ("e12", NCKD, 11)]:
            if D_SUB >= lvl:
                sems[k].n += amt


def emit_E(ctx, s):
    blk, ps = ctx.blk, ctx.ps
    e1, e2, e3, q1 = ctx.e1, ctx.e2, ctx.e3, ctx.q1
    O = ctx.big4.ap().rearrange("p (d t) -> p d t", d=NDT)

    @blk.vector
    def _(v):
        for dt in range(NDT):
            od = ctx.big4.ap()[:, dt * 2048:(dt + 1) * 2048]
            v.tensor_reduce(ctx.ECOLS[:, dt:dt + 1], od, AX.X, ALU.add)
            v.tensor_tensor(ctx.EXPD[:], od, od, ALU.mult)
            v.tensor_reduce(ctx.ECOLS[:, 8 + dt:9 + dt], ctx.EXPD[:], AX.X, ALU.add)
        v.drain().then_inc(e1.h, 1)
        e1.n += 1

    @blk.tensor
    def _(t):
        t.wait_ge(e1.h, e1.n)
        t.matmul(ps[0:2, 0:16], ctx.e2t[:], ctx.ECOLS[:, 0:16],
                 start=True, stop=True).then_inc(e1.h, 1)
        e1.n += 1

    @blk.vector
    def _(v):
        v.wait_ge(e1.h, e1.n)
        st = ctx.GAMS
        NI = 1.0 / (S * DH)
        v.tensor_scalar(st[0:2, 0:16], ps[0:2, 0:16], NI, None, ALU.mult)
        v.drain()
        v.tensor_tensor(ctx.CB[0:2, 0:8], st[0:2, 0:8], st[0:2, 0:8], ALU.mult)
        v.drain()
        v.tensor_tensor(st[0:2, 8:16], st[0:2, 8:16], ctx.CB[0:2, 0:8], ALU.subtract)
        v.drain().then_inc(e2.h, 1)
        e2.n += 1

    @blk.scalar
    def _(sc):
        sc.wait_ge(e2.h, e2.n)
        sc.activation(out=ctx.GAMS[0:2, 8:16], in_=ctx.GAMS[0:2, 8:16], func=AF.Ln,
                      bias=ctx.cst[0:2, 0:1])
        sc.drain()
        sc.activation(out=ctx.GAMS[0:2, 8:16], in_=ctx.GAMS[0:2, 8:16], func=AF.Exp,
                      scale=-0.5)
        sc.drain().then_inc(e2.h, 1)
        e2.n += 1

    @blk.tensor
    def _(t):
        t.wait_ge(e2.h, e2.n)
        t.matmul(ps[0:128, 512:520], ctx.exp2_f[:], ctx.GAMS[0:2, 0:8],
                 start=True, stop=True)
        t.matmul(ps[0:128, 520:528], ctx.exp2_f[:], ctx.GAMS[0:2, 8:16],
                 start=True, stop=True).then_inc(e2.h, 1)
        e2.n += 1

    @blk.vector
    def _(v):
        v.wait_ge(e2.h, e2.n)
        v.tensor_tensor(ctx.ecol[:], ps[0:128, 520:528], ctx.gnw_t[:], ALU.mult)
        v.drain()
        v.tensor_tensor(ctx.ecol2[:], ps[0:128, 512:520], ctx.ecol[:], ALU.mult)
        v.drain()
        v.tensor_tensor(ctx.ecol2[:], ctx.gnb_t[:], ctx.ecol2[:], ALU.subtract)
        v.drain().then_inc(e3.h, 1)
        e3.n += 1

    @blk.vector
    def _(v):
        v.wait_ge(e3.h, e3.n)
        for dt in range(NDT):
            od = ctx.big4.ap()[:, dt * 2048:(dt + 1) * 2048]
            v.tensor_scalar(od, od, ctx.ecol[:, dt:dt + 1], ctx.ecol2[:, dt:dt + 1],
                            ALU.mult, ALU.add)
        v.drain().then_inc(e3.h, 1)
        e3.n += 1

    wbufe = ctx.big2.ap()[:, 0:8192].rearrange("p (k o c) -> p k o c", k=NCH, o=NDT)
    xnv = ctx.big0.ap().rearrange("p (c t) -> p c t", c=NCH)
    bq1 = q1.n
    be4 = ctx.e4.n

    @blk.sync
    def _(sy):
        for k in range(NCH):
            if k >= 1:
                sy.wait_ge(ctx.e4.h, be4 + k)
            else:
                sy.wait_ge(e3.h, e3.n)
            sy.dma_start(out=ctx.WST2f[:],
                         in_=ctx.wga[k * 128:(k + 1) * 128, :]).then_inc(q1.h, 16)
        q1.n += 16 * NCH

    @blk.vector
    def _(v):
        for k in range(NCH):
            v.wait_ge(q1.h, bq1 + 16 * (k + 1))
            v.tensor_copy(wbufe[:, k, :, :], ctx.WST2f[:]).then_inc(ctx.e4.h, 1)
        ctx.e4.n += NCH

    be5 = ctx.e5.n

    @blk.tensor
    def _(t):
        t.wait_ge(ctx.e4.h, ctx.e4.n)
        for oc in range(NDT):
            for t4 in range(4):
                if oc * 4 + t4 >= 2:
                    t.wait_ge(ctx.e5.h, be5 + 2 * (oc * 4 + t4) - 2)
                for k in range(NCH):
                    t.matmul(ps[:, ((oc * 4 + t4) % 2) * 512:
                                ((oc * 4 + t4) % 2) * 512 + 512],
                             wbufe[:, k, oc, :], xnv[:, k, t4 * 512:(t4 + 1) * 512],
                             start=(k == 0), stop=(k == NCH - 1))
                t.drain().then_inc(ctx.e5.h, 1)
        ctx.e5.n += 32

    be6 = ctx.e6.n

    @blk.scalar
    def _(sc):
        for i in range(32):
            sc.wait_ge(ctx.e5.h, be5 + 2 * i + 1)
            if i >= 1:
                sc.wait_ge(ctx.e6.h, be6 + i)
            sc.activation(out=ctx.GT[:], in_=ps[:, (i % 2) * 512:(i % 2) * 512 + 512],
                          func=AF.Sigmoid)
            sc.drain().then_inc(ctx.e5.h, 1)
        ctx.e5.n += 32

    @blk.vector
    def _(v):
        for i in range(32):
            oc, t4 = i // 4, i % 4
            v.wait_ge(ctx.e5.h, be5 + 2 * i + 2)
            sl = slice(oc * 2048 + t4 * 512, oc * 2048 + (t4 + 1) * 512)
            v.tensor_tensor(ctx.big1.ap()[:, sl], ctx.big4.ap()[:, sl], ctx.GT[:],
                            ALU.mult).then_inc(ctx.e6.h, 1)
        ctx.e6.n += 32


def emit_F(ctx, s):
    blk, ps = ctx.blk, ctx.ps
    q1, q3 = ctx.q1, ctx.q3
    f1, f2, f3, f4 = ctx.f1, ctx.f2, ctx.f3, ctx.f4
    e1 = ctx.e1
    xnv = ctx.big0.ap().rearrange("p (c t) -> p c t", c=NCH)
    W1 = ctx.big3.ap().rearrange("p (k f) -> p k f", k=NCH)
    W2 = ctx.big4.ap().rearrange("p (f d) -> p f d", f=NCK)
    gfm = ctx.G_FM
    acc = ctx.big2.ap()
    bf1, bf2, bf3, bf4 = f1.n, f2.n, f3.n, f4.n

    def w_chunks(e):
        w1f = [ctx.w1[e][(k // 2) * 128:(k // 2 + 1) * 128,
                         (k % 2) * 1024:(k % 2 + 1) * 1024] for k in range(16)]
        w1b = [ctx.w1_b[e][(k // 2) * 128:(k // 2 + 1) * 128,
                           (k % 2) * 1024:(k % 2 + 1) * 1024] for k in range(16)]
        w1d = [W1[:, k // 2, (k % 2) * 1024:(k % 2 + 1) * 1024] for k in range(16)]
        w2f = [ctx.w2[e][k * 128:(k + 1) * 128, :] for k in range(NCK)]
        w2b = [ctx.w2_b[e][k * 128:(k + 1) * 128, :] for k in range(NCK)]
        w2d = [W2[:, k, :] for k in range(NCK)]
        return w1f, w1b, w1d, w2f, w2b, w2d

    fw = ctx.fw
    bfw = fw.n

    for e in range(E):
        w1f, w1b, w1d, w2f, w2b, w2d = w_chunks(e)
        if s == 0:
            @blk.gpsimd
            def _(g, e=e, w1f=w1f, w2f=w2f, w1b=w1b, w2b=w2b, w1d=w1d, w2d=w2d):
                bq = bfw + 16 * 32 * e
                g.wait_ge(f1.h, bf1 + 16 * e)
                for k in range(16):
                    g.dma_start(out=w1d[k], in_=w1f[k]).then_inc(fw.h, 16)
                g.wait_ge(f3.h, bf3 + 16 * e)
                for k in range(NCK):
                    g.dma_start(out=w2d[k], in_=w2f[k]).then_inc(fw.h, 16)
                for k in range(32):
                    g.wait_ge(fw.h, bq + 16 * (k + 1))
                    src = w1d[k] if k < 16 else w2d[k - 16]
                    dstc = w1b[k] if k < 16 else w2b[k - 16]
                    g.dma_start(out=dstc, in_=src).then_inc(q3.h, 16)
                q3.n += 16 * 32
            fw.n += 16 * 32
        else:
            @blk.sync
            def _(sy, e=e, w1b=w1b, w2b=w2b, w1d=w1d, w2d=w2d):
                for k in range(32):
                    if k == 0:
                        sy.wait_ge(f1.h, bf1 + 16 * e)
                    if k == 16:
                        sy.wait_ge(f3.h, bf3 + 16 * e)
                    dst = w1d[k] if k < 16 else w2d[k - 16]
                    src = w1b[k] if k < 16 else w2b[k - 16]
                    sy.dma_start(out=dst, in_=src).then_inc(fw.h, 16)
                q1.n += 0
            fw.n += 16 * 32

        @blk.tensor
        def _(t, e=e):
            t.wait_ge(fw.h, bfw + 512 * e + 256)
            for t4 in range(4):
                for j in range(4):
                    qidx = (e * 4 + t4) * 4 + j
                    hb = (j + 1) % 2
                    if qidx >= 2:
                        t.wait_ge(f2.h, bf2 + qidx - 1)
                    if j == 1:
                        t.wait_ge(f4.h, bf4 + 4 * (e * 4 + t4))
                    for i in range(4):
                        fcg = j * 4 + i
                        for k in range(NCH):
                            t.matmul(ps[:, hb * 2048 + i * 512:hb * 2048 + (i + 1) * 512],
                                     W1[:, k, fcg * 128:(fcg + 1) * 128],
                                     xnv[:, k, t4 * 512:(t4 + 1) * 512],
                                     start=(k == 0), stop=(k == NCH - 1))
                    t.drain().then_inc(f1.h, 1)
                t.wait_ge(f2.h, bf2 + (e * 4 + t4) * 4 + 4)
                t.wait_ge(fw.h, bfw + 512 * (e + 1))
                for ts in range(4):
                    tc = t4 * 4 + ts
                    i2 = e * 16 + tc
                    if i2 >= 2:
                        t.wait_ge(f4.h, bf4 + i2 - 1)
                    pbase = (tc % 2) * 1024
                    for fc in range(NCK):
                        t.matmul(ps[:, pbase:pbase + 512],
                                 gfm[:, fc, ts * 128:(ts + 1) * 128],
                                 W2[:, fc, 0:512], start=(fc == 0), stop=(fc == NCK - 1))
                        t.matmul(ps[:, pbase + 512:pbase + 1024],
                                 gfm[:, fc, ts * 128:(ts + 1) * 128],
                                 W2[:, fc, 512:1024], start=(fc == 0), stop=(fc == NCK - 1))
                    t.drain().then_inc(f3.h, 1)

        @blk.scalar
        def _(sc, e=e):
            for t4 in range(4):
                for j in range(4):
                    qidx = (e * 4 + t4) * 4 + j
                    hb = (j + 1) % 2
                    sc.wait_ge(f1.h, bf1 + qidx + 1)
                    if e * 4 + t4 >= 1 and j == 0:
                        sc.wait_ge(f3.h, bf3 + 4 * (e * 4 + t4))
                    for i in range(4):
                        sc.activation(out=gfm[:, j * 4 + i, :],
                                      in_=ps[:, hb * 2048 + i * 512:
                                             hb * 2048 + (i + 1) * 512], func=AF.Gelu)
                    sc.drain().then_inc(f2.h, 1)

        @blk.vector
        def _(v, e=e):
            for t4 in range(4):
                for ts in range(4):
                    tc = t4 * 4 + ts
                    i2 = e * 16 + tc
                    v.wait_ge(f3.h, bf3 + i2 + 1)
                    pbase = (tc % 2) * 1024
                    gate = ctx.gates_own[:, e, s * NCK + tc:s * NCK + tc + 1]
                    asl = acc[:, tc * 1024:(tc + 1) * 1024]
                    if e == 0:
                        v.tensor_scalar(asl[:, 0:512], ps[:, pbase:pbase + 512],
                                        gate, None, ALU.mult)
                        v.tensor_scalar(asl[:, 512:1024], ps[:, pbase + 512:pbase + 1024],
                                        gate, None, ALU.mult).then_inc(f4.h, 1)
                    else:
                        v.scalar_tensor_tensor(asl[:, 0:512], ps[:, pbase:pbase + 512],
                                               gate, asl[:, 0:512], ALU.mult, ALU.add)
                        v.scalar_tensor_tensor(asl[:, 512:1024],
                                               ps[:, pbase + 512:pbase + 1024],
                                               gate, asl[:, 512:1024],
                                               ALU.mult, ALU.add).then_inc(f4.h, 1)
        f1.n += 16
        f2.n += 16
        f3.n += 16
        f4.n += 16


def emit_G(ctx, s):
    blk, ps = ctx.blk, ctx.ps
    q1, q2, q3 = ctx.q1, ctx.q2, ctx.q3
    g1, g2, g3 = ctx.e7, ctx.e8, ctx.e9
    og = ctx.big1.ap().rearrange("p (k t) -> p k t", k=NCH)
    acc = ctx.big2.ap()
    wbufg = ctx.big3.ap()[:, 0:8192].rearrange("p (k d) -> p k d", k=NCH)
    bg1, bg2, bg3 = g1.n, g2.n, g3.n
    bq1, bq2, bq3 = q1.n, q2.n, q3.n
    xo = ctx.x_own
    outv = ctx.out

    @blk.sync
    def _(sy):
        for k in range(NCH):
            if k >= 1:
                sy.wait_ge(g1.h, bg1 + k)
            sy.dma_start(out=ctx.WST2f[:],
                         in_=ctx.wo[k * 128:(k + 1) * 128, :]).then_inc(q1.h, 16)
        q1.n += 16 * NCH

    @blk.vector
    def _(v):
        for k in range(NCH):
            v.wait_ge(q1.h, bq1 + 16 * (k + 1))
            v.tensor_copy(wbufg[:, k, :], ctx.WST2f[:]).then_inc(g1.h, 1)
        g1.n += NCH

    @blk.scalar
    def _(sc):
        for tc in range(NCK):
            if tc >= 2:
                sc.wait_ge(g3.h, bg3 + tc - 1)
            sc.dma_start(out=ctx.XG[:, tc % 2],
                         in_=xo[s * S + tc * 128:s * S + (tc + 1) * 128, :]
                         ).then_inc(q2.h, 16)
        q2.n += 16 * NCK

    @blk.tensor
    def _(t):
        t.wait_ge(g1.h, g1.n)
        for tc in range(NCK):
            if tc >= 2:
                t.wait_ge(g3.h, bg3 + tc - 1)
            pbase = (tc % 2) * 1024
            for k in range(NCH):
                t.matmul(ps[:, pbase:pbase + 512], og[:, k, tc * 128:(tc + 1) * 128],
                         wbufg[:, k, 0:512], start=(k == 0), stop=(k == NCH - 1))
                t.matmul(ps[:, pbase + 512:pbase + 1024],
                         og[:, k, tc * 128:(tc + 1) * 128],
                         wbufg[:, k, 512:1024], start=(k == 0), stop=(k == NCH - 1))
            t.drain().then_inc(g2.h, 1)
        g2.n += NCK

    @blk.vector
    def _(v):
        for tc in range(NCK):
            v.wait_ge(g2.h, bg2 + tc + 1)
            v.wait_ge(q2.h, bq2 + 16 * (tc + 1))
            if tc >= 2:
                v.wait_ge(q3.h, bq3 + 16 * (tc - 1))
            pbase = (tc % 2) * 1024
            o1 = ctx.OUTST[:, tc % 2]
            for hf in range(2):
                v.tensor_tensor(o1[:, hf * 512:(hf + 1) * 512],
                                ps[:, pbase + hf * 512:pbase + (hf + 1) * 512],
                                ctx.XG[:, tc % 2][:, hf * 512:(hf + 1) * 512], ALU.add)
                v.tensor_tensor(o1[:, hf * 512:(hf + 1) * 512],
                                o1[:, hf * 512:(hf + 1) * 512],
                                acc[:, tc * 1024 + hf * 512:tc * 1024 + (hf + 1) * 512],
                                ALU.add)
            v.drain().then_inc(g3.h, 1)
        g3.n += NCK

    @blk.gpsimd
    def _(g):
        for tc in range(NCK):
            g.wait_ge(g3.h, bg3 + tc + 1)
            g.dma_start(out=outv[s * S + tc * 128:s * S + (tc + 1) * 128, :],
                        in_=ctx.OUTST[:, tc % 2]).then_inc(q3.h, 16)
        q3.n += 16 * NCK


def kernel(L0_attn_x, L0_attn_Wq, L0_attn_Wk, L0_attn_Wv, L0_attn_Wg,
           L0_attn_Wa, L0_attn_ba, L0_attn_Wo, L0_attn_gn_w, L0_attn_gn_b,
           L0_ffn_Wg, L0_ffn_W1, L0_ffn_W2):
    x = np.ascontiguousarray(np.asarray(L0_attn_x, dtype=np.float32)).reshape(T, DM)
    base = {
        "x_all": x,
        "wq": np.ascontiguousarray(L0_attn_Wq, dtype=np.float32),
        "wk": np.ascontiguousarray(L0_attn_Wk, dtype=np.float32),
        "wv": np.ascontiguousarray(L0_attn_Wv, dtype=np.float32),
        "wga": np.ascontiguousarray(L0_attn_Wg, dtype=np.float32),
        "wa": np.ascontiguousarray(L0_attn_Wa, dtype=np.float32),
        "ba": np.ascontiguousarray(L0_attn_ba, dtype=np.float32),
        "wo": np.ascontiguousarray(L0_attn_Wo, dtype=np.float32),
        "gnw": np.ascontiguousarray(L0_attn_gn_w, dtype=np.float32),
        "gnb": np.ascontiguousarray(L0_attn_gn_b, dtype=np.float32),
        "wgf": np.ascontiguousarray(L0_ffn_Wg, dtype=np.float32),
        "w1": np.ascontiguousarray(L0_ffn_W1, dtype=np.float32),
        "w2": np.ascontiguousarray(L0_ffn_W2, dtype=np.float32),
    }
    in_maps = []
    for c in range(NC_):
        m = dict(base)
        m["x_own"] = np.ascontiguousarray(x[c * TOWN:(c + 1) * TOWN])
        m["csel"] = np.array([c], dtype=np.int32)
        in_maps.append(m)
    nc, _ctx = build_nc()
    res = run_bass_kernel_spmd(nc, in_maps, core_ids=list(range(NC_)))
    out = np.concatenate([r["out"] for r in res.results], axis=0)
    return out.reshape(B, S, DM)


# version 08 (final)
# speedup vs baseline: 1.4001x; 1.1059x over previous
import contextlib
import numpy as np
import ml_dtypes
import concourse.bass as bass
import concourse.mybir as mybir
from concourse.bass_utils import run_bass_kernel_spmd

F32 = mybir.dt.float32
BF16 = mybir.dt.bfloat16
I32 = mybir.dt.int32
AF = mybir.ActivationFunctionType
ALU = mybir.AluOpType
AX = mybir.AxisListType
bf16 = ml_dtypes.bfloat16
ds = bass.ds

NC_ = 8
B, S, DM = 16, 2048, 1024
H, DH, DI, FF, E = 16, 64, 1024, 2048, 8
T = B * S
TOWN = T // NC_
NSEQ = 2
NCH = DM // 128
NDT = DI // 128
NBLK = T // 128
NOWN = TOWN // 128
NCK = S // 128
CAP = T // E
EPS = 1e-5
NITER = 40
MEGA = E + H + 1

DEBUG_OUTS = []
STAGES = {"0", "A", "B", "P", "S"}
SEQ_PHASES = "DEFG"
D_MMB = True
D_MMA = True
D_NCK = None
D_SUB = 99
P_PARTS = {1, 2, 3}
DEBUG_BIS = False
B_PARTS = 3
B_NT = None
B_PLAIN = False


class Sem:
    __slots__ = ("h", "n")

    def __init__(self, es, nc, name):
        self.h = es.enter_context(nc.semaphore(name))
        self.n = 0


class Ctx:
    pass


def _sb(ctx, name, shape, dt):
    t = ctx.es.enter_context(ctx.nc.sbuf_tensor(name, list(shape), dt))
    setattr(ctx, name, t)
    return t


def build_nc():
    nc = bass.Bass("TRN2", target_bir_lowering=False, detect_race_conditions=False)
    ctx = Ctx()
    ctx.nc = nc
    es = contextlib.ExitStack()
    ctx.es = es

    def din(name, shape, dt=F32):
        ap = nc.dram_tensor(name, list(shape), dt, kind="ExternalInput").ap()
        setattr(ctx, name, ap)

    din("x_own", [TOWN, DM])
    din("x_all", [T, DM])
    din("wq", [DM, DI]); din("wk", [DM, DI]); din("wv", [DM, DI])
    din("wga", [DM, DI]); din("wa", [DM, H]); din("ba", [H])
    din("wo", [DI, DM]); din("gnw", [DI]); din("gnb", [DI])
    din("wgf", [DM, E]); din("w1", [E, DM, FF]); din("w2", [E, FF, DM])
    din("csel", [1], I32)
    ctx.out = nc.dram_tensor("out", [TOWN, DM], F32, kind="ExternalOutput").ap()
    for name, shape, dt, _fn in DEBUG_OUTS:
        setattr(ctx, "dbg_" + name,
                nc.dram_tensor("dbg_" + name, list(shape), dt, kind="ExternalOutput").ap())

    def scr(name, shape, dt):
        ap = nc.dram_tensor(name, list(shape), dt, kind="Internal").ap()
        setattr(ctx, name, ap)

    scr("xhiT_own", [DM, TOWN], BF16)
    scr("w1_b", [E, DM, FF], BF16); scr("w2_b", [E, FF, DM], BF16)
    scr("msrow_d", [2, TOWN], F32)

    ctx.ps = es.enter_context(nc.psum_tensor("ps", [128, 4096], F32))

    _sb(ctx, "ones_f", [128, 128], F32)
    _sb(ctx, "ones_b", [128, 128], BF16)
    _sb(ctx, "ident_f", [128, 128], F32)
    _sb(ctx, "ident_b", [128, 128], BF16)
    _sb(ctx, "maskadd", [128, 128], F32)
    _sb(ctx, "exp2_f", [2, 128], F32)
    _sb(ctx, "exp2_b", [2, 128], BF16)
    _sb(ctx, "mega_hi", [128, NCH, 32], BF16)
    _sb(ctx, "mega_lo", [128, NCH, 32], BF16)
    _sb(ctx, "corr", [128, MEGA], F32)
    _sb(ctx, "ba_t", [128, H], F32)
    _sb(ctx, "gnw_t", [128, NDT], F32)
    _sb(ctx, "gnb_t", [128, NDT], F32)
    _sb(ctx, "csel_sb", [1, 1], I32)
    _sb(ctx, "cst", [128, 4], F32)
    _sb(ctx, "e16", [16, NDT, 128], BF16)
    _sb(ctx, "e2t", [128, 2], F32)
    _sb(ctx, "ecol", [128, 8], F32)
    _sb(ctx, "ecol2", [128, 8], F32)
    _sb(ctx, "tiny", [2, 4, 64], F32)

    _sb(ctx, "gates_own", [128, E, NOWN], F32)
    _sb(ctx, "awa_own", [128, H, NOWN], F32)
    _sb(ctx, "tau", [128, E], F32)
    _sb(ctx, "m_own", [128, NOWN], F32)
    _sb(ctx, "sinv_own", [128, NOWN], F32)
    _sb(ctx, "bis", [1, 32], F32)
    _sb(ctx, "bis2", [128, 4, E], F32)
    _sb(ctx, "cnt8", [128, E], F32)

    _sb(ctx, "big0", [128, 16384], BF16)
    _sb(ctx, "big1", [128, 16384], BF16)
    _sb(ctx, "big2", [128, 16384], BF16)
    _sb(ctx, "big3", [128, 16384], BF16)
    _sb(ctx, "big4", [128, 16384], BF16)
    _sb(ctx, "arena", [128, 18528], BF16)

    b0f = ctx.big0.ap().bitcast(F32)
    b1f = ctx.big1.ap().bitcast(F32)
    b2f = ctx.big2.ap().bitcast(F32)
    arf = ctx.arena.ap().bitcast(F32)
    arb = ctx.arena.ap()
    ctx.ALL = b0f[:, 0:MEGA * NBLK].rearrange("p (a b) -> p a b", a=MEGA)
    ctx.cmp_f = b1f[:, 0:E * NBLK].rearrange("p (a b) -> p a b", a=E)
    ctx.gates_all = b1f[:, E * NBLK:2 * E * NBLK].rearrange("p (a b) -> p a b", a=E)
    ctx.val_mx = b1f[:, 2 * E * NBLK:2 * E * NBLK + 4 * NBLK].rearrange(
        "p (a b) -> p a b", a=4)
    ctx.sumsq = b2f[:, 0:NBLK]
    ctx.m_all = b2f[:, NBLK:2 * NBLK]
    ctx.sinv_all = b2f[:, 2 * NBLK:3 * NBLK]
    ctx.ALLo = b2f[:, 3 * NBLK:3 * NBLK + MEGA * NOWN].rearrange(
        "p (a b) -> p a b", a=MEGA)
    ctx.sumsq_o = b2f[:, 3 * NBLK + MEGA * NOWN:3 * NBLK + MEGA * NOWN + NOWN]
    ctx.m_allo = b2f[:, 3 * NBLK + MEGA * NOWN + NOWN:3 * NBLK + MEGA * NOWN + 2 * NOWN]
    ctx.sinv_allo = b2f[:, 3 * NBLK + MEGA * NOWN + 2 * NOWN:3 * NBLK + MEGA * NOWN + 3 * NOWN]
    ctx.cmp_o = b1f[:, 2 * E * NBLK + 4 * NBLK:2 * E * NBLK + 4 * NBLK + E * NOWN].rearrange(
        "p (a b) -> p a b", a=E)
    ctx.val_mxo = b1f[:, 2 * E * NBLK + 4 * NBLK + E * NOWN:
                      2 * E * NBLK + 4 * NBLK + E * NOWN + 4 * NOWN].rearrange(
        "p (a b) -> p a b", a=4)
    ctx.xstage = arf[:, 0:4096].rearrange("p (a b) -> p a b", a=4)
    ctx.xhi_st = arb[:, 8192:12288].rearrange("p (a b) -> p a b", a=4)
    ctx.xfmstage = arb[:, 12288:16384].rearrange("p (a b) -> p a b", a=NCH)
    ctx.wstage = arf[:, 4096:5120]
    ctx.ost = arf[:, 8192:8704][0:MEGA, :]

    ctx.XFM = arb[:, 0:1024].rearrange("p (a b) -> p a b", a=2)
    ctx.LN_TMP = arf[:, 512:1024]
    ctx.MSLm = arf[0:1, 1024:2048]
    ctx.MSLs = arf[0:1, 2048:3072]
    ctx.WST2 = arf[:, 1024:2048]
    ctx.CUM = arf[0:16, 3072:5120]
    ctx.EBUF = arf[0:16, 5120:7168]
    ctx.EBE = arb[0:16, 28672 // 2:32768 // 2]
    ctx.CUMT = arf[:, 8192:8448].rearrange("p (a b) -> p a b", a=NCK)
    ctx.EDET = arf[:, 8448:8704].rearrange("p (a b) -> p a b", a=NCK)
    ctx.GAM2 = arf[:, 8704:8832]
    ctx.G2 = arf[0:2, 8832:8960]
    ctx.GAMST = arf[0:16, 6144:6160]
    ctx.SP_T = arf[:, 8960:9216].rearrange("p (a b) -> p a b", a=H)
    ctx.CB = arf[0:16, 9216:9232]
    ctx.CE = arf[0:16, 9232:9248]
    ctx.GAMS = arf[0:16, 9248:9264]
    ctx.GFLAT = arf[0:1, 6144:6400]
    ctx.EXPD = arb[:, 0:2048]
    ctx.A_SB = arb[:, 2048:4096]
    ctx.KDEC = arb[:, 4096:5120]
    ctx.S_B = arb[:, 5120:5632]
    ctx.S_F = arf[:, 5120:5632]
    ctx.PA = arb[:, 12288:13312].rearrange("p (a b) -> p a b", a=NDT)
    ctx.ECOLS = arf[:, 6912:6944]
    ctx.GT = arb[:, 13312:13824]
    ctx.G_FM = arb[:, 0:8192].rearrange("p (a b) -> p a b", a=16)
    ctx.WST2f = arf[:, 4096:5120]
    ctx.XG = arf[:, 0:2048].rearrange("p (a b) -> p a b", a=2)
    ctx.OUTST = arf[:, 2048:4096].rearrange("p (a b) -> p a b", a=2)
    ctx.T1A = arf[:, 5632:6144]
    ctx.T1B = arf[:, 6656:7168]

    ctx.sems = {}
    for s in ["dS", "dG", "dT", "dW", "dO", "cA", "cB", "cC", "cD", "cE", "cF", "BAR",
              "q1", "q2", "q3", "e1", "e2", "e3", "e4", "e5", "e6", "e7", "e8",
              "e9", "e10", "e11", "e12", "f1", "f2", "f3", "f4", "fw",
              "a1", "a2", "a3", "a4", "a5", "a6", "a7"]:
        setattr(ctx, s, Sem(es, nc, s))
    ctx.bar_n = 0

    with nc.Block() as blk:
        ctx.blk = blk
        emit_all(ctx)

    es.close()
    return nc, ctx


def barrier(ctx):
    blk = ctx.blk
    ctx.bar_n += 1
    target = 5 * ctx.bar_n
    bar = ctx.BAR.h
    for name in ["sync", "tensor", "vector", "scalar", "gpsimd"]:
        def f(eng):
            eng.drain().then_inc(bar, 1)
            eng.wait_ge(bar, target)
        getattr(blk, name)(f)


def emit_all(ctx):
    nc = ctx.nc
    if "0" in STAGES:
        with nc.named_scope("s0"):
            emit_stage0(ctx)
            barrier(ctx)
            emit_stage0b(ctx)
    if "A" in STAGES:
        with nc.named_scope("sA"):
            emit_stageA(ctx)
    if "B" in STAGES:
        with nc.named_scope("sB"):
            emit_stageB(ctx)
            barrier(ctx)
    if "P" in STAGES:
        with nc.named_scope("sP"):
            emit_post_routing(ctx)
            barrier(ctx)
    if "S" in STAGES:
        with nc.named_scope("sM"):
            emit_msrow(ctx)
            barrier(ctx)
        for s in range(NSEQ):
            emit_seq(ctx, s)
    for name, _s, _d, fn in DEBUG_OUTS:
        emit_debug_out(ctx, name, fn)
    finalize(ctx)


def emit_debug_out(ctx, name, fn):
    blk = ctx.blk
    ap = fn(ctx)

    @blk.sync
    def _(sy):
        sy.dma_start(out=getattr(ctx, "dbg_" + name), in_=ap).then_inc(ctx.dO.h, 16)
    ctx.dO.n += 16


def finalize(ctx):
    blk = ctx.blk

    @blk.sync
    def _(sy):
        if ctx.dO.n:
            sy.wait_ge(ctx.dO.h, ctx.dO.n)

    @blk.gpsimd
    def _(g):
        g.wait_ge(ctx.q3.h, ctx.q3.n)


def emit_stage0(ctx):
    blk = ctx.blk
    dW, dS = ctx.dW, ctx.dS

    @blk.gpsimd
    def _(g):
        g.memset(ctx.ones_f[:], 1.0)
        g.memset(ctx.ones_b[:], 1.0)
        g.memset(ctx.cst[:, 0:1], EPS)
        g.memset(ctx.cst[:, 1:2], 1.0)
        g.memset(ctx.cst[:, 2:3], 0.0)
        g.affine_select(ctx.ident_f[:], ctx.ones_f[:], compare_op=ALU.is_equal,
                        base=0, pattern=[[1, 128]], channel_multiplier=-1, fill=0.0)
        g.affine_select(ctx.ident_b[:], ctx.ones_b[:], compare_op=ALU.is_equal,
                        base=0, pattern=[[1, 128]], channel_multiplier=-1, fill=0.0)
        g.memset(ctx.maskadd[:], 0.0)
        g.affine_select(ctx.maskadd[:], ctx.maskadd[:], compare_op=ALU.is_ge,
                        base=0, pattern=[[1, 128]], channel_multiplier=-1, fill=-1e30)
        g.memset(ctx.exp2_f[:], 1.0)
        g.affine_select(ctx.exp2_f[:], ctx.exp2_f[:], compare_op=ALU.is_ge,
                        base=0, pattern=[[1, 128]], channel_multiplier=-64, fill=0.0)
        g.affine_select(ctx.exp2_f[:], ctx.exp2_f[:], compare_op=ALU.is_ge,
                        base=63, pattern=[[-1, 128]], channel_multiplier=64, fill=0.0)
        g.tensor_copy(ctx.exp2_b[:], ctx.exp2_f[:])
        g.memset(ctx.e16[:], 1.0)
        for dt in range(NDT):
            g.affine_select(ctx.e16[:, dt, :], ctx.e16[:, dt, :], compare_op=ALU.is_ge,
                            base=128 * dt, pattern=[[1, 128]], channel_multiplier=-64,
                            fill=0.0)
            g.affine_select(ctx.e16[:, dt, :], ctx.e16[:, dt, :], compare_op=ALU.is_ge,
                            base=63 - 128 * dt, pattern=[[-1, 128]], channel_multiplier=64,
                            fill=0.0)
        g.memset(ctx.e2t[:], 1.0)
        g.affine_select(ctx.e2t[:], ctx.e2t[:], compare_op=ALU.is_ge,
                        base=0, pattern=[[-64, 2]], channel_multiplier=1, fill=0.0)
        g.affine_select(ctx.e2t[:], ctx.e2t[:], compare_op=ALU.is_ge,
                        base=63, pattern=[[64, 2]], channel_multiplier=-1, fill=0.0)

    @blk.sync
    def _(sy):
        sy.dma_start(out=ctx.wstage[:, 0:NCH * E],
                     in_=ctx.wgf.rearrange("(c p) e -> p c e", p=128)).then_inc(dS.h, 16)
        sy.dma_start(out=ctx.wstage[:, NCH * E:NCH * E + NCH * H],
                     in_=ctx.wa.rearrange("(c p) h -> p c h", p=128)).then_inc(dS.h, 16)
        sy.dma_start(out=ctx.tiny[0:1, 0, 0:H], in_=ctx.ba.unsqueeze(0)).then_inc(dS.h, 16)
        sy.dma_start(out=ctx.gnw_t[:],
                     in_=ctx.gnw.rearrange("(c p) -> p c", p=128),
                     allow_slow_non_contiguous=True).then_inc(dS.h, 16)
        sy.dma_start(out=ctx.gnb_t[:],
                     in_=ctx.gnb.rearrange("(c p) -> p c", p=128),
                     allow_slow_non_contiguous=True).then_inc(dS.h, 16)
        sy.dma_start(out=ctx.csel_sb[:], in_=ctx.csel.unsqueeze(0)).then_inc(dS.h, 16)
        dS.n += 16 * 6
        sy.wait_ge(dS.h, dS.n)


def emit_stage0b(ctx):
    blk, ps = ctx.blk, ctx.ps
    cB, cC = ctx.cB, ctx.cC

    @blk.vector
    def _(v):
        v.memset(ctx.mega_hi[:], 0.0)
        v.memset(ctx.mega_lo[:], 0.0)
        for c in range(NCH):
            src = ctx.wstage[:, c * E:(c + 1) * E]
            wa_src = ctx.wstage[:, NCH * E + c * H:NCH * E + (c + 1) * H]
            v.tensor_copy(ctx.mega_hi[:, c, 0:E], src)
            v.tensor_copy(ctx.mega_hi[:, c, E:E + H], wa_src)
            v.tensor_copy(ctx.mega_hi[:, c, E + H:MEGA], ctx.ones_f[:, 0:1])
            v.drain()
            v.scalar_tensor_tensor(ctx.mega_lo[:, c, 0:E], ctx.mega_hi[:, c, 0:E],
                                   -1.0, src, ALU.mult, ALU.add)
            v.scalar_tensor_tensor(ctx.mega_lo[:, c, E:E + H], ctx.mega_hi[:, c, E:E + H],
                                   -1.0, wa_src, ALU.mult, ALU.add)
        v.drain().then_inc(cB.h, 1)
        cB.n += 1

    @blk.tensor
    def _(t):
        for c in range(NCH):
            t.matmul(ps[0:1, 0:E], ctx.ones_f[:, 0:1], ctx.wstage[:, c * E:(c + 1) * E],
                     start=(c == 0), stop=(c == NCH - 1))
        for c in range(NCH):
            t.matmul(ps[0:1, 32:32 + H], ctx.ones_f[:, 0:1],
                     ctx.wstage[:, NCH * E + c * H:NCH * E + (c + 1) * H],
                     start=(c == 0), stop=(c == NCH - 1))
        t.drain().then_inc(cC.h, 1)
        cC.n += 1

    @blk.vector
    def _(v):
        v.wait_ge(cC.h, cC.n)
        v.tensor_copy(ctx.tiny[0:1, 1, 0:E], ps[0:1, 0:E])
        v.tensor_copy(ctx.tiny[0:1, 1, E:E + H], ps[0:1, 32:32 + H])
        v.memset(ctx.tiny[0:1, 1, E + H:MEGA], float(DM))
        v.drain().then_inc(cC.h, 1)
        cC.n += 1

    @blk.tensor
    def _(t):
        t.wait_ge(cC.h, cC.n)
        t.matmul(ps[:, 0:32][:, 0:MEGA], ctx.ones_f[0:1, :], ctx.tiny[0:1, 1, 0:MEGA],
                 start=True, stop=True)
        t.matmul(ps[:, 32:32 + H], ctx.ones_f[0:1, :], ctx.tiny[0:1, 0, 0:H],
                 start=True, stop=True)
        t.drain().then_inc(cC.h, 1)
        cC.n += 1

    @blk.vector
    def _(v):
        v.wait_ge(cC.h, cC.n)
        v.tensor_copy(ctx.corr[:], ps[:, 0:MEGA])
        v.tensor_copy(ctx.ba_t[:], ps[:, 32:32 + H])
        v.drain().then_inc(cC.h, 1)
        cC.n += 1


def emit_stageA(ctx):
    blk, ps = ctx.blk, ctx.ps
    psb = ctx.ps.ap().bitcast(BF16)
    dS, dW = ctx.dS, ctx.dW
    a1, a2, a3, a4, a5, a6 = (ctx.a1, ctx.a2, ctx.a3, ctx.a4, ctx.a5, ctx.a6)
    xa = ctx.x_all.rearrange("(n p) d -> n p d", p=128)
    xo = ctx.x_own.rearrange("(n p) d -> n p d", p=128)
    ntile = NBLK + NOWN
    ngrp = ntile // 4
    b1, b2, b3, b4, b5, b6 = a1.n, a2.n, a3.n, a4.n, a5.n, a6.n
    b7 = ctx.a7.n
    bS, bW = dS.n, dW.n
    mega_done = ctx.cB.n
    ps_free = ctx.cC.n

    @blk.sync
    def _(sy):
        for i in range(ntile):
            if i >= 4:
                sy.wait_ge(ctx.a7.h, b7 + (i - 4) + 1)
            src = xa[i] if i < NBLK else xo[i - NBLK]
            sy.dma_start(out=ctx.xstage[:, i % 4], in_=src).then_inc(dS.h, 16)
        dS.n += 16 * ntile

    @blk.scalar
    def _(sc):
        for i in range(ntile):
            sc.wait_ge(dS.h, bS + 16 * (i + 1))
            if i >= 4:
                sc.wait_ge(a2.h, b2 + (i - 4) + 1)
                sc.wait_ge(ctx.a7.h, b7 + (i - 4) + 1)
            sc.activation(out=ctx.xhi_st[:, i % 4], in_=ctx.xstage[:, i % 4],
                          func=AF.Copy)
            sc.drain().then_inc(a1.h, 1)
        a1.n += ntile

    @blk.vector
    def _(v):
        for i in range(ntile):
            b = i % 4
            g = i // 4
            v.wait_ge(a1.h, b1 + i + 1)
            acc = (ctx.sumsq[:, i:i + 1] if i < NBLK
                   else ctx.sumsq_o[:, i - NBLK:i - NBLK + 1])
            v.scalar_tensor_tensor(ctx.xstage[:, b], ctx.xhi_st[:, b], 1.0,
                                   ctx.xhi_st[:, b], ALU.bypass, ALU.mult,
                                   accum_out=acc)
            v.drain().then_inc(ctx.a7.h, 1)
            v.wait_ge(a2.h, b2 + i + 1)
            if i % 4 == 0:
                if i >= 4:
                    v.wait_ge(a4.h, b4 + i - 3)
                if 65 <= g <= 72:
                    v.wait_ge(dW.h, bW + 128 * (g - 64))
            for c in range(NCH):
                v.tensor_copy(ctx.xfmstage[:, c, (i % 4) * 128:(i % 4 + 1) * 128],
                              psb[:, (4 + i % 4) * 1024 + c * 128:
                                  (4 + i % 4) * 1024 + (c + 1) * 128])
            v.drain().then_inc(a3.h, 1)
            if i % 4 == 3:
                v.wait_ge(a4.h, b4 + i + 1)
                v.tensor_copy(ctx.ost[:], ps[0:MEGA, 0:512])
                v.drain().then_inc(a5.h, 1)
                v.wait_ge(a6.h, b6 + g + 1)
                for j in range(4):
                    if g < NBLK // 4:
                        dst = ctx.ALL[:, :, 4 * g + j]
                    else:
                        dst = ctx.ALLo[:, :, 4 * (g - NBLK // 4) + j]
                    v.tensor_copy(dst, ps[:, 512 + j * 32:512 + j * 32 + MEGA])
                v.drain().then_inc(a5.h, 1)
        ctx.a7.n += ntile
        a3.n += ntile
        a5.n += 2 * ngrp

    @blk.tensor
    def _(t):
        t.wait_ge(ctx.cB.h, mega_done)
        t.wait_ge(ctx.cC.h, ps_free)
        for i in range(ntile + 1):
            if i < ntile:
                t.wait_ge(a1.h, b1 + i + 1)
                if i >= 4:
                    t.wait_ge(a3.h, b3 + i - 3)
                for c in range(NCH):
                    t.transpose(psb[:, (4 + i % 4) * 1024 + c * 128:
                                    (4 + i % 4) * 1024 + (c + 1) * 128],
                                ctx.xhi_st[:, i % 4, c * 128:(c + 1) * 128],
                                ctx.ident_b[:])
                t.drain().then_inc(a2.h, 1)
            if i >= 1:
                j = i - 1
                g = j // 4
                t.wait_ge(a3.h, b3 + j + 1)
                if j % 4 == 0 and g >= 1:
                    t.wait_ge(a5.h, b5 + 2 * g - 1)
                for c in range(NCH):
                    t.matmul(ps[0:MEGA, (j % 4) * 128:(j % 4 + 1) * 128],
                             ctx.mega_hi[:, c, 0:MEGA],
                             ctx.xfmstage[:, c, (j % 4) * 128:(j % 4 + 1) * 128],
                             start=(c == 0), stop=False)
                    t.matmul(ps[0:MEGA, (j % 4) * 128:(j % 4 + 1) * 128],
                             ctx.mega_lo[:, c, 0:MEGA],
                             ctx.xfmstage[:, c, (j % 4) * 128:(j % 4 + 1) * 128],
                             start=False, stop=(c == NCH - 1))
                t.drain().then_inc(a4.h, 1)
                if j % 4 == 3:
                    t.wait_ge(a5.h, b5 + 2 * g + 1)
                    for jj in range(4):
                        t.transpose(ps[:, 512 + jj * 32:512 + jj * 32 + MEGA],
                                    ctx.ost[:, jj * 128:(jj + 1) * 128],
                                    ctx.ident_f[0:MEGA, 0:MEGA])
                    t.drain().then_inc(a6.h, 1)
        a2.n += ntile
        a4.n += ntile
        a6.n += ngrp

    @blk.gpsimd
    def _(g_):
        for g in range(NBLK // 4, ngrp):
            go = g - NBLK // 4
            g_.wait_ge(a3.h, b3 + 4 * (g + 1))
            for c in range(NCH):
                g_.dma_start(out=ctx.xhiT_own[c * 128:(c + 1) * 128,
                                              go * 512:(go + 1) * 512],
                             in_=ctx.xfmstage[:, c, :]).then_inc(dW.h, 16)
        dW.n += 16 * NCH * NOWN // 4


def emit_stageB(ctx):
    pass


def emit_post_routing(ctx):
    blk, ps = ctx.blk, ctx.ps
    cC, cD = ctx.cC, ctx.cD
    Bp = ctx.ALL[:][:, 0:E, :]
    sums = ctx.ALL[:][:, E + H, :]
    Bpo = ctx.ALLo[:][:, 0:E, :]
    awao = ctx.ALLo[:][:, E:E + H, :]
    sumso = ctx.ALLo[:][:, E + H, :]
    if 1 not in P_PARTS:
        return

    @blk.vector
    def _(v):
        v.tensor_scalar(ctx.m_all[:], sums, 1.0 / DM, None, ALU.mult)
        v.tensor_tensor(ctx.sinv_all[:], ctx.m_all[:], ctx.m_all[:], ALU.mult)
        v.scalar_tensor_tensor(ctx.sinv_all[:], ctx.sumsq[:], 1.0 / DM, ctx.sinv_all[:],
                               ALU.mult, ALU.subtract)
        v.tensor_scalar(ctx.m_allo[:], sumso, 1.0 / DM, None, ALU.mult)
        v.drain()
        v.tensor_tensor(ctx.sinv_allo[:], ctx.m_allo[:], ctx.m_allo[:], ALU.mult)
        v.drain()
        v.scalar_tensor_tensor(ctx.sinv_allo[:], ctx.sumsq_o[:], 1.0 / DM,
                               ctx.sinv_allo[:], ALU.mult, ALU.subtract)
        v.drain().then_inc(cC.h, 1)
        cC.n += 1

    @blk.scalar
    def _(s):
        s.wait_ge(cC.h, cC.n)
        s.activation(out=ctx.sinv_all[:], in_=ctx.sinv_all[:], func=AF.Ln, bias=ctx.cst[:, 0:1])
        s.activation(out=ctx.sinv_all[:], in_=ctx.sinv_all[:], func=AF.Exp, scale=-0.5)
        s.activation(out=ctx.sinv_allo[:], in_=ctx.sinv_allo[:], func=AF.Ln, bias=ctx.cst[:, 0:1])
        s.drain()
        s.activation(out=ctx.sinv_allo[:], in_=ctx.sinv_allo[:], func=AF.Exp, scale=-0.5)
        s.drain()
        s.drain().then_inc(cC.h, 1)
        cC.n += 1

    @blk.vector
    def _(v):
        v.wait_ge(cC.h, cC.n)
        m_v = ctx.m_all[:].unsqueeze(1).to_broadcast([128, E, NBLK])
        s_v = ctx.corr[:, 0:E].unsqueeze(2).to_broadcast([128, E, NBLK])
        sinv_v = ctx.sinv_all[:].unsqueeze(1).to_broadcast([128, E, NBLK])
        v.tensor_tensor(ctx.cmp_f[:], m_v, s_v, ALU.mult)
        v.tensor_tensor(Bp, Bp, ctx.cmp_f[:], ALU.subtract)
        v.tensor_tensor(Bp, Bp, sinv_v, ALU.mult)
        mx = ctx.val_mx[:]
        v.tensor_tensor(mx[:, 0:4, :], Bp[:, 0:4, :], Bp[:, 4:8, :], ALU.max)
        v.tensor_tensor(mx[:, 0:2, :], mx[:, 0:2, :], mx[:, 2:4, :], ALU.max)
        v.tensor_tensor(mx[:, 0:1, :], mx[:, 0:1, :], mx[:, 1:2, :], ALU.max)
        v.tensor_tensor(Bp, Bp, mx[:, 0:1, :].to_broadcast([128, E, NBLK]), ALU.subtract)
        m_vo = ctx.m_allo[:].unsqueeze(1).to_broadcast([128, E, NOWN])
        s_vo = ctx.corr[:, 0:E].unsqueeze(2).to_broadcast([128, E, NOWN])
        sinv_vo = ctx.sinv_allo[:].unsqueeze(1).to_broadcast([128, E, NOWN])
        v.tensor_tensor(ctx.cmp_o[:], m_vo, s_vo, ALU.mult)
        v.tensor_tensor(Bpo, Bpo, ctx.cmp_o[:], ALU.subtract)
        v.tensor_tensor(Bpo, Bpo, sinv_vo, ALU.mult)
        mo = ctx.val_mxo[:]
        v.tensor_tensor(mo[:, 0:4, :], Bpo[:, 0:4, :], Bpo[:, 4:8, :], ALU.max)
        v.tensor_tensor(mo[:, 0:2, :], mo[:, 0:2, :], mo[:, 2:4, :], ALU.max)
        v.drain()
        v.tensor_tensor(mo[:, 0:1, :], mo[:, 0:1, :], mo[:, 1:2, :], ALU.max)
        v.drain()
        v.tensor_tensor(Bpo, Bpo, mo[:, 0:1, :].to_broadcast([128, E, NOWN]), ALU.subtract)
        sw_vo = ctx.corr[:, E:E + H].unsqueeze(2).to_broadcast([128, H, NOWN])
        m_vo16 = ctx.m_allo[:].unsqueeze(1).to_broadcast([128, H, NOWN])
        sinv_vo16 = ctx.sinv_allo[:].unsqueeze(1).to_broadcast([128, H, NOWN])
        v.tensor_tensor(ctx.awa_own[:], m_vo16, sw_vo, ALU.mult)
        v.drain()
        v.tensor_tensor(ctx.awa_own[:], awao, ctx.awa_own[:], ALU.subtract)
        v.drain()
        v.tensor_tensor(ctx.awa_own[:], ctx.awa_own[:], sinv_vo16, ALU.mult)
        v.drain()
        v.tensor_copy(ctx.m_own[:], ctx.m_allo[:])
        v.tensor_copy(ctx.sinv_own[:], ctx.sinv_allo[:])
        v.drain().then_inc(cC.h, 1)
        cC.n += 1

    @blk.scalar
    def _(s):
        s.wait_ge(cC.h, cC.n)
        s.activation(out=Bp, in_=Bp, func=AF.Exp)
        s.activation(out=Bpo, in_=Bpo, func=AF.Exp)
        s.drain().then_inc(cC.h, 1)
        cC.n += 1

    @blk.vector
    def _(v):
        v.wait_ge(cC.h, cC.n)
        mx = ctx.val_mx[:]
        v.tensor_tensor(mx[:, 0:4, :], Bp[:, 0:4, :], Bp[:, 4:8, :], ALU.add)
        v.tensor_tensor(mx[:, 0:2, :], mx[:, 0:2, :], mx[:, 2:4, :], ALU.add)
        v.tensor_tensor(mx[:, 0:1, :], mx[:, 0:1, :], mx[:, 1:2, :], ALU.add)
        v.reciprocal(mx[:, 1:2, :], mx[:, 0:1, :])
        v.tensor_tensor(Bp, Bp, mx[:, 1:2, :].to_broadcast([128, E, NBLK]), ALU.mult)
        mo = ctx.val_mxo[:]
        v.tensor_tensor(mo[:, 0:4, :], Bpo[:, 0:4, :], Bpo[:, 4:8, :], ALU.add)
        v.tensor_tensor(mo[:, 0:2, :], mo[:, 0:2, :], mo[:, 2:4, :], ALU.add)
        v.drain()
        v.tensor_tensor(mo[:, 0:1, :], mo[:, 0:1, :], mo[:, 1:2, :], ALU.add)
        v.drain()
        v.reciprocal(mo[:, 1:2, :], mo[:, 0:1, :])
        v.drain()
        v.tensor_tensor(Bpo, Bpo, mo[:, 1:2, :].to_broadcast([128, E, NOWN]), ALU.mult)
        v.memset(ctx.bis2[:, 0, :], 0.0)
        v.memset(ctx.bis2[:, 1, :], 1.0)
        v.memset(ctx.tau[:], 0.5)
        v.drain()
        v.drain().then_inc(cC.h, 1)
        cC.n += 1

    if 2 not in P_PARTS:
        return
    for it in range(NITER):
        @blk.vector
        def _(v):
            v.wait_ge(cC.h, cC.n)
            tau_v = ctx.tau[:].unsqueeze(2).to_broadcast([128, E, NBLK])
            v.tensor_tensor(ctx.cmp_f[:], Bp, tau_v, ALU.is_ge)
            v.tensor_reduce(ctx.cnt8[:], ctx.cmp_f[:], AX.X, ALU.add).then_inc(cD.h, 1)
            cD.n += 1

        @blk.tensor
        def _(t):
            t.wait_ge(cD.h, cD.n)
            t.matmul(ps[0:128, 3584:3584 + E], ctx.ones_f[:, 0:128], ctx.cnt8[:],
                     start=True, stop=True).then_inc(cD.h, 1)
            cD.n += 1

        @blk.vector
        def _(v):
            v.wait_ge(cD.h, cD.n)
            lo = ctx.bis2[:, 0, :]
            hi = ctx.bis2[:, 1, :]
            d1 = ctx.bis2[:, 2, :]
            g = ctx.bis2[:, 3, :]
            cntR = ctx.cnt8[:]
            mid = ctx.tau[:]
            v.tensor_copy(cntR, ps[0:128, 3584:3584 + E])
            v.drain()
            v.tensor_scalar(g, cntR, float(CAP), None, ALU.is_ge)
            v.tensor_tensor(d1, mid, lo, ALU.subtract)
            v.drain()
            v.tensor_tensor(d1, d1, g, ALU.mult)
            v.drain()
            v.tensor_tensor(lo, lo, d1, ALU.add)
            v.tensor_tensor(d1, hi, mid, ALU.subtract)
            v.drain()
            v.tensor_tensor(d1, d1, g, ALU.mult)
            v.drain()
            v.tensor_tensor(hi, mid, d1, ALU.add)
            v.drain()
            v.tensor_tensor(d1, lo, hi, ALU.add)
            v.drain()
            v.tensor_scalar(ctx.tau[:], d1, 0.5, None, ALU.mult).then_inc(cC.h, 1)
            cC.n += 1
    @blk.vector
    def _(v):
        v.wait_ge(cC.h, cC.n)
        v.drain()
        v.tensor_copy(ctx.tau[:], ctx.bis2[:, 0, :])
        v.drain().then_inc(cC.h, 1)
        cC.n += 1

    if 3 not in P_PARTS:
        return

    @blk.vector
    def _(v):
        v.wait_ge(cC.h, cC.n)
        tau_v = ctx.tau[:].unsqueeze(2).to_broadcast([128, E, NOWN])
        sel = ctx.cmp_o[:]
        mx = ctx.val_mxo[:]
        v.tensor_tensor(sel, Bpo, tau_v, ALU.is_ge)
        v.scalar_tensor_tensor(sel, sel, 2.0, Bpo, ALU.mult, ALU.add)
        v.tensor_tensor(mx[:, 0:4, :], sel[:, 0:4, :], sel[:, 4:8, :], ALU.max)
        v.tensor_tensor(mx[:, 0:2, :], mx[:, 0:2, :], mx[:, 2:4, :], ALU.max)
        v.tensor_tensor(mx[:, 0:1, :], mx[:, 0:1, :], mx[:, 1:2, :], ALU.max)
        seen = mx[:, 1:2, :]
        hit = mx[:, 2:3, :]
        v.memset(seen, 0.0)
        v.drain()
        for e in range(E):
            ga = ctx.gates_own[:, e:e + 1, :]
            v.tensor_tensor(hit, sel[:, e:e + 1, :], mx[:, 0:1, :], ALU.is_ge)
            v.drain()
            v.tensor_tensor(ga, hit, seen, ALU.subtract)
            v.drain()
            v.tensor_scalar_max(ga, ga, 0.0)
            v.tensor_tensor(seen, seen, hit, ALU.max)
            v.drain()
            v.tensor_tensor(ga, ga, Bpo[:, e:e + 1, :], ALU.mult)
            v.drain()
        v.drain().then_inc(cC.h, 1)
        cC.n += 1



def emit_seq(ctx, s):
    nc = ctx.nc
    with nc.named_scope(f"C{s}"):
        emit_C(ctx, s)
        barrier(ctx)
    if "D" not in SEQ_PHASES:
        return
    with nc.named_scope(f"D{s}"):
        emit_D(ctx, s)
        barrier(ctx)
    if "E" not in SEQ_PHASES:
        return
    with nc.named_scope(f"E{s}"):
        emit_E(ctx, s)
        barrier(ctx)
    if "F" not in SEQ_PHASES:
        return
    with nc.named_scope(f"F{s}"):
        emit_F(ctx, s)
        barrier(ctx)
    with nc.named_scope(f"G{s}"):
        emit_G(ctx, s)
        barrier(ctx)


def emit_msrow(ctx):
    blk, ps = ctx.blk, ctx.ps

    @blk.tensor
    def _(t):
        t.transpose(ps[0:32, 0:128], ctx.m_own[:], ctx.ident_f[:])
        t.transpose(ps[0:32, 128:256], ctx.sinv_own[:], ctx.ident_f[:]).then_inc(ctx.e1.h, 1)
        ctx.e1.n += 1

    @blk.vector
    def _(v):
        v.wait_ge(ctx.e1.h, ctx.e1.n)
        v.tensor_copy(ctx.wstage[0:32, 0:128], ps[0:32, 0:128])
        v.tensor_copy(ctx.wstage[0:32, 128:256], ps[0:32, 128:256]).then_inc(ctx.e2.h, 1)
        ctx.e2.n += 1

    @blk.sync
    def _(sy):
        sy.wait_ge(ctx.e2.h, ctx.e2.n)
        sy.dma_start(out=ctx.msrow_d[0:1, :], in_=ctx.wstage[0:32, 0:128]).then_inc(ctx.q1.h, 16)
        sy.dma_start(out=ctx.msrow_d[1:2, :], in_=ctx.wstage[0:32, 128:256]).then_inc(ctx.q1.h, 16)
        ctx.q1.n += 32
        sy.wait_ge(ctx.q1.h, ctx.q1.n)


def emit_C(ctx, s):
    blk, ps = ctx.blk, ctx.ps
    nc = ctx.nc
    q1, q2, e1, e2, e3, e4 = ctx.q1, ctx.q2, ctx.e1, ctx.e2, ctx.e3, ctx.e4

    @blk.vector
    def _(v):
        v.tensor_tensor(ctx.SP_T[:], ctx.awa_own[:, :, s * NCK:(s + 1) * NCK],
                        ctx.ba_t[:].unsqueeze(2).to_broadcast([128, H, NCK]),
                        ALU.add).then_inc(e1.h, 1)
        e1.n += 1

    @blk.scalar
    def _(sc):
        sc.wait_ge(e1.h, e1.n)
        sc.activation(out=ctx.SP_T[:], in_=ctx.SP_T[:], func=AF.Exp)
        sc.drain()
        sc.activation(out=ctx.SP_T[:], in_=ctx.SP_T[:], func=AF.Ln,
                      bias=ctx.cst[:, 1:2])
        sc.drain().then_inc(e1.h, 1)
        e1.n += 1

    @blk.tensor
    def _(t):
        t.wait_ge(e1.h, e1.n)
        for tt in range(NCK):
            t.transpose(ps[0:16, tt * 128:tt * 128 + 128],
                        ctx.SP_T[:, :, tt], ctx.ident_f[:])
        t.drain().then_inc(e2.h, 1)
        e2.n += 1

    @blk.vector
    def _(v):
        v.wait_ge(e2.h, e2.n)
        v.tensor_copy(ctx.EBUF[:], ps[0:16, 0:2048])
        v.drain()
        v.tensor_tensor_scan(ctx.CUM[:], ctx.EBUF[:],
                             ctx.cst[0:16, 2:3].to_broadcast([16, 2048]),
                             0.0, ALU.add, ALU.add)
        v.drain()
        v.memset(ctx.CB[:], 0.0)
        v.drain()
        v.tensor_copy(ctx.CB[:, 1:16], ctx.CUM[:, 127:2047:128])
        v.tensor_copy(ctx.CE[:], ctx.CUM[:, 127:2048:128])
        v.drain()
        v.tensor_tensor(ctx.EBUF[:],
                        ctx.CB[:].unsqueeze(2).to_broadcast([16, NCK, 128]),
                        ctx.CUM[:].rearrange("p (c i) -> p c i", c=NCK),
                        ALU.subtract)
        v.tensor_tensor(ctx.GAMS[:], ctx.CB[:], ctx.CE[:], ALU.subtract)
        v.drain().then_inc(e2.h, 1)
        e2.n += 1

    @blk.scalar
    def _(sc):
        sc.wait_ge(e2.h, e2.n)
        sc.activation(out=ctx.EBE[:], in_=ctx.EBUF[:], func=AF.Exp)
        sc.drain()
        sc.activation(out=ctx.GAMS[:], in_=ctx.GAMS[:], func=AF.Exp)
        sc.drain().then_inc(e3.h, 1)
        e3.n += 1

    @blk.vector
    def _(v):
        v.wait_ge(e3.h, e3.n)
        v.tensor_tensor(ctx.EBUF[:],
                        ctx.CUM[:].rearrange("p (c i) -> p c i", c=NCK),
                        ctx.CE[:].unsqueeze(2).to_broadcast([16, NCK, 128]),
                        ALU.subtract).then_inc(e3.h, 1)
        e3.n += 1

    @blk.scalar
    def _(sc):
        sc.wait_ge(e3.h, e3.n)
        sc.activation(out=ctx.EBUF[:], in_=ctx.EBUF[:], func=AF.Exp)
        sc.drain().then_inc(e3.h, 1)
        e3.n += 1

    @blk.tensor
    def _(t):
        t.wait_ge(e3.h, e3.n)
        t.transpose(ps[0:16, 512:528], ctx.GAMS[:], ctx.ident_f[0:16, 0:16]
                    ).then_inc(e3.h, 1)
        e3.n += 1

    @blk.vector
    def _(v):
        v.wait_ge(e3.h, e3.n)
        v.tensor_copy(ctx.GAMST[:], ps[0:16, 512:528]).then_inc(e3.h, 1)
        e3.n += 1

    @blk.sync
    def _(sy):
        sy.wait_ge(e3.h, e3.n)
        for j in range(2):
            sy.dma_start(out=ctx.G2[j:j + 1, :],
                         in_=ctx.GAMST[:, j:16:2],
                         allow_slow_non_contiguous=True).then_inc(q1.h, 16)
        q1.n += 32

    @blk.tensor
    def _(t):
        t.wait_ge(e3.h, e3.n)
        for c in range(NCK):
            t.transpose(ps[0:128, 2048 + c * 32:2048 + c * 32 + 16],
                        ctx.CUM[:, c * 128:(c + 1) * 128], ctx.ident_f[0:16, 0:16])
            t.transpose(ps[0:128, 3072 + c * 32:3072 + c * 32 + 16],
                        ctx.EBUF[:, c * 128:(c + 1) * 128], ctx.ident_f[0:16, 0:16])
        t.drain().then_inc(e4.h, 1)
        e4.n += 1
        t.wait_ge(q1.h, q1.n)
        t.matmul(ps[0:128, 1536:1664], ctx.exp2_f[:], ctx.G2[:],
                 start=True, stop=True).then_inc(e4.h, 1)
        e4.n += 1

    @blk.vector
    def _(v):
        v.wait_ge(e4.h, e4.n)
        for c in range(NCK):
            v.tensor_copy(ctx.CUMT[:, c, :], ps[0:128, 2048 + c * 32:2048 + c * 32 + 16])
            v.tensor_copy(ctx.EDET[:, c, :], ps[0:128, 3072 + c * 32:3072 + c * 32 + 16])
        v.tensor_copy(ctx.GAM2[:], ps[0:128, 1536:1664])
        v.drain().then_inc(e4.h, 1)
        e4.n += 1

    xho = ctx.xhiT_own
    base_q2 = q2.n
    base_e5 = ctx.e5.n
    base_e6 = ctx.e6.n

    @blk.scalar
    def _(sc):
        for t4 in range(4):
            if t4 >= 1:
                sc.wait_ge(ctx.e5.h, base_e5 + t4)
            tok0 = s * S + t4 * 512
            sc.dma_start(out=ctx.MSLm[:, (t4 % 2) * 512:(t4 % 2) * 512 + 512],
                         in_=ctx.msrow_d[0:1, s * S + t4 * 512: s * S + (t4 + 1) * 512]
                         ).then_inc(q2.h, 16)
            sc.dma_start(out=ctx.MSLs[:, (t4 % 2) * 512:(t4 % 2) * 512 + 512],
                         in_=ctx.msrow_d[1:2, s * S + t4 * 512: s * S + (t4 + 1) * 512]
                         ).then_inc(q2.h, 16)
            for ch in range(NCH):
                k = t4 * NCH + ch
                if k >= 2:
                    sc.wait_ge(ctx.e6.h, base_e6 + (k - 2) + 1)
                sc.dma_start(
                    out=ctx.XFM[:, k % 2],
                    in_=xho[ch * 128:(ch + 1) * 128,
                            s * S + t4 * 512:s * S + (t4 + 1) * 512]
                    ).then_inc(q2.h, 16)
        q2.n += 16 * 4 * (NCH + 2)

    @blk.tensor
    def _(t, _e4n=ctx.e4.n):
        t.wait_ge(ctx.e4.h, _e4n)
        for t4 in range(4):
            t.wait_ge(q2.h, base_q2 + 16 * (t4 * (NCH + 2) + 2))
            if t4 >= 1:
                t.wait_ge(ctx.e6.h, base_e6 + t4 * NCH)
            t.matmul(ps[:, 3072:3584], ctx.ones_f[0:1, :],
                     ctx.MSLm[:, (t4 % 2) * 512:(t4 % 2) * 512 + 512],
                     start=True, stop=True)
            t.matmul(ps[:, 3584:4096], ctx.ones_f[0:1, :],
                     ctx.MSLs[:, (t4 % 2) * 512:(t4 % 2) * 512 + 512],
                     start=True, stop=True).then_inc(ctx.e5.h, 1)
        ctx.e5.n += 4

    @blk.vector
    def _(v):
        for t4 in range(4):
            for ch in range(NCH):
                k = t4 * NCH + ch
                v.wait_ge(q2.h, base_q2 + 16 * (t4 * (NCH + 2) + 2 + ch + 1))
                v.wait_ge(ctx.e5.h, base_e5 + t4 + 1)
                v.tensor_tensor(ctx.LN_TMP[:], ctx.XFM[:, k % 2],
                                ps[:, 3072:3584], ALU.subtract)
                v.tensor_tensor(ctx.big0[:, ch * 2048 + t4 * 512:
                                         ch * 2048 + (t4 + 1) * 512],
                                ctx.LN_TMP[:], ps[:, 3584:4096],
                                ALU.mult).then_inc(ctx.e6.h, 1)
        ctx.e6.n += 4 * NCH

    emit_proj(ctx, s)


def emit_proj(ctx, s):
    blk, ps = ctx.blk, ctx.ps
    q1, e7, e8 = ctx.q1, ctx.e7, ctx.e8
    wbuf = ctx.big4.ap()[:, 0:8192].rearrange("p (k o c) -> p k o c", k=NCH, o=NDT)
    xnv = ctx.big0.ap().rearrange("p (c t) -> p c t", c=NCH)

    for wi, (wsrc, dstbig, act) in enumerate([
            (ctx.wq, ctx.big1, "silu"), (ctx.wk, ctx.big2, "silu"),
            (ctx.wv, ctx.big3, "copy")]):
        base_q1 = q1.n
        base_e7 = e7.n
        base_e8 = e8.n

        @blk.sync
        def _(sy, wsrc=wsrc, base_e8=base_e8, base_e7=base_e7, wi=wi):
            for k in range(NCH):
                if k >= 1:
                    sy.wait_ge(e8.h, base_e8 + k)
                elif wi == 0:
                    sy.wait_ge(ctx.e6.h, ctx.e6.n)
                else:
                    sy.wait_ge(e8.h, base_e8)
                    sy.wait_ge(e7.h, base_e7)
                sy.dma_start(out=ctx.WST2[:],
                             in_=wsrc[k * 128:(k + 1) * 128, :]).then_inc(q1.h, 16)
            q1.n += 16 * NCH

        @blk.vector
        def _(v, base_q1=base_q1, wi=wi):
            for k in range(NCH):
                v.wait_ge(q1.h, base_q1 + 16 * (k + 1))
                if wi > 0 or k > 0:
                    pass
                v.tensor_copy(wbuf[:, k, :, :], ctx.WST2[:]).then_inc(e8.h, 1)
            e8.n += NCH

        if act == "silu":
            @blk.tensor
            def _(t, base_e7=base_e7, dstbig=dstbig):
                t.wait_ge(e8.h, e8.n)
                t.wait_ge(e7.h, base_e7)
                for oc in range(NDT):
                    if oc >= 1:
                        t.wait_ge(e7.h, base_e7 + 2 * oc)
                    for k in range(NCH):
                        for t4 in range(4):
                            t.matmul(ps[:, (t4 % 4) * 512:(t4 % 4) * 512 + 512],
                                     wbuf[:, k, oc, :], xnv[:, k, t4 * 512:(t4 + 1) * 512],
                                     start=(k == 0), stop=(k == NCH - 1))
                    t.drain().then_inc(e7.h, 1)

            @blk.scalar
            def _(sc, base_e7=base_e7, dstbig=dstbig):
                for oc in range(NDT):
                    sc.wait_ge(e7.h, base_e7 + 2 * oc + 1)
                    for t4 in range(4):
                        sc.activation(out=dstbig[:, oc * 2048 + t4 * 512:
                                                 oc * 2048 + (t4 + 1) * 512],
                                      in_=ps[:, (t4 % 4) * 512:(t4 % 4) * 512 + 512],
                                      func=AF.Silu)
                    sc.drain().then_inc(e7.h, 1)
            e7.n += 2 * NDT
        else:
            @blk.tensor
            def _(t, base_e7=base_e7, _e4n=ctx.e4.n):
                t.wait_ge(e8.h, e8.n)
                t.wait_ge(e7.h, base_e7)
                t.wait_ge(ctx.e4.h, _e4n)
                wvk = ctx.big4.ap()[:, 0:8192].rearrange("p (k d) -> p k d", k=NCH)
                for tc in range(NCK):
                    if tc >= 1:
                        t.wait_ge(e7.h, base_e7 + 2 * tc)
                    for k in range(NCH):
                        t.matmul(ps[:, 2048:2560], xnv[:, k, tc * 128:(tc + 1) * 128],
                                 wvk[:, k, 0:512], start=(k == 0), stop=(k == NCH - 1))
                        t.matmul(ps[:, 2560:3072], xnv[:, k, tc * 128:(tc + 1) * 128],
                                 wvk[:, k, 512:1024], start=(k == 0), stop=(k == NCH - 1))
                    t.drain().then_inc(e7.h, 1)

            @blk.vector
            def _(v, base_e7=base_e7):
                for tc in range(NCK):
                    v.wait_ge(e7.h, base_e7 + 2 * tc + 1)
                    v.tensor_copy(ctx.big3[:, tc * 1024:(tc + 1) * 1024],
                                  ps[:, 2048:3072]).then_inc(e7.h, 1)
            e7.n += 2 * NCK


def emit_D(ctx, s):
    blk, ps = ctx.blk, ctx.ps
    psb = ctx.ps.ap().bitcast(BF16)
    q1 = ctx.q1
    sems = {k: getattr(ctx, k) for k in
            ["e1", "e2", "e3", "e4", "e5", "e6", "e7", "e8", "e9", "e10", "e11", "e12"]}
    B = {k: v.n for k, v in sems.items()}
    bq1 = q1.n
    Q = ctx.big1.ap().rearrange("p (d t) -> p d t", d=NDT)
    K = ctx.big2.ap().rearrange("p (d t) -> p d t", d=NDT)
    V = ctx.big3.ap().rearrange("p (c d) -> p c d", c=NCK)
    O = ctx.big4.ap().rearrange("p (d t) -> p d t", d=NDT)

    NCKD = NCK if D_NCK is None else D_NCK

    @blk.vector
    def _(v):
        v.memset(ctx.S_F[:], 0.0)
        v.memset(ctx.S_B[:], 0.0)
        v.drain().then_inc(sems["e12"].h, 1)
    sems["e12"].n += 1
    B["e12"] = sems["e12"].n - 1

    @blk.sync
    def _(sy):
        for c in range(NCKD):
            if c >= 1:
                sy.wait_ge(sems["e1"].h, B["e1"] + c)
            sy.dma_start(out=ctx.T1A[0:1, :],
                         in_=ctx.CUM[0:4, c * 128:(c + 1) * 128]).then_inc(q1.h, 16)
            sy.dma_start(out=ctx.T1A[32:33, :],
                         in_=ctx.CUM[4:8, c * 128:(c + 1) * 128]).then_inc(q1.h, 16)
            sy.dma_start(out=ctx.T1A[64:65, :],
                         in_=ctx.CUM[8:12, c * 128:(c + 1) * 128]).then_inc(q1.h, 16)
            sy.dma_start(out=ctx.T1B[0:1, :],
                         in_=ctx.CUM[12:16, c * 128:(c + 1) * 128]).then_inc(q1.h, 16)
        q1.n += 64 * NCKD

    @blk.tensor
    def _(t):
        e = sems
        for c in range(NCKD):
            if D_SUB >= 1:
                t.wait_ge(q1.h, bq1 + 64 * (c + 1))
                if c >= 1:
                    t.wait_ge(e["e7"].h, B["e7"] + c)
                    t.wait_ge(e["e2"].h, B["e2"] + c)
                pairs = [(ctx.ones_f[0:1, :], ctx.T1A[0:1, :]),
                         (ctx.ones_f[32:33, :], ctx.T1A[32:33, :]),
                         (ctx.ones_f[64:65, :], ctx.T1A[64:65, :]),
                         (ctx.ones_f[0:1, :], ctx.T1B[0:1, :])]
                tps = [(0, 0), (32, 0), (64, 0), (0, 0)]
                for q in range(4):
                    t.matmul(ps[:, 2048 + q * 512:2048 + (q + 1) * 512],
                             pairs[q][0], pairs[q][1], start=True, stop=True,
                             tile_position=tps[q])
                t.drain().then_inc(e["e1"].h, 1)
            if D_SUB >= 4:
                t.wait_ge(e["e2"].h, B["e2"] + c + 1)
                for par in range(2):
                    for h in range(par, H, 2):
                        pb = par * 64
                        dt = h // 2
                        t.matmul(ps[:, 2048 + h * 128:2048 + (h + 1) * 128],
                                 K[pb:pb + 64, dt, c * 128:(c + 1) * 128],
                                 Q[pb:pb + 64, dt, c * 128:(c + 1) * 128],
                                 start=True, stop=True,
                                 tile_position=(pb, 0))
                    t.drain()
                t.drain().then_inc(e["e4"].h, 1)
            if D_SUB >= 5:
                if c >= 1:
                    t.wait_ge(e["e9"].h, B["e9"] + c)
                for dt in range(NDT):
                    t.transpose(psb[:, 3072 + dt * 128:3072 + (dt + 1) * 128],
                                K[:, dt, c * 128:(c + 1) * 128], ctx.ident_b[:])
                t.drain().then_inc(e["e8"].h, 1)
            if D_SUB >= 7:
                t.wait_ge(e["e5"].h, B["e5"] + c + 1)
                for dt in range(NDT):
                    t.matmul(ps[:, 2048 + dt * 128:2048 + (dt + 1) * 128],
                             ctx.e16[:, dt, :], ctx.EBE[:, c * 128:(c + 1) * 128],
                             start=True, stop=True)
                t.drain().then_inc(e["e6"].h, 1)
            if D_SUB >= 10:
                t.wait_ge(e["e7"].h, B["e7"] + c + 1)
                t.wait_ge(e["e9"].h, B["e9"] + c + 1)
                t.wait_ge(e["e12"].h, B["e12"] + c + 1)
                if c >= 1:
                    t.wait_ge(e["e11"].h, B["e11"] + 2 * c)
                for h in range(H):
                    dt = h // 2
                    pb = (h % 2) * 64
                    if D_MMA:
                        t.matmul(ps[pb:pb + 64, dt * 128:(dt + 1) * 128],
                                 V[:, c, h * 64:(h + 1) * 64],
                                 ctx.A_SB[:, h * 128:(h + 1) * 128],
                                 start=True, stop=(not D_MMB), tile_position=(0, pb))
                    if D_MMB:
                        t.matmul(ps[pb:pb + 64, dt * 128:(dt + 1) * 128],
                                 ctx.S_B[pb:pb + 64, dt * 64:(dt + 1) * 64],
                                 ctx.PA[pb:pb + 64, dt, :],
                                 start=(not D_MMA), stop=True, tile_position=(pb, pb))
                for h in range(H):
                    dt = h // 2
                    pb = (h % 2) * 64
                    t.matmul(ps[pb:pb + 64, 1024 + dt * 64:1024 + (dt + 1) * 64],
                             ctx.KDEC.rearrange("p (h d) -> p h d", h=H)[:, h, :],
                             V[:, c, h * 64:(h + 1) * 64],
                             start=True, stop=True, tile_position=(0, pb))
                t.drain().then_inc(e["e10"].h, 1)
        for k, lvl in [("e1", 1), ("e4", 4), ("e8", 5), ("e6", 7), ("e10", 10)]:
            if D_SUB >= lvl:
                sems[k].n += NCKD

    @blk.scalar
    def _(sc):
        e = sems
        for c in range(NCKD):
            if D_SUB >= 3:
                sc.wait_ge(e["e2"].h, B["e2"] + c + 1)
                sc.activation(out=ctx.EXPD[:], in_=ctx.EXPD[:], func=AF.Exp)
                sc.drain().then_inc(e["e3"].h, 1)
        if D_SUB >= 3:
            sems["e3"].n += NCKD

    @blk.vector
    def _(v):
        e = sems
        for c in range(NCKD):
            if D_SUB >= 2:
                v.wait_ge(e["e1"].h, B["e1"] + c + 1)
                v.tensor_tensor(ctx.EXPD.rearrange("p (h i) -> p h i", h=H),
                                ctx.CUMT[:, c, :].unsqueeze(2).to_broadcast([128, H, 128]),
                                ps[:, 2048:4096].rearrange("p (h i) -> p h i", h=H),
                                ALU.subtract)
                v.tensor_tensor(ctx.EXPD.rearrange("p (h i) -> p h i", h=H),
                                ctx.EXPD.rearrange("p (h i) -> p h i", h=H),
                                ctx.maskadd[:].unsqueeze(1).to_broadcast([128, H, 128]),
                                ALU.add)
                v.drain().then_inc(e["e2"].h, 1)
            if D_SUB >= 6:
                v.wait_ge(e["e4"].h, B["e4"] + c + 1)
                v.wait_ge(e["e3"].h, B["e3"] + c + 1)
                if c >= 1:
                    v.wait_ge(e["e10"].h, B["e10"] + c)
                v.tensor_tensor(ctx.A_SB[:], ps[:, 2048:4096], ctx.EXPD[:],
                                ALU.mult).then_inc(e["e5"].h, 1)
            if D_SUB >= 8:
                v.wait_ge(e["e6"].h, B["e6"] + c + 1)
                for dt in range(NDT):
                    v.tensor_tensor(ctx.PA[:, dt, :], Q[:, dt, c * 128:(c + 1) * 128],
                                    ps[:, 2048 + dt * 128:2048 + (dt + 1) * 128],
                                    ALU.mult)
                v.drain().then_inc(e["e7"].h, 1)
            if D_SUB >= 9:
                v.wait_ge(e["e8"].h, B["e8"] + c + 1)
                v.tensor_tensor(ctx.KDEC.rearrange("p (h d) -> p h d", h=H),
                                psb[:, 3072:4096].rearrange("p (h d) -> p h d", h=H),
                                ctx.EDET[:, c, :].unsqueeze(2).to_broadcast([128, H, 64]),
                                ALU.mult).then_inc(e["e9"].h, 1)
            if D_SUB >= 11:
                v.wait_ge(e["e10"].h, B["e10"] + c + 1)
                v.tensor_copy(O[:, 0:4, c * 128:(c + 1) * 128], ps[:, 0:512])
                v.tensor_copy(O[:, 4:8, c * 128:(c + 1) * 128], ps[:, 512:1024])
                v.drain().then_inc(e["e11"].h, 1)
                gv = ctx.GAM2[:, c * 8:(c + 1) * 8].unsqueeze(2).to_broadcast([128, NDT, 64])
                v.tensor_tensor(ctx.S_F.rearrange("p (d v) -> p d v", d=NDT),
                                ctx.S_F.rearrange("p (d v) -> p d v", d=NDT), gv, ALU.mult)
                v.drain()
                v.tensor_tensor(ctx.S_F[:], ctx.S_F[:], ps[:, 1024:1536], ALU.add)
                v.drain().then_inc(e["e11"].h, 1)
                v.tensor_copy(ctx.S_B[:], ctx.S_F[:])
                v.drain().then_inc(e["e12"].h, 1)
        for k, amt, lvl in [("e2", NCKD, 2), ("e5", NCKD, 6), ("e7", NCKD, 8),
                            ("e9", NCKD, 9), ("e11", 2 * NCKD, 11), ("e12", NCKD, 11)]:
            if D_SUB >= lvl:
                sems[k].n += amt


def emit_E(ctx, s):
    blk, ps = ctx.blk, ctx.ps
    e1, e2, e3, q1 = ctx.e1, ctx.e2, ctx.e3, ctx.q1
    O = ctx.big4.ap().rearrange("p (d t) -> p d t", d=NDT)

    @blk.vector
    def _(v):
        for dt in range(NDT):
            od = ctx.big4.ap()[:, dt * 2048:(dt + 1) * 2048]
            v.tensor_reduce(ctx.ECOLS[:, dt:dt + 1], od, AX.X, ALU.add)
            v.tensor_tensor(ctx.EXPD[:], od, od, ALU.mult)
            v.tensor_reduce(ctx.ECOLS[:, 8 + dt:9 + dt], ctx.EXPD[:], AX.X, ALU.add)
        v.drain().then_inc(e1.h, 1)
        e1.n += 1

    @blk.tensor
    def _(t):
        t.wait_ge(e1.h, e1.n)
        t.matmul(ps[0:2, 0:16], ctx.e2t[:], ctx.ECOLS[:, 0:16],
                 start=True, stop=True).then_inc(e1.h, 1)
        e1.n += 1

    @blk.vector
    def _(v):
        v.wait_ge(e1.h, e1.n)
        st = ctx.GAMS
        NI = 1.0 / (S * DH)
        v.tensor_scalar(st[0:2, 0:16], ps[0:2, 0:16], NI, None, ALU.mult)
        v.drain()
        v.tensor_tensor(ctx.CB[0:2, 0:8], st[0:2, 0:8], st[0:2, 0:8], ALU.mult)
        v.drain()
        v.tensor_tensor(st[0:2, 8:16], st[0:2, 8:16], ctx.CB[0:2, 0:8], ALU.subtract)
        v.drain().then_inc(e2.h, 1)
        e2.n += 1

    @blk.scalar
    def _(sc):
        sc.wait_ge(e2.h, e2.n)
        sc.activation(out=ctx.GAMS[0:2, 8:16], in_=ctx.GAMS[0:2, 8:16], func=AF.Ln,
                      bias=ctx.cst[0:2, 0:1])
        sc.drain()
        sc.activation(out=ctx.GAMS[0:2, 8:16], in_=ctx.GAMS[0:2, 8:16], func=AF.Exp,
                      scale=-0.5)
        sc.drain().then_inc(e2.h, 1)
        e2.n += 1

    @blk.tensor
    def _(t):
        t.wait_ge(e2.h, e2.n)
        t.matmul(ps[0:128, 512:520], ctx.exp2_f[:], ctx.GAMS[0:2, 0:8],
                 start=True, stop=True)
        t.matmul(ps[0:128, 520:528], ctx.exp2_f[:], ctx.GAMS[0:2, 8:16],
                 start=True, stop=True).then_inc(e2.h, 1)
        e2.n += 1

    @blk.vector
    def _(v):
        v.wait_ge(e2.h, e2.n)
        v.tensor_tensor(ctx.ecol[:], ps[0:128, 520:528], ctx.gnw_t[:], ALU.mult)
        v.drain()
        v.tensor_tensor(ctx.ecol2[:], ps[0:128, 512:520], ctx.ecol[:], ALU.mult)
        v.drain()
        v.tensor_tensor(ctx.ecol2[:], ctx.gnb_t[:], ctx.ecol2[:], ALU.subtract)
        v.drain().then_inc(e3.h, 1)
        e3.n += 1

    @blk.vector
    def _(v):
        v.wait_ge(e3.h, e3.n)
        for dt in range(NDT):
            od = ctx.big4.ap()[:, dt * 2048:(dt + 1) * 2048]
            v.tensor_scalar(od, od, ctx.ecol[:, dt:dt + 1], ctx.ecol2[:, dt:dt + 1],
                            ALU.mult, ALU.add)
        v.drain().then_inc(e3.h, 1)
        e3.n += 1

    wbufe = ctx.big2.ap()[:, 0:8192].rearrange("p (k o c) -> p k o c", k=NCH, o=NDT)
    xnv = ctx.big0.ap().rearrange("p (c t) -> p c t", c=NCH)
    bq1 = q1.n
    be4 = ctx.e4.n

    @blk.sync
    def _(sy):
        for k in range(NCH):
            if k >= 1:
                sy.wait_ge(ctx.e4.h, be4 + k)
            else:
                sy.wait_ge(e3.h, e3.n)
            sy.dma_start(out=ctx.WST2f[:],
                         in_=ctx.wga[k * 128:(k + 1) * 128, :]).then_inc(q1.h, 16)
        q1.n += 16 * NCH

    @blk.vector
    def _(v):
        for k in range(NCH):
            v.wait_ge(q1.h, bq1 + 16 * (k + 1))
            v.tensor_copy(wbufe[:, k, :, :], ctx.WST2f[:]).then_inc(ctx.e4.h, 1)
        ctx.e4.n += NCH

    be5 = ctx.e5.n

    @blk.tensor
    def _(t):
        t.wait_ge(ctx.e4.h, ctx.e4.n)
        for oc in range(NDT):
            for t4 in range(4):
                if oc * 4 + t4 >= 2:
                    t.wait_ge(ctx.e5.h, be5 + 2 * (oc * 4 + t4) - 2)
                for k in range(NCH):
                    t.matmul(ps[:, ((oc * 4 + t4) % 2) * 512:
                                ((oc * 4 + t4) % 2) * 512 + 512],
                             wbufe[:, k, oc, :], xnv[:, k, t4 * 512:(t4 + 1) * 512],
                             start=(k == 0), stop=(k == NCH - 1))
                t.drain().then_inc(ctx.e5.h, 1)
        ctx.e5.n += 32

    be6 = ctx.e6.n

    @blk.scalar
    def _(sc):
        for i in range(32):
            sc.wait_ge(ctx.e5.h, be5 + 2 * i + 1)
            if i >= 1:
                sc.wait_ge(ctx.e6.h, be6 + i)
            sc.activation(out=ctx.GT[:], in_=ps[:, (i % 2) * 512:(i % 2) * 512 + 512],
                          func=AF.Sigmoid)
            sc.drain().then_inc(ctx.e5.h, 1)
        ctx.e5.n += 32

    @blk.vector
    def _(v):
        for i in range(32):
            oc, t4 = i // 4, i % 4
            v.wait_ge(ctx.e5.h, be5 + 2 * i + 2)
            sl = slice(oc * 2048 + t4 * 512, oc * 2048 + (t4 + 1) * 512)
            v.tensor_tensor(ctx.big1.ap()[:, sl], ctx.big4.ap()[:, sl], ctx.GT[:],
                            ALU.mult).then_inc(ctx.e6.h, 1)
        ctx.e6.n += 32


def emit_F(ctx, s):
    blk, ps = ctx.blk, ctx.ps
    q1, q3 = ctx.q1, ctx.q3
    f1, f2, f3, f4 = ctx.f1, ctx.f2, ctx.f3, ctx.f4
    e1 = ctx.e1
    xnv = ctx.big0.ap().rearrange("p (c t) -> p c t", c=NCH)
    W1 = ctx.big3.ap().rearrange("p (k f) -> p k f", k=NCH)
    W2 = ctx.big4.ap().rearrange("p (f d) -> p f d", f=NCK)
    gfm = ctx.G_FM
    acc = ctx.big2.ap()
    bf1, bf2, bf3, bf4 = f1.n, f2.n, f3.n, f4.n

    def w_chunks(e):
        w1f = [ctx.w1[e][(k // 2) * 128:(k // 2 + 1) * 128,
                         (k % 2) * 1024:(k % 2 + 1) * 1024] for k in range(16)]
        w1b = [ctx.w1_b[e][(k // 2) * 128:(k // 2 + 1) * 128,
                           (k % 2) * 1024:(k % 2 + 1) * 1024] for k in range(16)]
        w1d = [W1[:, k // 2, (k % 2) * 1024:(k % 2 + 1) * 1024] for k in range(16)]
        w2f = [ctx.w2[e][k * 128:(k + 1) * 128, :] for k in range(NCK)]
        w2b = [ctx.w2_b[e][k * 128:(k + 1) * 128, :] for k in range(NCK)]
        w2d = [W2[:, k, :] for k in range(NCK)]
        return w1f, w1b, w1d, w2f, w2b, w2d

    fw = ctx.fw
    bfw = fw.n

    for e in range(E):
        w1f, w1b, w1d, w2f, w2b, w2d = w_chunks(e)
        if s == 0:
            @blk.gpsimd
            def _(g, e=e, w1f=w1f, w2f=w2f, w1b=w1b, w2b=w2b, w1d=w1d, w2d=w2d):
                bq = bfw + 16 * 32 * e
                g.wait_ge(f1.h, bf1 + 16 * e)
                for k in range(16):
                    g.dma_start(out=w1d[k], in_=w1f[k]).then_inc(fw.h, 16)
                g.wait_ge(f3.h, bf3 + 16 * e)
                for k in range(NCK):
                    g.dma_start(out=w2d[k], in_=w2f[k]).then_inc(fw.h, 16)
                for k in range(32):
                    g.wait_ge(fw.h, bq + 16 * (k + 1))
                    src = w1d[k] if k < 16 else w2d[k - 16]
                    dstc = w1b[k] if k < 16 else w2b[k - 16]
                    g.dma_start(out=dstc, in_=src).then_inc(q3.h, 16)
                q3.n += 16 * 32
            fw.n += 16 * 32
        else:
            @blk.sync
            def _(sy, e=e, w1b=w1b, w2b=w2b, w1d=w1d, w2d=w2d):
                for k in range(32):
                    if k == 0:
                        sy.wait_ge(f1.h, bf1 + 16 * e)
                    if k == 16:
                        sy.wait_ge(f3.h, bf3 + 16 * e)
                    dst = w1d[k] if k < 16 else w2d[k - 16]
                    src = w1b[k] if k < 16 else w2b[k - 16]
                    sy.dma_start(out=dst, in_=src).then_inc(fw.h, 16)
                q1.n += 0
            fw.n += 16 * 32

        @blk.tensor
        def _(t, e=e):
            t.wait_ge(fw.h, bfw + 512 * e + 256)
            for t4 in range(4):
                for j in range(4):
                    qidx = (e * 4 + t4) * 4 + j
                    hb = (j + 1) % 2
                    if qidx >= 2:
                        t.wait_ge(f2.h, bf2 + qidx - 1)
                    if j == 1:
                        t.wait_ge(f4.h, bf4 + 4 * (e * 4 + t4))
                    for i in range(4):
                        fcg = j * 4 + i
                        for k in range(NCH):
                            t.matmul(ps[:, hb * 2048 + i * 512:hb * 2048 + (i + 1) * 512],
                                     W1[:, k, fcg * 128:(fcg + 1) * 128],
                                     xnv[:, k, t4 * 512:(t4 + 1) * 512],
                                     start=(k == 0), stop=(k == NCH - 1))
                    t.drain().then_inc(f1.h, 1)
                t.wait_ge(f2.h, bf2 + (e * 4 + t4) * 4 + 4)
                t.wait_ge(fw.h, bfw + 512 * (e + 1))
                for ts in range(4):
                    tc = t4 * 4 + ts
                    i2 = e * 16 + tc
                    if i2 >= 2:
                        t.wait_ge(f4.h, bf4 + i2 - 1)
                    pbase = (tc % 2) * 1024
                    for fc in range(NCK):
                        t.matmul(ps[:, pbase:pbase + 512],
                                 gfm[:, fc, ts * 128:(ts + 1) * 128],
                                 W2[:, fc, 0:512], start=(fc == 0), stop=(fc == NCK - 1))
                        t.matmul(ps[:, pbase + 512:pbase + 1024],
                                 gfm[:, fc, ts * 128:(ts + 1) * 128],
                                 W2[:, fc, 512:1024], start=(fc == 0), stop=(fc == NCK - 1))
                    t.drain().then_inc(f3.h, 1)

        @blk.scalar
        def _(sc, e=e):
            for t4 in range(4):
                for j in range(4):
                    qidx = (e * 4 + t4) * 4 + j
                    hb = (j + 1) % 2
                    sc.wait_ge(f1.h, bf1 + qidx + 1)
                    if e * 4 + t4 >= 1 and j == 0:
                        sc.wait_ge(f3.h, bf3 + 4 * (e * 4 + t4))
                    for i in range(4):
                        sc.activation(out=gfm[:, j * 4 + i, :],
                                      in_=ps[:, hb * 2048 + i * 512:
                                             hb * 2048 + (i + 1) * 512], func=AF.Gelu)
                    sc.drain().then_inc(f2.h, 1)

        @blk.vector
        def _(v, e=e):
            for t4 in range(4):
                for ts in range(4):
                    tc = t4 * 4 + ts
                    i2 = e * 16 + tc
                    v.wait_ge(f3.h, bf3 + i2 + 1)
                    pbase = (tc % 2) * 1024
                    gate = ctx.gates_own[:, e, s * NCK + tc:s * NCK + tc + 1]
                    asl = acc[:, tc * 1024:(tc + 1) * 1024]
                    if e == 0:
                        v.tensor_scalar(asl[:, 0:512], ps[:, pbase:pbase + 512],
                                        gate, None, ALU.mult)
                        v.tensor_scalar(asl[:, 512:1024], ps[:, pbase + 512:pbase + 1024],
                                        gate, None, ALU.mult).then_inc(f4.h, 1)
                    else:
                        v.scalar_tensor_tensor(asl[:, 0:512], ps[:, pbase:pbase + 512],
                                               gate, asl[:, 0:512], ALU.mult, ALU.add)
                        v.scalar_tensor_tensor(asl[:, 512:1024],
                                               ps[:, pbase + 512:pbase + 1024],
                                               gate, asl[:, 512:1024],
                                               ALU.mult, ALU.add).then_inc(f4.h, 1)
        f1.n += 16
        f2.n += 16
        f3.n += 16
        f4.n += 16


def emit_G(ctx, s):
    blk, ps = ctx.blk, ctx.ps
    q1, q2, q3 = ctx.q1, ctx.q2, ctx.q3
    g1, g2, g3 = ctx.e7, ctx.e8, ctx.e9
    og = ctx.big1.ap().rearrange("p (k t) -> p k t", k=NCH)
    acc = ctx.big2.ap()
    wbufg = ctx.big3.ap()[:, 0:8192].rearrange("p (k d) -> p k d", k=NCH)
    bg1, bg2, bg3 = g1.n, g2.n, g3.n
    bq1, bq2, bq3 = q1.n, q2.n, q3.n
    xo = ctx.x_own
    outv = ctx.out

    @blk.sync
    def _(sy):
        for k in range(NCH):
            if k >= 1:
                sy.wait_ge(g1.h, bg1 + k)
            sy.dma_start(out=ctx.WST2f[:],
                         in_=ctx.wo[k * 128:(k + 1) * 128, :]).then_inc(q1.h, 16)
        q1.n += 16 * NCH

    @blk.vector
    def _(v):
        for k in range(NCH):
            v.wait_ge(q1.h, bq1 + 16 * (k + 1))
            v.tensor_copy(wbufg[:, k, :], ctx.WST2f[:]).then_inc(g1.h, 1)
        g1.n += NCH

    @blk.scalar
    def _(sc):
        for tc in range(NCK):
            if tc >= 2:
                sc.wait_ge(g3.h, bg3 + tc - 1)
            sc.dma_start(out=ctx.XG[:, tc % 2],
                         in_=xo[s * S + tc * 128:s * S + (tc + 1) * 128, :]
                         ).then_inc(q2.h, 16)
        q2.n += 16 * NCK

    @blk.tensor
    def _(t):
        t.wait_ge(g1.h, g1.n)
        for tc in range(NCK):
            if tc >= 2:
                t.wait_ge(g3.h, bg3 + tc - 1)
            pbase = (tc % 2) * 1024
            for k in range(NCH):
                t.matmul(ps[:, pbase:pbase + 512], og[:, k, tc * 128:(tc + 1) * 128],
                         wbufg[:, k, 0:512], start=(k == 0), stop=(k == NCH - 1))
                t.matmul(ps[:, pbase + 512:pbase + 1024],
                         og[:, k, tc * 128:(tc + 1) * 128],
                         wbufg[:, k, 512:1024], start=(k == 0), stop=(k == NCH - 1))
            t.drain().then_inc(g2.h, 1)
        g2.n += NCK

    @blk.vector
    def _(v):
        for tc in range(NCK):
            v.wait_ge(g2.h, bg2 + tc + 1)
            v.wait_ge(q2.h, bq2 + 16 * (tc + 1))
            if tc >= 2:
                v.wait_ge(q3.h, bq3 + 16 * (tc - 1))
            pbase = (tc % 2) * 1024
            o1 = ctx.OUTST[:, tc % 2]
            for hf in range(2):
                v.tensor_tensor(o1[:, hf * 512:(hf + 1) * 512],
                                ps[:, pbase + hf * 512:pbase + (hf + 1) * 512],
                                ctx.XG[:, tc % 2][:, hf * 512:(hf + 1) * 512], ALU.add)
                v.tensor_tensor(o1[:, hf * 512:(hf + 1) * 512],
                                o1[:, hf * 512:(hf + 1) * 512],
                                acc[:, tc * 1024 + hf * 512:tc * 1024 + (hf + 1) * 512],
                                ALU.add)
            v.drain().then_inc(g3.h, 1)
        g3.n += NCK

    @blk.gpsimd
    def _(g):
        for tc in range(NCK):
            g.wait_ge(g3.h, bg3 + tc + 1)
            g.dma_start(out=outv[s * S + tc * 128:s * S + (tc + 1) * 128, :],
                        in_=ctx.OUTST[:, tc % 2]).then_inc(q3.h, 16)
        q3.n += 16 * NCK


def kernel(L0_attn_x, L0_attn_Wq, L0_attn_Wk, L0_attn_Wv, L0_attn_Wg,
           L0_attn_Wa, L0_attn_ba, L0_attn_Wo, L0_attn_gn_w, L0_attn_gn_b,
           L0_ffn_Wg, L0_ffn_W1, L0_ffn_W2):
    x = np.ascontiguousarray(np.asarray(L0_attn_x, dtype=np.float32)).reshape(T, DM)
    base = {
        "x_all": x,
        "wq": np.ascontiguousarray(L0_attn_Wq, dtype=np.float32),
        "wk": np.ascontiguousarray(L0_attn_Wk, dtype=np.float32),
        "wv": np.ascontiguousarray(L0_attn_Wv, dtype=np.float32),
        "wga": np.ascontiguousarray(L0_attn_Wg, dtype=np.float32),
        "wa": np.ascontiguousarray(L0_attn_Wa, dtype=np.float32),
        "ba": np.ascontiguousarray(L0_attn_ba, dtype=np.float32),
        "wo": np.ascontiguousarray(L0_attn_Wo, dtype=np.float32),
        "gnw": np.ascontiguousarray(L0_attn_gn_w, dtype=np.float32),
        "gnb": np.ascontiguousarray(L0_attn_gn_b, dtype=np.float32),
        "wgf": np.ascontiguousarray(L0_ffn_Wg, dtype=np.float32),
        "w1": np.ascontiguousarray(L0_ffn_W1, dtype=np.float32),
        "w2": np.ascontiguousarray(L0_ffn_W2, dtype=np.float32),
    }
    in_maps = []
    for c in range(NC_):
        m = dict(base)
        m["x_own"] = np.ascontiguousarray(x[c * TOWN:(c + 1) * TOWN])
        m["csel"] = np.array([c], dtype=np.int32)
        in_maps.append(m)
    nc, _ctx = build_nc()
    res = run_bass_kernel_spmd(nc, in_maps, core_ids=list(range(NC_)))
    out = np.concatenate([r["out"] for r in res.results], axis=0)
    return out.reshape(B, S, DM)
